# version 13
# baseline (speedup 1.0000x reference)
import numpy as np
import ml_dtypes
import contextlib
import concourse.bass as bass
import concourse.mybir as mybir
from concourse.bass_utils import run_bass_kernel_spmd
from concourse.alu_op_type import AluOpType as ALU

F32 = mybir.dt.float32
BF16 = mybir.dt.bfloat16
AF = mybir.ActivationFunctionType
AP = bass.AP

NCORE = 8
NSEG = 4
DM = 1024
DI = 2048
NH = 32
HP = 64
Q = 128
SUP = 512
CPS = SUP // Q
NXS = 16
NBC = 18
KD = 8
EPS = 1e-5
BF = ml_dtypes.bfloat16

ENGS = ("sync", "vector", "scalar", "tensor", "gpsimd")
SEMS = ("sy", "sc", "gp", "te", "ve", "se", "po", "cc", "e0", "e1", "dc")


class Prog:
    def __init__(self):
        self.items = {e: [] for e in ENGS}
        self.cnt = {s: 0 for s in SEMS}

    def emit(self, eng, fn, waits=(), inc=None, step=1):
        if inc is not None:
            self.cnt[inc] += step
        self.items[eng].append((fn, tuple((s, v) for (s, v) in waits if v > 0), inc, step))

    def v(self, name):
        return self.cnt[name]

    def barrier_vals(self):
        return [(s, self.cnt[s]) for s in SEMS if self.cnt[s] > 0]


def _emit_setup(P, nc, env):
    (w_in_bf, wstage, wstage2, cwt, cbt, dtb, alog, acol, dti, rmsw, ibf, irp,
     alct, gamt, igamt, conv_wt, conv_bt, dtb_col, alog_col, d_tile, rms_t,
     i_bf, irep, alpha_t, gam_t, igam_t, w_in, dmaf) = (
        env["w_in_bf"], env["wstage"], env["wstage2"], env["cwt"], env["cbt"],
        env["dtb"], env["alog"], env["acol"], env["dti"], env["rmsw"], env["ibf"],
        env["irp"], env["alct"], env["gamt"], env["igamt"], env["conv_wt"],
        env["conv_bt"], env["dtb_col"], env["alog_col"], env["d_tile"], env["rms_t"],
        env["i_bf"], env["irep"], env["alpha_t"], env["gam_t"], env["igam_t"],
        env["w_in"], env["dmaf"])
    cep, eps_c = env["cep"], env["eps_c"]
    for (dst, src) in ((cwt, conv_wt), (cbt, conv_bt), (dtb, dtb_col),
                       (alog, alog_col), (dti, d_tile), (rmsw, rms_t),
                       (ibf, i_bf), (irp, irep), (alct, alpha_t),
                       (gamt, gam_t), (igamt, igam_t), (cep, eps_c)):
        P.emit("sync", dmaf(dst[:], src), inc="sy", step=16)
    setup_sy = P.v("sy")
    col1 = env["col1"]
    P.emit("scalar", lambda e: e.activation(out=col1[0:32, :], in_=alog[:], func=AF.Exp),
           waits=[("sy", setup_sy)], inc="se")
    P.emit("vector", lambda e: e.tensor_scalar(out=acol[:], in0=col1[0:32, :],
                                               scalar1=-1.0, scalar2=None,
                                               op0=ALU.mult),
           waits=[("se", P.v("se"))], inc="ve")
    QS = (("sync", "sy"), ("scalar", "sc"), ("gpsimd", "gp"))
    stg = (wstage, wstage2)
    conv_done = [0] * KD
    for k in range(KD):
        qeng, qsem = QS[k % 3]
        w = [("ve", conv_done[k - 2])] if k >= 2 else []
        P.emit(qeng, dmaf(stg[k % 2][:], w_in[k * 128:(k + 1) * 128, :]),
               waits=w, inc=qsem, step=16)
        ld = (qsem, P.v(qsem))
        P.emit("vector", lambda e, k=k: e.tensor_scalar(
            out=w_in_bf[:, k * 4384:(k + 1) * 4384], in0=stg[k % 2][:],
            scalar1=1.0, scalar2=None, op0=ALU.mult),
            waits=[ld], inc="ve")
        conv_done[k] = P.v("ve")


def build_nc(seg_len, debug=False):
    assert seg_len % SUP == 0
    nsup = seg_len // SUP
    nch = seg_len // Q

    nc = bass.Bass("TRN2", target_bir_lowering=False)

    x_seg = nc.dram_tensor("x_seg", [seg_len, DM], F32, kind="ExternalInput").ap()
    x_halo = nc.dram_tensor("x_halo", [16, DM], F32, kind="ExternalInput").ap()
    w_in = nc.dram_tensor("w_in", [DM, 4384], F32, kind="ExternalInput").ap()
    conv_wt = nc.dram_tensor("conv_wt", [128, NBC * 4], F32, kind="ExternalInput").ap()
    conv_bt = nc.dram_tensor("conv_bt", [128, NBC], F32, kind="ExternalInput").ap()
    dtb_col = nc.dram_tensor("dtb_col", [32, 1], F32, kind="ExternalInput").ap()
    alog_col = nc.dram_tensor("alog_col", [32, 1], F32, kind="ExternalInput").ap()
    d_tile = nc.dram_tensor("d_tile", [128, 32], F32, kind="ExternalInput").ap()
    rms_t = nc.dram_tensor("rms_t", [128, 16], F32, kind="ExternalInput").ap()
    w_out = nc.dram_tensor("w_out", [DI, DM], F32, kind="ExternalInput").ap()
    wf = nc.dram_tensor("wf", [DM, DM], F32, kind="ExternalInput").ap()
    a1 = nc.dram_tensor("a1", [DM, DM], F32, kind="ExternalInput").ap()
    a2 = nc.dram_tensor("a2", [DM, DM], F32, kind="ExternalInput").ap()
    lng_t = nc.dram_tensor("lng_t", [128, DM], F32, kind="ExternalInput").ap()
    lnb_t = nc.dram_tensor("lnb_t", [128, DM], F32, kind="ExternalInput").ap()
    fbias_t = nc.dram_tensor("fbias_t", [128, DM], F32, kind="ExternalInput").ap()
    i_bf = nc.dram_tensor("i_bf", [128, 128], BF16, kind="ExternalInput").ap()
    irep = nc.dram_tensor("irep", [128, 512], BF16, kind="ExternalInput").ap()
    alpha_t = nc.dram_tensor("alpha_t", [128, 8], F32, kind="ExternalInput").ap()
    gam_t = nc.dram_tensor("gam_t", [128, 8], F32, kind="ExternalInput").ap()
    igam_t = nc.dram_tensor("igam_t", [128, 8], F32, kind="ExternalInput").ap()
    eps_c = nc.dram_tensor("eps_c", [128, 1], F32, kind="ExternalInput").ap()

    out_seg = nc.dram_tensor("out_seg", [seg_len, DM], F32, kind="ExternalOutput").ap()

    st_ypre = nc.dram_tensor("st_ypre", [seg_len, DI], BF16).ap()
    st_sz = nc.dram_tensor("st_sz", [seg_len, DI], BF16).ap()
    st_c = nc.dram_tensor("st_c", [nch, 128, 128], BF16).ap()
    st_g = nc.dram_tensor("st_g", [nch, 128, 32], BF16).ap()
    expa_scr = nc.dram_tensor("expa_scr", [seg_len // Q, 32 * Q], BF16).ap()
    cd_scr = nc.dram_tensor("cd_scr", [nch, 32], F32).ap()
    wout_bfh = nc.dram_tensor("wout_bfh", [DI, DM], BF16).ap()
    wf_bfh = nc.dram_tensor("wf_bfh", [DM, DM], BF16).ap()
    a1_bfh = nc.dram_tensor("a1_bfh", [DM, DM], BF16).ap()
    a2_bfh = nc.dram_tensor("a2_bfh", [DM, DM], BF16).ap()
    exch_in = nc.dram_tensor("exch_in", [129, DI], F32).ap()
    gath = nc.dram_tensor("gath", [NCORE * 129, DI], F32).ap()

    dbg = {}
    if debug:
        dbg["d_ypre"] = nc.dram_tensor("d_ypre", [seg_len, DI], F32, kind="ExternalOutput").ap()
        dbg["d_sz"] = nc.dram_tensor("d_sz", [seg_len, DI], F32, kind="ExternalOutput").ap()
        dbg["d_r"] = nc.dram_tensor("d_r", [129, DI], F32, kind="ExternalOutput").ap()
        dbg["d_R"] = nc.dram_tensor("d_R", [128, DI], F32, kind="ExternalOutput").ap()
        dbg["d_dt"] = nc.dram_tensor("d_dt", [32, SUP], F32, kind="ExternalOutput").ap()
        dbg["d_yg"] = nc.dram_tensor("d_yg", [SUP, DI], F32, kind="ExternalOutput").ap()
        dbg["d_h"] = nc.dram_tensor("d_h", [SUP, DM], F32, kind="ExternalOutput").ap()
        dbg["d_f1"] = nc.dram_tensor("d_f1", [SUP, DM], F32, kind="ExternalOutput").ap()
        dbg["d_ynb"] = nc.dram_tensor("d_ynb", [SUP, DI], BF16, kind="ExternalOutput").ap()
        dbg["d_xn32"] = nc.dram_tensor("d_xn32", [SUP, DM], F32, kind="ExternalOutput").ap()

    P = Prog()

    with contextlib.ExitStack() as ST:
        def sb(name, p, f, dt):
            return ST.enter_context(nc.sbuf_tensor(name, [p, f], dt))

        x_t = sb("x_t", 128, DM, F32)
        bnr = sb("bnr", 128, 24, F32)
        bna = sb("bna", 128, 2, F32)
        col1 = sb("col1", 128, 1, F32)
        xn_tmb = sb("xn_tmb", 128, DM, BF16)
        xn_fm = sb("xn_fm", 128, KD * SUP, BF16)
        dtb = sb("dtb", 32, 1, F32)
        alog = sb("alog", 32, 1, F32)
        acol = sb("acol", 32, 1, F32)
        dti = sb("dti", 128, 32, F32)
        rmsw = sb("rmsw", 128, 16, F32)
        ibf = sb("ibf", 128, 128, BF16)
        irp = sb("irp", 128, 512, BF16)
        alct = sb("alct", 128, 8, F32)
        gamt = sb("gamt", 128, 8, F32)
        igamt = sb("igamt", 128, 8, F32)
        cwt = sb("cwt", 128, NBC * 4, F32)
        cbt = sb("cbt", 128, NBC, F32)
        sig = sb("sig", 32, 1, F32)
        sig2 = sb("sig2", 32, 1, F32)
        cep = sb("cep", 128, 1, F32)
        col2 = sb("col2", 128, 1, F32)

        pA = ST.enter_context(nc.psum_tensor("pA", [128, DI], F32))
        pB = ST.enter_context(nc.psum_tensor("pB", [128, DI], F32))

        sems = {}
        for s in SEMS:
            sems[s] = ST.enter_context(nc.semaphore("s_" + s))

        def bcast_col(t, n_inner):
            b = t[:]
            return AP(b.tensor, b.offset, [[b.ap[0][0], b.ap[0][1]], [0, n_inner]])

        def hb(t, off, h_cnt, p_cnt, pdim=128):
            b = t[:]
            return AP(b.tensor, b.offset + off, [[b.ap[0][0], pdim], [1, h_cnt], [0, p_cnt]])

        def r3(t, off, pdim, dims):
            b = t[:]
            return AP(b.tensor, b.offset + off, [[b.ap[0][0], pdim]] + [list(d) for d in dims])

        def hq(ap_):
            return ap_.rearrange("p (h q) -> p h q", h=NH)

        def dmaf(out, in_, **kw):
            return lambda e: e.dma_start(out=out, in_=in_, **kw)

        with contextlib.ExitStack() as S1:
            def sb1(name, p, f, dt):
                return ST.enter_context(nc.sbuf_tensor(name, [p, f], dt))

            def sb1p(name, p, f, dt):
                return S1.enter_context(nc.sbuf_tensor(name, [p, f], dt))

            w_in_bf = sb1p("w_in_bf", 128, KD * 4384, BF16)
            wcst = sb1p("wcst", 128, 512, F32)
            wcbf = sb1p("wcbf", 128, 512, BF16)
            S0 = contextlib.ExitStack()
            wstage = S0.enter_context(nc.sbuf_tensor("wstage", [128, 4384], F32))
            wstage2 = S0.enter_context(nc.sbuf_tensor("wstage2", [128, 4384], F32))
            P.emit("gpsimd", dmaf(x_t[0:16, 0:DM], x_halo), inc="gp", step=16)
            halo_gp = P.v("gp")
            _emit_setup(P, nc, locals())
            S0.close()

            wrounds = []
            for k in range(16):
                for h in range(2):
                    wrounds.append((w_out, wout_bfh, k, h, True))
            for (wsrc, wdst) in ((wf, wf_bfh), (a1, a1_bfh), (a2, a2_bfh)):
                for k in range(KD):
                    for h in range(2):
                        wrounds.append((wsrc, wdst, k, h, False))
            wr_n = len(wrounds)
            wr_ld = [0]
            wr_cv = [0]
            wr_load_sc = {}
            wr_conv_ve = {}
            wr_store_gp = {}

            def emit_wround_load():
                i = wr_ld[0]
                if i >= wr_n:
                    return
                wr_ld[0] = i + 1
                src, dst, k, h, scl = wrounds[i]
                w = [("ve", wr_conv_ve[i - 1])] if i >= 1 else []
                P.emit("scalar", dmaf(
                    wcst[:], src[k * 128:(k + 1) * 128, h * 512:(h + 1) * 512]),
                    waits=w, inc="sc", step=16)
                wr_load_sc[i] = P.v("sc")

            def emit_wround_conv():
                i = wr_cv[0]
                if i >= wr_n or i >= wr_ld[0]:
                    return
                wr_cv[0] = i + 1
                src, dst, k, h, scl = wrounds[i]
                w = [("sc", wr_load_sc[i])]
                if i >= 1:
                    w.append(("gp", wr_store_gp[i - 1]))
                P.emit("vector", lambda e, k=k, scl=scl: e.tensor_scalar(
                    out=wcbf[:], in0=wcst[:],
                    scalar1=(rmsw[:, k:k + 1] if scl else 1.0),
                    scalar2=None, op0=ALU.mult),
                    waits=w, inc="ve")
                wr_conv_ve[i] = P.v("ve")
                P.emit("gpsimd", dmaf(
                    dst[k * 128:(k + 1) * 128, h * 512:(h + 1) * 512], wcbf[:]),
                    waits=[("ve", wr_conv_ve[i])], inc="gp", step=16)
                wr_store_gp[i] = P.v("gp")
            cbuf = sb1p("cbuf", 128, 2 * 520, F32)
            tail = sb1p("tail", 128, NBC * 4, F32)
            ca_v = sb1p("ca_v", 128, 2 * 512, F32)
            ca_p = sb1p("ca_p", 128, 3 * 512, F32)
            xbc_bf = sb1p("xbc_bf", 128, 4 * 512, BF16)
            t_dt = sb1p("t_dt", 32, SUP, F32)
            t_cs = t_dt
            t_u1 = sb1p("t_u1", 32, SUP, F32)
            t_u2 = sb1p("t_u2", 32, SUP, F32)
            t_A = t_u2
            w_bf = sb1p("w_bf", 32, SUP, BF16)
            t_g = sb1p("t_g", 32, SUP, F32)
            t_gsb = sb1p("t_gsb", 32, SUP, BF16)
            t_gcb = sb1p("t_gcb", 32, SUP, BF16)
            t_dsb = sb1p("t_dsb", 32, SUP, BF16)
            t_dtb = sb1p("t_dtb", 32, SUP, BF16)
            cd4 = sb1p("cd4", 32, CPS, F32)
            xtm = sb1p("xtm", 128, CPS * DI, BF16)
            btm = sb1p("btm", 128, CPS * 128, BF16)
            dttm = sb1p("dttm", 128, CPS * 32, BF16)
            gctm = sb1p("gctm", 128, CPS * 32, BF16)
            gstm = sb1p("gstm", 128, CPS * 32, BF16)
            dstm = sb1p("dstm", 128, CPS * 32, BF16)
            eab = sb1p("eab", 128, 4 * 1024, BF16)
            t_eab_b = sb1p("t_eab_b", 32, SUP, BF16)
            e_t = sb1p("e_t", 128, NH * Q, BF16)
            m_t = sb1p("m_t", 128, DI, BF16)
            g_sb = sb1p("g_sb", 128, 128, BF16)
            xt_t = sb1p("xt_t", 128, DI, BF16)
            xd_t = sb1p("xd_t", 128, DI, BF16)
            xtd = sb1p("xtd", 128, DI, BF16)
            r_bf = sb1p("r_bf", 128, DI, BF16)
            r_t = sb1p("r_t", 128, DI, F32)
            dcb = sb1p("dcb", 128, CPS * 32, F32)
            ypre = sb1p("ypre", 128, DI, F32)
            sz_t = sb1p("sz_t", 128, DI, F32)
            yp_b = sb1p("yp_b", 128, DI, BF16)

            P.emit("vector", lambda e: e.memset(r_t[:], 0.0), inc="ve")
            P.emit("vector", lambda e: e.memset(sig[:], 1.0), inc="ve")
            P.emit("vector", lambda e: e.memset(tail[:], 0.0), inc="ve")

            P.emit("vector", lambda e: e.bn_stats(out=bnr[0:16, 0:6], in_=x_t[0:16, 0:512]),
                   waits=[("gp", halo_gp)], inc="ve")
            P.emit("vector", lambda e: e.bn_stats(out=bnr[0:16, 6:12], in_=x_t[0:16, 512:1024]),
                   inc="ve")
            P.emit("vector", lambda e: e.bn_aggr(out=bna[0:16, :], in_=bnr[0:16, 0:12]),
                   waits=[("ve", P.v("ve"))], inc="ve")
            P.emit("scalar", lambda e: e.activation(out=col1[0:16, :], in_=bna[0:16, 1:2],
                                                    func=AF.Ln, bias=cep[0:16, :]),
                   waits=[("ve", P.v("ve"))], inc="se")
            P.emit("scalar", lambda e: e.activation(out=col2[0:16, :], in_=col1[0:16, :],
                                                    func=AF.Exp, scale=-0.5), inc="se")
            P.emit("vector", lambda e: e.scalar_tensor_tensor(
                out=xn_tmb[0:16, :], in0=x_t[0:16, 0:DM], scalar=bna[0:16, 0:1],
                in1=AP(col2[:].tensor, col2[:].offset, [[col2[:].ap[0][0], 16], [0, DM]]),
                op0=ALU.subtract, op1=ALU.mult),
                waits=[("se", P.v("se"))], inc="ve")
            P.emit("sync", lambda e: e.dma_start_transpose(
                r3(xn_fm, 0, 128, [(16, KD), (1, 16)]), xn_tmb[0:16, :]),
                waits=[("ve", P.v("ve"))], inc="sy", step=16)
            hsy = P.v("sy")
            for f in range(NBC):
                for k in range(KD):
                    w0 = ([("sy", hsy), ("ve", P.v("ve")), ("se", P.v("se"))]
                          if k == 0 else [])
                    P.emit("tensor", lambda e, f=f, k=k: e.matmul(
                        pB[:, 0:16],
                        w_in_bf[:, k * 4384 + 2048 + f * 128:k * 4384 + 2048 + (f + 1) * 128],
                        xn_fm[:, k * 16:(k + 1) * 16],
                        start=(k == 0), stop=(k == KD - 1)),
                        waits=w0, inc=("te" if k == KD - 1 else None))
                P.emit("scalar", lambda e, f=f: e.copy(out=tail[:, f * 4:f * 4 + 3],
                                                       in_=pB[:, 13:16]),
                       waits=[("te", P.v("te"))], inc="se")

            ln_sy = {}
            xnfm_free = [P.v("te")]

            def emit_ln_chunk(s2, c2):
                t0 = s2 * SUP + c2 * Q
                P.emit("scalar", dmaf(x_t[:], x_seg[t0:t0 + 128, :]),
                       waits=[("ve", P.v("ve"))], inc="sc", step=16)
                xs_ = P.v("sc")
                P.emit("vector", lambda e: e.bn_stats(
                    out=bnr[:, 0:6], in_=x_t[:, 0:512]),
                    waits=[("sc", xs_), ("se", P.v("se"))], inc="ve")
                P.emit("vector", lambda e: e.bn_stats(
                    out=bnr[:, 6:12], in_=x_t[:, 512:1024]),
                    inc="ve")
                P.emit("vector", lambda e: e.bn_aggr(out=bna[:], in_=bnr[:, 0:12]),
                       waits=[("ve", P.v("ve"))], inc="ve")
                P.emit("scalar", lambda e: e.activation(out=col1[:], in_=bna[:, 1:2],
                                                        func=AF.Ln, bias=cep[:]),
                       waits=[("ve", P.v("ve"))], inc="se")
                P.emit("scalar", lambda e: e.activation(out=col2[:], in_=col1[:],
                                                        func=AF.Exp, scale=-0.5),
                       inc="se")
                P.emit("vector", lambda e: e.scalar_tensor_tensor(
                    out=xn_tmb[:], in0=x_t[:],
                    scalar=bna[:, 0:1], in1=bcast_col(col2, DM),
                    op0=ALU.subtract, op1=ALU.mult),
                    waits=[("se", P.v("se")), ("sy", P.v("sy"))], inc="ve")
                P.emit("sync", lambda e, c2=c2: e.dma_start_transpose(
                    r3(xn_fm, c2 * 128, 128, [(SUP, KD), (1, 128)]), xn_tmb[:]),
                    waits=[("ve", P.v("ve")), ("te", xnfm_free[0])],
                    inc="sy", step=16)
                ln_sy[s2] = P.v("sy")

            for s in range(nsup):
                sup_off = s * SUP
                if s == 0:
                    for c in range(CPS):
                        emit_ln_chunk(0, c)
                xn_sy = ln_sy[s]

                tsy = {}
                copy_se = {}
                conv_po = {}
                for f in range(NBC):
                    cslot = f % 2
                    half = f % 2
                    xoff = (f % 2) * 512 if f < 16 else (1024 if f == 16 else 1536)
                    for k in range(KD):
                        if k != 0:
                            w0 = []
                        elif f < 2:
                            w0 = [("sy", xn_sy), ("se", P.v("se")),
                                  ("ve", P.v("ve"))]
                        else:
                            w0 = [("se", copy_se[f - 2])]
                        P.emit("tensor", lambda e, f=f, k=k, half=half: e.matmul(
                            pB[:, half * 512:(half + 1) * 512],
                            w_in_bf[:, k * 4384 + 2048 + f * 128:k * 4384 + 2048 + (f + 1) * 128],
                            xn_fm[:, k * SUP:(k + 1) * SUP],
                            start=(k == 0), stop=(k == KD - 1)),
                            waits=w0, inc=("te" if k == KD - 1 else None))
                    te_mm = P.v("te")
                    P.emit("vector", lambda e, cslot=cslot, f=f: e.tensor_copy(
                        cbuf[:, cslot * 520:cslot * 520 + 3], tail[:, f * 4:f * 4 + 3]),
                        waits=[("po", P.v("po")), ("se", P.v("se"))], inc="ve")
                    cp_w = [("te", te_mm), ("ve", P.v("ve"))]
                    if f >= 2 and (f % 2) == 1:
                        cp_w.append(("po", conv_po[f - 2]))
                    P.emit("scalar", lambda e, cslot=cslot, half=half: e.copy(
                        out=cbuf[:, cslot * 520 + 3:cslot * 520 + 515],
                        in_=pB[:, half * 512:(half + 1) * 512]),
                        waits=cp_w, inc="se")
                    ev_se = P.v("se")
                    copy_se[f] = ev_se
                    P.emit("vector", lambda e, cslot=cslot, f=f: e.tensor_copy(
                        tail[:, f * 4:f * 4 + 3],
                        cbuf[:, cslot * 520 + 512:cslot * 520 + 515]),
                        waits=[("se", ev_se)], inc="ve")
                    ceng = "vector" if f % 2 == 0 else "gpsimd"
                    cinc = "ve" if ceng == "vector" else "po"
                    cab = ca_v if f % 2 == 0 else ca_p
                    off = cslot * 520
                    if ceng == "vector":
                        P.emit(ceng, lambda e, off=off, f=f, cab=cab: e.tensor_scalar(
                            out=cab[:, 0:512], in0=cbuf[:, off:off + 512],
                            scalar1=cwt[:, f * 4:f * 4 + 1], scalar2=None, op0=ALU.mult),
                            waits=[("se", P.v("se"))], inc=cinc)
                        for j in (1, 2, 3):
                            src2 = 0 if j % 2 == 1 else 512
                            dst2 = 512 - src2
                            P.emit(ceng, lambda e, off=off, f=f, j=j, src2=src2, dst2=dst2,
                                   cab=cab: e.scalar_tensor_tensor(
                                       out=cab[:, dst2:dst2 + 512],
                                       in0=cbuf[:, off + j:off + j + 512],
                                       scalar=cwt[:, f * 4 + j:f * 4 + j + 1],
                                       in1=cab[:, src2:src2 + 512],
                                       op0=ALU.mult, op1=ALU.add), inc=cinc)
                    else:
                        def wcol(f, j):
                            b = cwt[:]
                            return AP(b.tensor, b.offset + f * 4 + j,
                                      [[b.ap[0][0], 128], [0, 512]])
                        P.emit(ceng, lambda e, off=off, f=f, cab=cab: e.tensor_tensor(
                            out=cab[:, 0:512], in0=cbuf[:, off:off + 512],
                            in1=wcol(f, 0), op=ALU.mult),
                            waits=[("se", P.v("se"))], inc=cinc)
                        plan = [(1, 512, 0, 512, 1024), (2, 0, 1024, 0, 512),
                                (3, 0, 512, 0, 1024)]
                        for (j, mdst, a0, a1_, adst) in plan:
                            P.emit(ceng, lambda e, off=off, f=f, j=j, mdst=mdst, cab=cab:
                                   e.tensor_tensor(
                                       out=cab[:, mdst:mdst + 512],
                                       in0=cbuf[:, off + j:off + j + 512],
                                       in1=wcol(f, j), op=ALU.mult), inc=cinc)
                            P.emit(ceng, lambda e, cab=cab, a0=a0, a1_=a1_, adst=adst:
                                   e.tensor_tensor(
                                       out=cab[:, adst:adst + 512], in0=cab[:, a0:a0 + 512],
                                       in1=cab[:, a1_:a1_ + 512], op=ALU.add), inc=cinc)
                    conv_po[f] = P.v("po")
                    sil_src = (512, 1024) if ceng == "vector" else (1024, 1536)
                    silw = [(cinc, P.v(cinc))]
                    if f >= 2 and f < 16:
                        silw.append(("sy", tsy[f - 2]))
                    if f >= 16:
                        silw.append(("te", P.v("te")))
                        silw.append(("gp", P.v("gp")))
                    P.emit("scalar", lambda e, f=f, cab=cab, sil_src=sil_src, xoff=xoff:
                           e.activation(
                               out=xbc_bf[:, xoff:xoff + 512],
                               in_=cab[:, sil_src[0]:sil_src[1]],
                               func=AF.Silu, bias=cbt[:, f:f + 1]),
                           waits=silw, inc="se")
                    sil_se = P.v("se")
                    if f < 16:
                        P.emit("sync", lambda e, f=f, xoff=xoff: e.dma_start_transpose(
                            r3(xtm, f * 128, 128, [(DI, CPS), (1, 128)]),
                            xbc_bf[:, xoff:xoff + 512]),
                            waits=[("se", sil_se)], inc="sy", step=16)
                        tsy[f] = P.v("sy")
                    elif f == 16:
                        P.emit("sync", lambda e: e.dma_start_transpose(
                            r3(btm, 0, 128, [(128, CPS), (1, 128)]),
                            xbc_bf[:, 1024:1536]),
                            waits=[("se", sil_se)], inc="sy", step=16)
                    else:
                        P.emit("gpsimd", lambda e, s=s: e.dma_start(
                            out=AP(st_c.tensor, st_c.offset + s * CPS * 128 * 128,
                                   [[128, 128], [128 * 128, CPS], [1, 128]]),
                            in_=r3(xbc_bf, 1536, 128, [(128, CPS), (1, 128)])),
                            waits=[("se", sil_se)], inc="gp", step=16)
                tm_sy = P.v("sy")
                conv_se = P.v("se")

                for c in range(CPS):
                    ch = s * CPS + c
                    for q in range(4):
                        for k in range(KD):
                            w0 = ([("ve", P.v("ve")), ("se", P.v("se"))]
                                  if (q == 0 and k == 0) else [])
                            P.emit("tensor", lambda e, q=q, k=k, c=c: e.matmul(
                                pA[:, q * 512:(q + 1) * 512],
                                xn_fm[:, k * SUP + c * 128:k * SUP + (c + 1) * 128],
                                w_in_bf[:, k * 4384 + q * 512:k * 4384 + (q + 1) * 512],
                                start=(k == 0), stop=(k == KD - 1)),
                                waits=w0,
                                inc=("te" if (k == KD - 1 and q == 3) else None))
                    z_te = P.v("te")
                    for q in range(4):
                        P.emit("scalar", lambda e, q=q: e.activation(
                            out=yp_b[:, q * 512:(q + 1) * 512],
                            in_=pA[:, q * 512:(q + 1) * 512], func=AF.Silu),
                            waits=([("te", z_te), ("gp", P.v("gp"))] if q == 0 else []),
                            inc="se")
                    z_se = P.v("se")
                    P.emit("gpsimd", lambda e, ch=ch: e.dma_start(
                        out=st_sz[ch * 128:(ch + 1) * 128, :], in_=yp_b[:]),
                        waits=[("se", z_se)], inc="gp", step=16)
                zphase_se = P.v("se")

                for k in range(KD):
                    w0 = [("se", conv_se), ("ve", P.v("ve"))] if k == 0 else []
                    P.emit("tensor", lambda e, k=k: e.matmul(
                        pB[0:32, 1024:1536],
                        w_in_bf[:, k * 4384 + 4352:k * 4384 + 4384],
                        xn_fm[:, k * SUP:(k + 1) * SUP],
                        start=(k == 0), stop=(k == KD - 1)),
                        waits=w0, inc=("te" if k == KD - 1 else None))
                dt_te = P.v("te")
                xnfm_free[0] = dt_te
                P.emit("vector", lambda e: e.tensor_scalar(
                    out=t_u1[:], in0=pB[0:32, 1024:1536], scalar1=dtb[:],
                    scalar2=0.0, op0=ALU.add, op1=ALU.max),
                    waits=[("te", dt_te)], inc="ve")
                P.emit("vector", lambda e: e.tensor_scalar(
                    out=t_u2[:], in0=pB[0:32, 1024:1536], scalar1=dtb[:],
                    scalar2=None, op0=ALU.add), inc="ve")
                P.emit("vector", lambda e: e.scalar_tensor_tensor(
                    out=t_g[:], in0=t_u1[:], scalar=-2.0, in1=t_u2[:],
                    op0=ALU.mult, op1=ALU.add), inc="ve")
                P.emit("scalar", lambda e: e.activation(out=t_u2[:], in_=t_g[:],
                                                        func=AF.Exp),
                       waits=[("ve", P.v("ve"))], inc="se")
                P.emit("scalar", lambda e: e.activation(out=t_g[:], in_=t_u2[:],
                                                        func=AF.Ln, bias=1.0), inc="se")
                P.emit("vector", lambda e: e.tensor_tensor(out=t_dt[:], in0=t_u1[:],
                                                           in1=t_g[:], op=ALU.add),
                       waits=[("se", P.v("se"))], inc="ve")
                P.emit("vector", lambda e: e.tensor_scalar(
                    out=t_A[:], in0=t_dt[:], scalar1=acol[:], scalar2=None,
                    op0=ALU.mult), inc="ve")
                P.emit("vector", lambda e: e.tensor_scalar(
                    out=t_dtb[:], in0=t_dt[:], scalar1=1.0, scalar2=None, op0=ALU.mult),
                    inc="ve")
                for c in range(CPS):
                    P.emit("vector", lambda e, c=c: e.tensor_tensor_scan(
                        out=t_cs[:, c * 128:(c + 1) * 128],
                        data0=t_A[:, c * 128:(c + 1) * 128],
                        data1=t_A[:, c * 128:(c + 1) * 128], initial=0.0,
                        op0=ALU.add, op1=ALU.bypass), inc="ve")
                cs_ve = P.v("ve")
                P.emit("scalar", lambda e: e.activation(out=t_eab_b[:], in_=t_A[:],
                                                        func=AF.Exp),
                       waits=[("ve", cs_ve), ("gp", P.v("gp"))], inc="se")
                P.emit("vector", lambda e: e.memset(
                    AP(t_eab_b[:].tensor, t_eab_b[:].offset,
                       [[t_eab_b[:].ap[0][0], 32], [128, CPS]]),
                    0.0),
                    waits=[("se", P.v("se"))], inc="ve")
                P.emit("gpsimd", lambda e, s=s: e.dma_start(
                    out=AP(expa_scr.tensor, expa_scr.offset + s * CPS * 32 * Q,
                           [[Q, 32], [32 * Q, CPS], [1, Q]]),
                    in_=t_eab_b[:]),
                    waits=[("ve", P.v("ve"))], inc="gp", step=16)
                expa_gp_s = P.v("gp")
                P.emit("scalar", lambda e: e.activation(out=t_g[:], in_=t_cs[:], func=AF.Exp),
                       waits=[("ve", cs_ve)], inc="se")
                g_se = P.v("se")
                P.emit("vector", lambda e: e.tensor_scalar(
                    out=t_gcb[:], in0=t_g[:], scalar1=1.0, scalar2=None, op0=ALU.mult),
                    waits=[("se", g_se), ("sy", P.v("sy"))], inc="ve")
                for c in range(CPS):
                    ch0 = s * CPS + c
                    scur = sig if ch0 % 2 == 0 else sig2
                    snxt = sig2 if ch0 % 2 == 0 else sig
                    P.emit("vector", lambda e, c=c, scur=scur: e.tensor_scalar(
                        out=t_gsb[:, c * 128:(c + 1) * 128],
                        in0=t_g[:, c * 128:(c + 1) * 128],
                        scalar1=scur[:], scalar2=None, op0=ALU.mult), inc="ve")
                    P.emit("scalar", lambda e, c=c: e.activation(
                        out=cd4[:, c:c + 1], in_=t_cs[:, c * 128 + 127:c * 128 + 128],
                        func=AF.Exp),
                        waits=[("ve", P.v("ve")), ("gp", P.v("gp"))], inc="se")
                    P.emit("vector", lambda e, c=c, scur=scur, snxt=snxt: e.tensor_scalar(
                        out=snxt[:], in0=scur[:], scalar1=cd4[:, c:c + 1], scalar2=None,
                        op0=ALU.mult),
                        waits=[("se", P.v("se"))], inc="ve")
                P.emit("vector", lambda e: e.tensor_tensor(
                    out=t_u1[:], in0=t_cs[:],
                    in1=AP(t_cs[:].tensor, t_cs[:].offset + 127,
                           [[t_cs[:].ap[0][0], 32], [128, CPS], [0, 128]]),
                    op=ALU.subtract), inc="ve")
                P.emit("scalar", lambda e: e.activation(out=t_dsb[:], in_=t_u1[:],
                                                        func=AF.Exp, scale=-1.0),
                       waits=[("ve", P.v("ve"))], inc="se")
                P.emit("vector", lambda e: e.tensor_tensor(
                    out=w_bf[:], in0=t_dtb[:], in1=t_dsb[:], op=ALU.mult),
                    waits=[("se", P.v("se"))], inc="ve")
                dt_ve = P.v("ve")
                dt_se = P.v("se")
                P.emit("gpsimd", lambda e, s=s: e.dma_start(
                    out=AP(cd_scr.tensor, cd_scr.offset + s * CPS * 32,
                           [[1, 32], [32, CPS]]),
                    in_=cd4[:], allow_slow_non_contiguous=True),
                    waits=[("ve", dt_ve), ("se", dt_se)], inc="gp", step=16)
                P.emit("scalar", lambda e, s=s: e.dma_start(
                    out=dcb[:], in_=AP(cd_scr.tensor, cd_scr.offset + s * CPS * 32,
                                       [[0, 128], [1, CPS * 32]])),
                    waits=[("gp", P.v("gp")), ("po", P.v("po"))], inc="dc", step=16)
                dc_s = P.v("dc")
                P.emit("sync", lambda e: e.dma_start_transpose(
                    r3(dttm, 0, 128, [(32, CPS), (1, 32)]), t_dtb[:]),
                    waits=[("ve", dt_ve), ("po", P.v("po")), ("te", P.v("te"))],
                    inc="sy", step=16)
                P.emit("sync", lambda e: e.dma_start_transpose(
                    r3(gctm, 0, 128, [(32, CPS), (1, 32)]), t_gcb[:]),
                    inc="sy", step=16)
                P.emit("sync", lambda e: e.dma_start_transpose(
                    r3(gstm, 0, 128, [(32, CPS), (1, 32)]), t_gsb[:]),
                    inc="sy", step=16)
                P.emit("sync", lambda e: e.dma_start_transpose(
                    r3(dstm, 0, 128, [(32, CPS), (1, 32)]), w_bf[:]),
                    waits=[("ve", dt_ve)], inc="sy", step=16)
                P.emit("gpsimd", lambda e, s=s: e.dma_start(
                    out=AP(st_g.tensor, st_g.offset + s * CPS * 128 * 32,
                           [[32, 128], [128 * 32, CPS], [1, 32]]),
                    in_=r3(gstm, 0, 128, [(32, CPS), (1, 32)])),
                    waits=[("sy", P.v("sy"))], inc="gp", step=16)
                if s + 1 < nsup:
                    emit_ln_chunk(s + 1, 0)

                def emit_G(c, wv):
                    P.emit("tensor", lambda e, c=c: e.matmul(
                        pA[:, 1024:1152],
                        xbc_bf[:, 1024 + c * 128:1024 + (c + 1) * 128],
                        xbc_bf[:, 1536 + c * 128:1536 + (c + 1) * 128],
                        start=True, stop=True),
                        waits=wv, inc="te")
                    P.emit("vector", lambda e: e.tensor_copy(g_sb[:], pA[:, 1024:1152]),
                           waits=[("te", P.v("te"))], inc="ve")
                    return P.v("ve")

                def emit_eab_loads(ch2):
                    s2 = ch2 // CPS
                    P.emit("scalar", lambda e, ch2=ch2: e.dma_start(
                        out=eab[:, 0:4096],
                        in_=AP(expa_scr.tensor, expa_scr.offset + ch2 * 32 * Q,
                               [[0, 128], [1, 32 * Q]])),
                        waits=list(eab_cons[0]) + [("gp", expa_gp[s2])],
                        inc="e0", step=16)
                    eab_load_e0[ch2] = P.v("e0")

                def emit_scans(ch2):
                    for ll in range(4):
                        for i4 in range(2):
                            i = ll * 2 + i4
                            w0 = []
                            if i == 0:
                                w0 = [("e0", eab_load_e0[ch2]),
                                      ("ve", mtil_done[0])]
                            P.emit("vector", lambda e, i=i, i4=i4, ll=ll:
                                   e.tensor_tensor_scan(
                                       out=e_t[:, i * 512:(i + 1) * 512],
                                       data0=eab[:, ll * 1024 + i4 * 512:ll * 1024 + (i4 + 1) * 512],
                                       data1=irp[:], initial=0.0,
                                       op0=ALU.mult, op1=ALU.add),
                                   waits=w0, inc="ve")
                    eab_cons[0] = [("ve", P.v("ve"))]
                    scan_cnt[0] = (P.v("ve"), 0)
                    return P.v("ve")

                if s == 0:
                    eab_cons = [[]]
                    eab_load_e0 = {}
                    expa_gp = {}
                    mtil_done = [0]
                    scan_cnt = [(0, 0)]
                    prev_ev2 = 0
                    prev_ev1 = 0
                    prev_add_po = 0
                    prev_s_te = 0
                    prev_yo_te = 0
                    prev_r2 = 0
                expa_gp[s] = expa_gp_s

                emit_eab_loads(s * CPS)
                gsb_ve = emit_G(0, [("ve", prev_ev2), ("se", zphase_se),
                                    ("te", P.v("te")), ("ve", P.v("ve"))])
                scan_done = emit_scans(s * CPS)
                P.emit("vector", lambda e: e.tensor_tensor(
                    out=hq(xt_t[:]), in0=hq(xtm[:, 0:DI]),
                    in1=r3(dttm, 0, 128, [(1, NH), (0, HP)]), op=ALU.mult),
                    waits=[("sy", tm_sy), ("te", P.v("te"))], inc="ve")
                xtil_done = ("ve", P.v("ve"))
                P.emit("gpsimd", lambda e: e.tensor_tensor(
                    out=hq(xd_t[:]), in0=hq(xtm[:, 0:DI]),
                    in1=r3(dti, 0, 128, [(1, NH), (0, HP)]), op=ALU.mult),
                    waits=[("sy", tm_sy), ("te", P.v("te"))], inc="po")
                xdt_done = ("po", P.v("po"))

                for c in range(CPS):
                    ch = s * CPS + c
                    for q in range(4):
                        w0 = ([("se", zphase_se), xdt_done, ("ve", P.v("ve"))]
                              if q == 0 else [])
                        P.emit("tensor", lambda e, q=q: e.matmul(
                            pA[:, q * 512:(q + 1) * 512], ibf[:],
                            xd_t[:, q * 512:(q + 1) * 512], start=True, stop=False),
                            waits=w0, inc=("te" if q == 3 else None))
                    imm_te = P.v("te")
                    for mh in range(2):
                        P.emit("vector", lambda e, mh=mh: e.tensor_tensor(
                            out=m_t[:].rearrange("p (h q) -> p h q", h=NH // 2),
                            in0=e_t[:, mh * DI:(mh + 1) * DI]
                                .rearrange("p (h q) -> p h q", h=NH // 2),
                            in1=r3(g_sb, 0, 128, [(0, NH // 2), (1, 128)]),
                            op=ALU.mult),
                            waits=([("ve", scan_cnt[0][0])] if mh == 0 else
                                   [("te", P.v("te"))]), inc="ve")
                        mtil_v = P.v("ve")
                        if mh == 1:
                            mtil_done[0] = mtil_v
                        for h2 in range(NH // 2):
                            h = mh * 16 + h2
                            w0 = ([("ve", mtil_v), xtil_done] if h2 == 0 else [])
                            P.emit("tensor", lambda e, h=h, h2=h2: e.matmul(
                                pA[:, h * 64:(h + 1) * 64],
                                m_t[:, h2 * 128:(h2 + 1) * 128],
                                xt_t[:, h * 64:(h + 1) * 64], start=False,
                                stop=(h % 8 == 7)),
                                waits=w0,
                                inc=("te" if h2 == NH // 2 - 1 else None))
                    yd_te = P.v("te")

                    P.emit("gpsimd", lambda e, c=c: e.tensor_tensor(
                        out=hq(xtd[:]), in0=hq(xtm[:, c * DI:(c + 1) * DI]),
                        in1=r3(dstm, c * 32, 128, [(1, NH), (0, HP)]),
                        op=ALU.mult),
                        waits=[("te", prev_s_te), ("sy", tm_sy)], inc="po")
                    xtd_done = P.v("po")
                    P.emit("vector", lambda e: e.tensor_copy(r_bf[:], r_t[:]),
                           waits=[("te", prev_yo_te)], inc="ve")
                    rbf_ve = P.v("ve")
                    P.emit("gpsimd", lambda e, c=c: e.tensor_tensor(
                        out=hq(sz_t[:]), in0=hq(r_t[:]),
                        in1=r3(dcb, c * 32, 128, [(1, NH), (0, HP)]),
                        op=ALU.mult),
                        waits=[("dc", dc_s), ("ve", prev_r2)],
                        inc="po")
                    r1_po = P.v("po")
                    for q in range(4):
                        w0 = ([("ve", max(prev_ev1, dt_ve) if c == 0 else prev_ev1),
                               ("po", xtd_done)] if q == 0 else [])
                        P.emit("tensor", lambda e, c=c, q=q: e.matmul(
                            pB[:, q * 512:(q + 1) * 512],
                            btm[:, c * 128:(c + 1) * 128],
                            xtd[:, q * 512:(q + 1) * 512], start=True, stop=True),
                            waits=w0, inc=("te" if q == 3 else None))
                    s_te = P.v("te")
                    P.emit("vector", lambda e: e.tensor_tensor(
                        out=r_t[:], in0=sz_t[:], in1=pB[:], op=ALU.add),
                        waits=[("po", r1_po), ("te", s_te)], inc="ve")
                    prev_r2 = P.v("ve")
                    for q in range(4):
                        w0 = [("ve", prev_r2)] if q == 0 else []
                        P.emit("tensor", lambda e, c=c, q=q: e.matmul(
                            pB[:, q * 512:(q + 1) * 512],
                            xbc_bf[:, 1536 + c * 128:1536 + (c + 1) * 128],
                            r_bf[:, q * 512:(q + 1) * 512], start=True, stop=True),
                            waits=w0, inc=("te" if q == 3 else None))
                    yo_te = P.v("te")
                    P.emit("vector", lambda e, c=c: e.tensor_tensor(
                        out=hq(ypre[:]), in0=hq(pB[:]),
                        in1=r3(gctm, c * 32, 128, [(1, NH), (0, HP)]),
                        op=ALU.mult),
                        waits=[("te", yo_te)], inc="ve")
                    ev1_ve = P.v("ve")
                    prev_ev1 = ev1_ve
                    if c < CPS - 1:
                        P.emit("gpsimd", lambda e, c=c: e.tensor_tensor(
                            out=hq(xt_t[:]), in0=hq(xtm[:, (c + 1) * DI:(c + 2) * DI]),
                            in1=r3(dttm, (c + 1) * 32, 128, [(1, NH), (0, HP)]),
                            op=ALU.mult),
                            waits=[("te", yd_te)], inc="po")
                        xtil_done = ("po", P.v("po"))
                        P.emit("gpsimd", lambda e, c=c: e.tensor_tensor(
                            out=hq(xd_t[:]), in0=hq(xtm[:, (c + 1) * DI:(c + 2) * DI]),
                            in1=r3(dti, 0, 128, [(1, NH), (0, HP)]),
                            op=ALU.mult),
                            waits=[("te", imm_te)], inc="po")
                        xdt_done = ("po", P.v("po"))
                    P.emit("vector", lambda e: e.tensor_tensor(
                        out=yp_b[:], in0=ypre[:], in1=pA[:], op=ALU.add),
                        waits=[("gp", P.v("gp")), ("te", yd_te)], inc="ve")
                    ev2_ve = P.v("ve")
                    prev_ev2 = ev2_ve
                    P.emit("gpsimd", lambda e, ch=ch: e.dma_start(
                        out=st_ypre[ch * 128:(ch + 1) * 128, :], in_=yp_b[:]),
                        waits=[("ve", ev2_ve)], inc="gp", step=16)
                    if c < CPS - 1:
                        gsb_ve = emit_G(c + 1, [("ve", ev2_ve)])
                    if c < CPS - 1:
                        emit_eab_loads(ch + 1)
                        scan_done = emit_scans(ch + 1)
                    for _wr in range(3 if ch < 16 else 2):
                        emit_wround_conv()
                        emit_wround_load()
                    if s + 1 < nsup and c < CPS - 1:
                        emit_ln_chunk(s + 1, c + 1)
                    prev_s_te = s_te
                    prev_yo_te = yo_te

            while wr_cv[0] < wr_n:
                emit_wround_conv()
                emit_wround_load()

            P.emit("gpsimd", dmaf(exch_in[0:128, :], r_t[:]),
                   waits=[("ve", P.v("ve"))], inc="gp", step=16)
            sig_fin = sig if nch % 2 == 0 else sig2
            P.emit("vector", lambda e: e.memset(ypre[0:32, :], 0.0),
                   waits=[("gp", P.v("gp"))], inc="ve")
            P.emit("gpsimd", lambda e: e.dma_start(
                out=exch_in[128:129, :], in_=ypre[0:1, :]),
                waits=[("ve", P.v("ve"))], inc="gp", step=16)
            P.emit("gpsimd", lambda e, sig_fin=sig_fin: e.dma_start(
                out=AP(exch_in.tensor, exch_in.offset + 128 * DI, [[1, 32], [1, 1]]),
                in_=sig_fin[:], allow_slow_non_contiguous=True),
                waits=[("gp", P.v("gp"))], inc="gp", step=16)
            P.emit("gpsimd", lambda e: e.collective_compute(
                "AllGather", ALU.bypass, replica_groups=[list(range(NCORE))],
                ins=[exch_in.opt()], outs=[gath.opt()]),
                waits=[("gp", P.v("gp"))], inc="cc")
            if debug:
                P.emit("gpsimd", dmaf(dbg["d_r"], exch_in),
                       waits=[("cc", P.v("cc"))], inc="gp", step=16)
                P.emit("gpsimd", dmaf(dbg["d_ypre"], st_ypre), inc="gp", step=16)
                P.emit("gpsimd", dmaf(dbg["d_sz"], st_sz), inc="gp", step=16)
                P.emit("gpsimd", lambda e: e.dma_start(out=dbg["d_dt"], in_=t_dt[:]),
                       inc="gp", step=16)

            barrier = P.barrier_vals()

        with contextlib.ExitStack() as S2:
            def sb2(name, p, f, dt):
                return S2.enter_context(nc.sbuf_tensor(name, [p, f], dt))

            woutb = sb2("woutb", 128, 16 * 1024, BF16)
            wffn = sb2("wffn", 128, KD * 1024, BF16)
            lngT = sb2("lngT", 128, DM, F32)
            lnbT = sb2("lnbT", 128, DM, F32)
            fbT = sb2("fbT", 128, DM, F32)
            R_t = sb2("R_t", 128, DI, F32)
            R_bf = sb2("R_bf", 128, DI, BF16)
            Dm_t = sb2("Dm_t", 128, 32, F32)
            Dm2_t = sb2("Dm2_t", 128, 32, F32)
            gr = sb2("gr", 128, DI, F32)
            yprd = sb2("yprd", 128, 3 * DI, BF16)
            szrd = sb2("szrd", 128, 3 * DI, BF16)
            crd = sb2("crd", 128, 3 * 128, BF16)
            grd = sb2("grd", 128, 3 * 32, BF16)
            y2 = sb2("y2", 128, DI, F32)
            ynb = sb2("ynb", 128, DI, BF16)
            ynfm = sb2("ynfm", 128, 2 * DI, BF16)
            bnrB = sb2("bnrB", 128, 12, F32)
            bnaB = sb2("bnaB", 128, 2, F32)
            col1B = sb2("col1B", 128, 1, F32)
            col2B = sb2("col2B", 128, 1, F32)
            muS = sb2("muS", 128, CPS, F32)
            sigS = sb2("sigS", 128, CPS, F32)
            xn32 = sb2("xn32", 128, 4 * DM, F32)
            h_t = sb2("h_t", 128, DM, F32)
            u1_t = sb2("u1_t", 128, DM, F32)
            f1_t = sb2("f1_t", 128, DM, F32)
            h1b = sb2("h1b", 128, KD * 512, BF16)
            h2b = sb2("h2b", 128, KD * 512, BF16)
            outst = sb2("outst", 128, 2 * DM, F32)

            first = {e: True for e in ENGS}

            def em(eng, fn, waits=(), inc=None, step=1):
                w = list(waits)
                if first[eng]:
                    w = barrier + w
                    first[eng] = False
                P.emit(eng, fn, w, inc, step)

            cc1 = P.v("cc")
            em("sync", dmaf(woutb[:].rearrange("p (a b) -> p a b", a=16),
                            wout_bfh.rearrange("(a p) b -> p a b", p=128)),
               waits=[("cc", cc1)], inc="sy", step=16)
            em("sync", dmaf(lngT[:], lng_t), inc="sy", step=16)
            em("sync", dmaf(lnbT[:], lnb_t), inc="sy", step=16)
            em("sync", dmaf(fbT[:], fbias_t), inc="sy", step=16)
            w2_sy = P.v("sy")

            R2_t = y2
            em("vector", lambda e: e.memset(R_t[:], 0.0), waits=[("cc", cc1)], inc="ve")
            for u in range(NCORE):
                em("scalar", lambda e, u=u: e.dma_start(
                    out=Dm_t[:], in_=AP(gath.tensor,
                                        gath.offset + (u * 129 + 128) * DI,
                                        [[0, 128], [1, 32]])),
                    waits=[("cc", cc1), ("ve", P.v("ve"))], inc="sc", step=16)
                em("vector", lambda e, u=u: e.tensor_scalar(
                    out=Dm2_t[:], in0=Dm_t[:], scalar1=gamt[:, u:u + 1],
                    scalar2=igamt[:, u:u + 1], op0=ALU.mult, op1=ALU.add),
                    waits=[("sc", P.v("sc"))], inc="ve")
                em("gpsimd", lambda e, u=u: e.dma_start(
                    out=gr[:], in_=gath[u * 129:u * 129 + 128, :]),
                    waits=[("cc", cc1), ("ve", P.v("ve"))], inc="gp", step=16)
                em("vector", lambda e: e.tensor_tensor(
                    out=hq(R2_t[:]), in0=hq(R_t[:]),
                    in1=r3(Dm2_t, 0, 128, [(1, NH), (0, HP)]), op=ALU.mult),
                    inc="ve")
                em("vector", lambda e, u=u: e.scalar_tensor_tensor(
                    out=R_t[:], in0=gr[:], scalar=alct[:, u:u + 1], in1=R2_t[:],
                    op0=ALU.mult, op1=ALU.add),
                    waits=[("gp", P.v("gp"))], inc="ve")
            em("vector", lambda e: e.tensor_scalar(
                out=R_bf[:], in0=R_t[:], scalar1=1.0, scalar2=None, op0=ALU.mult),
                inc="ve")
            R_ve = P.v("ve")
            if debug:
                em("gpsimd", dmaf(dbg["d_R"], R_t[:]), waits=[("ve", R_ve)],
                   inc="gp", step=16)

            yprd_free = {0: 0, 1: 0, 2: 0}
            szrd_free = {0: 0, 1: 0, 2: 0}
            crd_free = {0: 0, 1: 0, 2: 0}
            grd_free = {0: 0, 1: 0, 2: 0}
            ynfm_free = {0: 0, 1: 0}
            stash_sc = {}
            yn_sy_d = {}
            pa_free = [0]
            pb_lo_free = [0]
            pb_hi_free = [0]
            f1_free = [0]
            ln2_sy = {}
            ph2_xnfm_free = [P.v("te")]

            def emit_ln2_chunk(s2, c2):
                t0 = s2 * SUP + c2 * Q
                em("gpsimd", dmaf(x_t[:], x_seg[t0:t0 + 128, :]),
                   waits=[("ve", P.v("ve")), ("po", P.v("po"))], inc="gp", step=16)
                xs_ = P.v("gp")
                em("vector", lambda e: e.bn_stats(
                    out=bnr[:, 0:6], in_=x_t[:, 0:512]),
                    waits=[("gp", xs_), ("se", P.v("se"))], inc="ve")
                em("vector", lambda e: e.bn_stats(
                    out=bnr[:, 6:12], in_=x_t[:, 512:1024]),
                    inc="ve")
                em("vector", lambda e: e.bn_aggr(out=bna[:], in_=bnr[:, 0:12]),
                   waits=[("ve", P.v("ve"))], inc="ve")
                em("scalar", lambda e: e.activation(out=col1[:], in_=bna[:, 1:2],
                                                    func=AF.Ln, bias=cep[:]),
                   waits=[("ve", P.v("ve"))], inc="se")
                em("scalar", lambda e: e.activation(out=col2[:], in_=col1[:],
                                                    func=AF.Exp, scale=-0.5), inc="se")
                em("scalar", lambda e, c2=c2: e.activation(
                    out=sigS[:, c2:c2 + 1], in_=col1[:], func=AF.Exp, scale=0.5),
                    inc="se")
                em("vector", lambda e, c2=c2: e.tensor_copy(
                    muS[:, c2:c2 + 1], bna[:, 0:1]),
                    waits=[("se", P.v("se"))], inc="ve")
                em("vector", lambda e, c2=c2: e.scalar_tensor_tensor(
                    out=xn32[:, c2 * DM:(c2 + 1) * DM],
                    in0=x_t[:], scalar=bna[:, 0:1],
                    in1=bcast_col(col2, DM), op0=ALU.subtract, op1=ALU.mult),
                    inc="ve")
                em("vector", lambda e, c2=c2: e.tensor_scalar(
                    out=xn_tmb[:], in0=xn32[:, c2 * DM:(c2 + 1) * DM],
                    scalar1=1.0, scalar2=None, op0=ALU.mult),
                    waits=[("sy", P.v("sy"))], inc="ve")
                em("sync", lambda e, c2=c2: e.dma_start_transpose(
                    r3(xn_fm, c2 * 128, 128, [(SUP, KD), (1, 128)]), xn_tmb[:]),
                    waits=[("ve", P.v("ve")), ("te", ph2_xnfm_free[0])],
                    inc="sy", step=16)
                ln2_sy[s2] = P.v("sy")

            for s in range(nsup):
                sup_off = s * SUP
                if s == 0:
                    for c in range(CPS):
                        emit_ln2_chunk(0, c)
                xn2_sy = ln2_sy[s]

                em("gpsimd", dmaf(wffn[:].rearrange("p (a b) -> p a b", a=KD),
                                  wf_bfh.rearrange("(a p) b -> p a b", p=128)),
                   waits=[("te", P.v("te"))], inc="gp", step=16)
                wl_gp = P.v("gp")
                for o in range(KD):
                    for k in range(KD):
                        w0 = ([("sy", xn2_sy), ("gp", wl_gp), ("ve", P.v("ve")),
                               ("se", P.v("se"))]
                              if (o == 0 and k == 0) else
                              ([("se", P.v("se"))] if k == 0 else []))
                        em("tensor", lambda e, o=o, k=k: e.matmul(
                            pB[:, 1024:1536],
                            wffn[:, k * 1024 + o * 128:k * 1024 + (o + 1) * 128],
                            xn_fm[:, k * SUP:(k + 1) * SUP],
                            start=(k == 0), stop=(k == KD - 1)),
                            waits=w0, inc=("te" if k == KD - 1 else None))
                    em("scalar", lambda e, o=o: e.copy(
                        out=h1b[:, o * 512:(o + 1) * 512], in_=pB[:, 1024:1536]),
                        waits=[("te", P.v("te"))], inc="se")
                h1_se = P.v("se")
                em("gpsimd", dmaf(wffn[:].rearrange("p (a b) -> p a b", a=KD),
                                  a1_bfh.rearrange("(a p) b -> p a b", p=128)),
                   waits=[("te", P.v("te"))], inc="gp", step=16)
                wl_gp = P.v("gp")
                for o in range(KD):
                    for k in range(KD):
                        w0 = ([("se", h1_se), ("gp", wl_gp)] if (o == 0 and k == 0)
                              else ([("se", P.v("se"))] if k == 0 else []))
                        em("tensor", lambda e, o=o, k=k: e.matmul(
                            pB[:, 1024:1536],
                            wffn[:, k * 1024 + o * 128:k * 1024 + (o + 1) * 128],
                            h1b[:, k * 512:(k + 1) * 512],
                            start=(k == 0), stop=(k == KD - 1)),
                            waits=w0, inc=("te" if k == KD - 1 else None))
                    em("scalar", lambda e, o=o: e.copy(
                        out=h2b[:, o * 512:(o + 1) * 512], in_=pB[:, 1024:1536]),
                        waits=[("te", P.v("te"))], inc="se")
                h2_se = P.v("se")
                ph2_xnfm_free[0] = P.v("te")
                em("gpsimd", dmaf(wffn[:].rearrange("p (a b) -> p a b", a=KD),
                                  a2_bfh.rearrange("(a p) b -> p a b", p=128)),
                   waits=[("te", P.v("te"))], inc="gp", step=16)
                a2_gp = P.v("gp")

                def emit_stash(ch2):
                    if ch2 >= nch or ch2 in stash_sc:
                        return
                    sl2 = ch2 % 3
                    em("scalar", lambda e, ch2=ch2, sl2=sl2: e.dma_start(
                        out=yprd[:, sl2 * DI:(sl2 + 1) * DI],
                        in_=st_ypre[ch2 * 128:(ch2 + 1) * 128, :]),
                        waits=[("ve", yprd_free[sl2])], inc="sc", step=16)
                    em("scalar", lambda e, ch2=ch2, sl2=sl2: e.dma_start(
                        out=szrd[:, sl2 * DI:(sl2 + 1) * DI],
                        in_=st_sz[ch2 * 128:(ch2 + 1) * 128, :]),
                        waits=[("ve", szrd_free[sl2])], inc="sc", step=16)
                    em("scalar", lambda e, ch2=ch2, sl2=sl2: e.dma_start(
                        out=crd[:, sl2 * 128:(sl2 + 1) * 128], in_=st_c[ch2]),
                        waits=[("te", crd_free[sl2])], inc="sc", step=16)
                    em("scalar", lambda e, ch2=ch2, sl2=sl2: e.dma_start(
                        out=grd[:, sl2 * 32:(sl2 + 1) * 32], in_=st_g[ch2]),
                        waits=[("ve", grd_free[sl2])], inc="sc", step=16)
                    stash_sc[ch2] = P.v("sc")

                if s == 0:
                    emit_stash(0)
                    emit_stash(1)

                def emit_A(c):
                    ch = s * CPS + c
                    sl = ch % 2
                    sl3 = ch % 3
                    for q in range(4):
                        w0 = ([("ve", R_ve), ("sc", stash_sc[ch]),
                               ("ve", pa_free[0])] if q == 0 else [])
                        em("tensor", lambda e, sl3=sl3, q=q: e.matmul(
                            pA[:, q * 512:(q + 1) * 512],
                            crd[:, sl3 * 128:(sl3 + 1) * 128],
                            R_bf[:, q * 512:(q + 1) * 512], start=True, stop=True),
                            waits=w0, inc=("te" if q == 3 else None))
                    dy_te = P.v("te")
                    crd_free[sl3] = dy_te
                    em("vector", lambda e, sl3=sl3: e.tensor_tensor(
                        out=hq(gr[:]), in0=hq(pA[:]),
                        in1=r3(grd, sl3 * 32, 128, [(1, NH), (0, HP)]), op=ALU.mult),
                        waits=[("te", dy_te)] + ([("gp", P.v("gp"))] if debug else []),
                        inc="ve")
                    gate_ve = P.v("ve")
                    pa_free[0] = gate_ve
                    grd_free[sl3] = gate_ve
                    em("vector", lambda e, sl3=sl3: e.tensor_tensor(
                        out=y2[:], in0=gr[:], in1=yprd[:, sl3 * DI:(sl3 + 1) * DI],
                        op=ALU.add), inc="ve")
                    yprd_free[sl3] = P.v("ve")
                    em("vector", lambda e, sl3=sl3: e.tensor_tensor(
                        out=gr[:], in0=y2[:], in1=szrd[:, sl3 * DI:(sl3 + 1) * DI],
                        op=ALU.mult), inc="ve")
                    yg_ve = P.v("ve")
                    szrd_free[sl3] = yg_ve
                    if debug and s == nsup - 1:
                        em("gpsimd", lambda e, c=c: e.dma_start(
                            out=dbg["d_yg"][c * 128:(c + 1) * 128, :], in_=gr[:]),
                            waits=[("ve", yg_ve)], inc="gp", step=16)
                    for t4 in range(4):
                        em("vector", lambda e, t4=t4: e.bn_stats(
                            out=bnr[:, t4 * 6:(t4 + 1) * 6],
                            in_=gr[:, t4 * 512:(t4 + 1) * 512]),
                            waits=([("se", P.v("se"))] if t4 == 0 else []), inc="ve")
                    em("vector", lambda e: e.bn_aggr(out=bna[:], in_=bnr[:, 0:24]),
                       waits=[("ve", P.v("ve"))], inc="ve")
                    em("vector", lambda e: e.scalar_tensor_tensor(
                        out=col1[:], in0=bna[:, 0:1], scalar=bna[:, 0:1],
                        in1=bna[:, 1:2], op0=ALU.mult, op1=ALU.add), inc="ve")
                    em("scalar", lambda e: e.activation(out=col1[:], in_=col1[:],
                                                        func=AF.Ln, bias=cep[:]),
                       waits=[("ve", P.v("ve"))], inc="se")
                    em("scalar", lambda e: e.activation(out=col2[:], in_=col1[:],
                                                        func=AF.Exp, scale=-0.5), inc="se")
                    em("vector", lambda e: e.tensor_scalar(
                        out=ynb[:, 0:1024], in0=gr[:, 0:1024], scalar1=col2[:],
                        scalar2=None, op0=ALU.mult),
                        waits=[("se", P.v("se")), ("sy", P.v("sy"))] +
                              ([("gp", P.v("gp"))] if debug else []), inc="ve")
                    em("sync", lambda e, sl=sl: e.dma_start_transpose(
                        r3(ynfm, sl * DI, 128, [(128, NXS // 2), (1, 128)]),
                        ynb[:, 0:1024]),
                        waits=[("ve", P.v("ve")), ("te", ynfm_free[sl])],
                        inc="sy", step=16)
                    yn_h1 = P.v("sy")
                    em("vector", lambda e: e.tensor_scalar(
                        out=ynb[:, 1024:2048], in0=gr[:, 1024:2048], scalar1=col2[:],
                        scalar2=None, op0=ALU.mult), inc="ve")
                    if debug and s == nsup - 1:
                        em("gpsimd", lambda e, c=c: e.dma_start(
                            out=dbg["d_ynb"][c * 128:(c + 1) * 128, :], in_=ynb[:]),
                            waits=[("ve", P.v("ve"))], inc="gp", step=16)
                        em("gpsimd", lambda e, c=c: e.dma_start(
                            out=dbg["d_xn32"][c * 128:(c + 1) * 128, :],
                            in_=xn32[:, c * DM:(c + 1) * DM]),
                            inc="gp", step=16)
                    em("sync", lambda e, sl=sl: e.dma_start_transpose(
                        r3(ynfm, sl * DI + 1024, 128, [(128, NXS // 2), (1, 128)]),
                        ynb[:, 1024:2048]),
                        waits=[("ve", P.v("ve"))] +
                              ([("gp", P.v("gp"))] if debug else []),
                        inc="sy", step=16)
                    yn_sy_d[ch] = (yn_h1, P.v("sy"))
                    emit_stash(ch + 2)

                def emit_B(c):
                    ch = s * CPS + c
                    sl = ch % 2
                    mcol = muS[:, c:c + 1]
                    em("vector", lambda e, c=c, mcol=mcol: e.scalar_tensor_tensor(
                        out=f1_t[:], in0=xn32[:, c * DM:(c + 1) * DM],
                        scalar=sigS[:, c:c + 1],
                        in1=AP(mcol.tensor, mcol.offset,
                               [[mcol.ap[0][0], 128], [0, DM]]),
                        op0=ALU.mult, op1=ALU.add),
                        waits=[("se", f1_free[0])], inc="ve")
                    xrec_ve = P.v("ve")
                    for q in range(2):
                        for k in range(16):
                            if q == 0 and k == 0:
                                w0 = [("sy", yn_sy_d[ch][0]),
                                      ("ve", pb_lo_free[0])]
                            elif q == 0 and k == 8:
                                w0 = [("sy", yn_sy_d[ch][1])]
                            else:
                                w0 = []
                            em("tensor", lambda e, sl=sl, q=q, k=k: e.matmul(
                                pB[:, q * 512:(q + 1) * 512],
                                ynfm[:, sl * DI + k * 128:sl * DI + (k + 1) * 128],
                                woutb[:, k * 1024 + q * 512:k * 1024 + (q + 1) * 512],
                                start=(k == 0), stop=(k == 15)),
                                waits=w0,
                                inc=("te" if (k == 15 and q == 1) else None))
                    wo_te = P.v("te")
                    ynfm_free[sl] = wo_te
                    em("vector", lambda e, c=c: e.tensor_tensor(
                        out=h_t[:], in0=xn32[:, c * DM:(c + 1) * DM],
                        in1=pB[:, 0:1024], op=ALU.add),
                        waits=[("te", wo_te)], inc="ve")
                    pb_lo_free[0] = P.v("ve")
                    if debug and s == nsup - 1:
                        em("gpsimd", lambda e, c=c: e.dma_start(
                            out=dbg["d_h"][c * 128:(c + 1) * 128, :], in_=h_t[:]),
                            waits=[("ve", P.v("ve"))], inc="gp", step=16)
                    em("vector", lambda e: e.bn_stats(out=bnrB[:, 0:6],
                                                      in_=h_t[:, 0:512]),
                       waits=[("se", P.v("se"))], inc="ve")
                    em("vector", lambda e: e.bn_stats(out=bnrB[:, 6:12],
                                                      in_=h_t[:, 512:1024]),
                       inc="ve")
                    em("vector", lambda e: e.bn_aggr(out=bnaB[:], in_=bnrB[:, 0:12]),
                       waits=[("ve", P.v("ve"))], inc="ve")
                    em("scalar", lambda e: e.activation(out=col1B[:], in_=bnaB[:, 1:2],
                                                        func=AF.Ln, bias=cep[:]),
                       waits=[("ve", P.v("ve"))], inc="se")
                    em("scalar", lambda e: e.activation(out=col2B[:], in_=col1B[:],
                                                        func=AF.Exp, scale=-0.5),
                       inc="se")
                    em("vector", lambda e: e.scalar_tensor_tensor(
                        out=u1_t[:], in0=h_t[:], scalar=bnaB[:, 0:1],
                        in1=bcast_col(col2B, DM), op0=ALU.subtract, op1=ALU.mult),
                        waits=[("se", P.v("se"))] + ([("gp", P.v("gp"))] if debug else []),
                        inc="ve")
                    em("vector", lambda e: e.tensor_tensor(
                        out=h_t[:], in0=u1_t[:], in1=lngT[:], op=ALU.mult),
                        waits=([("gp", P.v("gp"))] if debug else []), inc="ve")
                    em("gpsimd", lambda e: e.tensor_tensor(
                        out=u1_t[:], in0=h_t[:], in1=lnbT[:], op=ALU.add),
                        waits=[("ve", P.v("ve"))], inc="po")
                    em("gpsimd", lambda e: e.tensor_tensor(
                        out=h_t[:], in0=u1_t[:], in1=f1_t[:],
                        op=ALU.add),
                        waits=[("ve", xrec_ve)], inc="po")
                    u1_po = P.v("po")
                    for q in range(2):
                        for k in range(KD):
                            w0 = ([("se", h2_se), ("gp", a2_gp),
                                   ("ve", pb_hi_free[0])]
                                  if (q == 0 and k == 0) else [])
                            em("tensor", lambda e, c=c, q=q, k=k: e.matmul(
                                pB[:, 1024 + q * 512:1024 + (q + 1) * 512],
                                h2b[:, k * 512 + c * 128:k * 512 + (c + 1) * 128],
                                wffn[:, k * 1024 + q * 512:k * 1024 + (q + 1) * 512],
                                start=(k == 0), stop=(k == KD - 1)),
                                waits=w0,
                                inc=("te" if (k == KD - 1 and q == 1) else None))
                    f3_te = P.v("te")
                    em("vector", lambda e: e.tensor_tensor(
                        out=f1_t[:], in0=pB[:, 1024:2048], in1=fbT[:], op=ALU.add),
                        waits=[("te", f3_te), ("po", u1_po)], inc="ve")
                    pb_hi_free[0] = P.v("ve")
                    em("scalar", lambda e: e.activation(out=u1_t[:], in_=f1_t[:],
                                                        func=AF.Tanh),
                       waits=[("ve", P.v("ve")), ("po", u1_po)] +
                             ([("gp", P.v("gp"))] if debug else []), inc="se")
                    f1_free[0] = P.v("se")
                    if debug and s == nsup - 1:
                        em("gpsimd", lambda e, c=c: e.dma_start(
                            out=dbg["d_f1"][c * 128:(c + 1) * 128, :], in_=u1_t[:]),
                            waits=[("ve", P.v("ve"))], inc="gp", step=16)
                    em("vector", lambda e, sl=sl: e.tensor_tensor(
                        out=outst[:, sl * DM:(sl + 1) * DM], in0=h_t[:], in1=u1_t[:],
                        op=ALU.add),
                        waits=[("se", P.v("se")), ("po", u1_po), ("gp", P.v("gp"))],
                        inc="ve")
                    em("gpsimd", lambda e, ch=ch, sl=sl: e.dma_start(
                        out=out_seg[ch * 128:(ch + 1) * 128, :],
                        in_=outst[:, sl * DM:(sl + 1) * DM]),
                        waits=[("ve", P.v("ve"))], inc="gp", step=16)

                for c in range(CPS):
                    emit_A(c)
                    if c > 0:
                        emit_B(c - 1)
                        if s + 1 < nsup:
                            emit_ln2_chunk(s + 1, c - 1)
                emit_B(CPS - 1)
                if s + 1 < nsup:
                    emit_ln2_chunk(s + 1, CPS - 1)

            P.emit("gpsimd", lambda e: e.nop(), waits=[("gp", P.v("gp"))])

            OWN = {"vector": "ve", "scalar": "se", "gpsimd": "po", "sync": "sy"}
            with nc.Block() as block:
                for eng_name in ENGS:
                    def make(eng_name):
                        def body(e):
                            own = OWN.get(eng_name)
                            own_cnt = 0
                            last_wait = 0
                            for (fn, waits, inc, step) in P.items[eng_name]:
                                if own is not None and own_cnt > last_wait:
                                    e.wait_ge(sems[own], own_cnt)
                                    last_wait = own_cnt
                                for (sname, val) in waits:
                                    e.wait_ge(sems[sname], val)
                                r = fn(e)
                                if inc is not None:
                                    r.then_inc(sems[inc], step)
                                    if sname_is_own := (inc == own):
                                        own_cnt += step
                        return body
                    getattr(block, eng_name)(make(eng_name))

    return nc, dbg


def kernel(**inputs):
    return _run(inputs, debug=False)[0]


def _run(inputs, debug=False):
    x = np.asarray(inputs["L0_attn_hidden_states"])
    B, T, _ = x.shape
    seg_len = T // NSEG
    w_in = np.ascontiguousarray(inputs["L0_attn_W_in"])
    conv_w = np.asarray(inputs["L0_attn_conv_w"])
    conv_b = np.asarray(inputs["L0_attn_conv_b"])
    dt_bias = np.asarray(inputs["L0_attn_dt_bias"])
    A_log = np.asarray(inputs["L0_attn_A_log"])
    D = np.asarray(inputs["L0_attn_D"])
    rms_w = np.asarray(inputs["L0_attn_rms_w"])
    w_out = np.ascontiguousarray(inputs["L0_attn_W_out"])
    ln_g = np.asarray(inputs["L0_attn_ln_g"])
    ln_b = np.asarray(inputs["L0_attn_ln_b"])
    wf = np.ascontiguousarray(inputs["L0_ffn_weight"])
    a1 = np.ascontiguousarray(inputs["L0_ffn_aux_weight1"])
    a2 = np.ascontiguousarray(inputs["L0_ffn_aux_weight2"])
    fb = np.asarray(inputs["L0_ffn_bias"])

    conv_wt = np.ascontiguousarray(
        conv_w.reshape(NBC, 128, 4).transpose(1, 0, 2).reshape(128, NBC * 4))
    conv_bt = np.ascontiguousarray(conv_b.reshape(NBC, 128).T)
    dtb_col = np.ascontiguousarray(dt_bias.reshape(32, 1))
    alog_col = np.ascontiguousarray(A_log.reshape(32, 1))
    d_tile = np.ascontiguousarray(np.tile(D.reshape(1, 32), (128, 1)))
    rms_t = np.ascontiguousarray(rms_w.reshape(16, 128).T)
    lng_t = np.ascontiguousarray(np.tile(ln_g.reshape(1, DM), (128, 1)))
    lnb_t = np.ascontiguousarray(np.tile(ln_b.reshape(1, DM), (128, 1)))
    fbias_t = np.ascontiguousarray(np.tile(fb.reshape(1, DM), (128, 1)))
    i_bf = np.eye(128, dtype=BF)
    irep = np.ascontiguousarray(np.tile(np.eye(128, dtype=BF), (1, 4)))

    nc, _ = build_nc(seg_len, debug=debug)

    in_maps = []
    for core in range(NCORE):
        b, s = core // NSEG, core % NSEG
        t0 = s * seg_len
        xs = np.ascontiguousarray(x[b, t0:t0 + seg_len])
        if s == 0:
            xh = np.zeros((16, DM), np.float32)
        else:
            xh = np.ascontiguousarray(x[b, t0 - 16:t0])
        alpha = np.zeros(8, np.float32)
        gam = np.zeros(8, np.float32)
        for u in range(NCORE):
            if u // NSEG == b and (u % NSEG) < s:
                alpha[u] = 1.0
                gam[u] = 1.0
        alpha_t = np.ascontiguousarray(np.tile(alpha.reshape(1, 8), (128, 1)))
        gam_t = np.ascontiguousarray(np.tile(gam.reshape(1, 8), (128, 1)))
        igam_t = np.ascontiguousarray(1.0 - gam_t)
        in_maps.append(dict(
            x_seg=xs, x_halo=xh, w_in=w_in, conv_wt=conv_wt, conv_bt=conv_bt,
            dtb_col=dtb_col, alog_col=alog_col, d_tile=d_tile, rms_t=rms_t,
            w_out=w_out, wf=wf, a1=a1, a2=a2, lng_t=lng_t, lnb_t=lnb_t,
            fbias_t=fbias_t, i_bf=i_bf, irep=irep, alpha_t=alpha_t,
            gam_t=gam_t, igam_t=igam_t,
            eps_c=np.full((128, 1), EPS, np.float32)))

    res = run_bass_kernel_spmd(nc, in_maps, core_ids=list(range(NCORE)))
    out = np.empty((B, T, DM), np.float32)
    for core in range(NCORE):
        b, s = core // NSEG, core % NSEG
        out[b, s * seg_len:(s + 1) * seg_len] = res.results[core]["out_seg"]
    return out, res.results


# version 14
# speedup vs baseline: 1.0528x; 1.0528x over previous
import numpy as np
import ml_dtypes
import contextlib
import concourse.bass as bass
import concourse.mybir as mybir
from concourse.bass_utils import run_bass_kernel_spmd
from concourse.alu_op_type import AluOpType as ALU

F32 = mybir.dt.float32
BF16 = mybir.dt.bfloat16
AF = mybir.ActivationFunctionType
AP = bass.AP

NCORE = 8
NSEG = 4
DM = 1024
DI = 2048
NH = 32
HP = 64
Q = 128
SUP = 512
CPS = SUP // Q
NXS = 16
NBC = 18
KD = 8
EPS = 1e-5
BF = ml_dtypes.bfloat16

ENGS = ("sync", "vector", "scalar", "tensor", "gpsimd")
SEMS = ("sy", "sc", "gp", "te", "ve", "se", "po", "cc", "e0", "e1", "dc")


class Prog:
    def __init__(self):
        self.items = {e: [] for e in ENGS}
        self.cnt = {s: 0 for s in SEMS}

    def emit(self, eng, fn, waits=(), inc=None, step=1):
        if inc is not None:
            self.cnt[inc] += step
        self.items[eng].append((fn, tuple((s, v) for (s, v) in waits if v > 0), inc, step))

    def v(self, name):
        return self.cnt[name]

    def barrier_vals(self):
        return [(s, self.cnt[s]) for s in SEMS if self.cnt[s] > 0]


def _emit_setup(P, nc, env):
    (w_in_bf, wstage, wstage2, cwt, cbt, dtb, alog, acol, dti, rmsw, ibf, irp,
     alct, gamt, igamt, conv_wt, conv_bt, dtb_col, alog_col, d_tile, rms_t,
     i_bf, irep, alpha_t, gam_t, igam_t, w_in, dmaf) = (
        env["w_in_bf"], env["wstage"], env["wstage2"], env["cwt"], env["cbt"],
        env["dtb"], env["alog"], env["acol"], env["dti"], env["rmsw"], env["ibf"],
        env["irp"], env["alct"], env["gamt"], env["igamt"], env["conv_wt"],
        env["conv_bt"], env["dtb_col"], env["alog_col"], env["d_tile"], env["rms_t"],
        env["i_bf"], env["irep"], env["alpha_t"], env["gam_t"], env["igam_t"],
        env["w_in"], env["dmaf"])
    cep, eps_c = env["cep"], env["eps_c"]
    for (dst, src) in ((cwt, conv_wt), (cbt, conv_bt), (dtb, dtb_col),
                       (alog, alog_col), (dti, d_tile), (rmsw, rms_t),
                       (ibf, i_bf), (irp, irep), (alct, alpha_t),
                       (gamt, gam_t), (igamt, igam_t), (cep, eps_c)):
        P.emit("sync", dmaf(dst[:], src), inc="sy", step=16)
    setup_sy = P.v("sy")
    col1 = env["col1"]
    P.emit("scalar", lambda e: e.activation(out=col1[0:32, :], in_=alog[:], func=AF.Exp),
           waits=[("sy", setup_sy)], inc="se")
    P.emit("vector", lambda e: e.tensor_scalar(out=acol[:], in0=col1[0:32, :],
                                               scalar1=-1.0, scalar2=None,
                                               op0=ALU.mult),
           waits=[("se", P.v("se"))], inc="ve")
    QS = (("sync", "sy"), ("scalar", "sc"), ("gpsimd", "gp"))
    stg = (wstage, wstage2)
    conv_done = [0] * KD
    for k in range(KD):
        qeng, qsem = QS[k % 3]
        w = [("ve", conv_done[k - 2])] if k >= 2 else []
        P.emit(qeng, dmaf(stg[k % 2][:], w_in[k * 128:(k + 1) * 128, :]),
               waits=w, inc=qsem, step=16)
        ld = (qsem, P.v(qsem))
        P.emit("vector", lambda e, k=k: e.tensor_scalar(
            out=w_in_bf[:, k * 4384:(k + 1) * 4384], in0=stg[k % 2][:],
            scalar1=1.0, scalar2=None, op0=ALU.mult),
            waits=[ld], inc="ve")
        conv_done[k] = P.v("ve")


def build_nc(seg_len, debug=False):
    assert seg_len % SUP == 0
    nsup = seg_len // SUP
    nch = seg_len // Q

    nc = bass.Bass("TRN2", target_bir_lowering=False)

    x_seg = nc.dram_tensor("x_seg", [seg_len, DM], F32, kind="ExternalInput").ap()
    x_halo = nc.dram_tensor("x_halo", [16, DM], F32, kind="ExternalInput").ap()
    w_in = nc.dram_tensor("w_in", [DM, 4384], F32, kind="ExternalInput").ap()
    conv_wt = nc.dram_tensor("conv_wt", [128, NBC * 4], F32, kind="ExternalInput").ap()
    conv_bt = nc.dram_tensor("conv_bt", [128, NBC], F32, kind="ExternalInput").ap()
    dtb_col = nc.dram_tensor("dtb_col", [32, 1], F32, kind="ExternalInput").ap()
    alog_col = nc.dram_tensor("alog_col", [32, 1], F32, kind="ExternalInput").ap()
    d_tile = nc.dram_tensor("d_tile", [128, 32], F32, kind="ExternalInput").ap()
    rms_t = nc.dram_tensor("rms_t", [128, 16], F32, kind="ExternalInput").ap()
    w_out = nc.dram_tensor("w_out", [DI, DM], F32, kind="ExternalInput").ap()
    wf = nc.dram_tensor("wf", [DM, DM], F32, kind="ExternalInput").ap()
    a1 = nc.dram_tensor("a1", [DM, DM], F32, kind="ExternalInput").ap()
    a2 = nc.dram_tensor("a2", [DM, DM], F32, kind="ExternalInput").ap()
    lng_t = nc.dram_tensor("lng_t", [128, DM], F32, kind="ExternalInput").ap()
    lnb_t = nc.dram_tensor("lnb_t", [128, DM], F32, kind="ExternalInput").ap()
    fbias_t = nc.dram_tensor("fbias_t", [128, DM], F32, kind="ExternalInput").ap()
    i_bf = nc.dram_tensor("i_bf", [128, 128], BF16, kind="ExternalInput").ap()
    irep = nc.dram_tensor("irep", [128, 512], BF16, kind="ExternalInput").ap()
    alpha_t = nc.dram_tensor("alpha_t", [128, 8], F32, kind="ExternalInput").ap()
    gam_t = nc.dram_tensor("gam_t", [128, 8], F32, kind="ExternalInput").ap()
    igam_t = nc.dram_tensor("igam_t", [128, 8], F32, kind="ExternalInput").ap()
    eps_c = nc.dram_tensor("eps_c", [128, 1], F32, kind="ExternalInput").ap()

    out_seg = nc.dram_tensor("out_seg", [seg_len, DM], F32, kind="ExternalOutput").ap()

    st_ypre = nc.dram_tensor("st_ypre", [seg_len, DI], BF16).ap()
    st_sz = nc.dram_tensor("st_sz", [seg_len, DI], BF16).ap()
    st_c = nc.dram_tensor("st_c", [nch, 128, 128], BF16).ap()
    st_g = nc.dram_tensor("st_g", [nch, 128, 32], BF16).ap()
    expa_scr = nc.dram_tensor("expa_scr", [seg_len // Q, 32 * Q], BF16).ap()
    cd_scr = nc.dram_tensor("cd_scr", [nch, 32], F32).ap()
    wout_bfh = nc.dram_tensor("wout_bfh", [DI, DM], BF16).ap()
    wf_bfh = nc.dram_tensor("wf_bfh", [DM, DM], BF16).ap()
    a1_bfh = nc.dram_tensor("a1_bfh", [DM, DM], BF16).ap()
    a2_bfh = nc.dram_tensor("a2_bfh", [DM, DM], BF16).ap()
    exch_in = nc.dram_tensor("exch_in", [129, DI], F32).ap()
    gath = nc.dram_tensor("gath", [NCORE * 129, DI], F32).ap()

    dbg = {}
    if debug:
        dbg["d_ypre"] = nc.dram_tensor("d_ypre", [seg_len, DI], F32, kind="ExternalOutput").ap()
        dbg["d_sz"] = nc.dram_tensor("d_sz", [seg_len, DI], F32, kind="ExternalOutput").ap()
        dbg["d_r"] = nc.dram_tensor("d_r", [129, DI], F32, kind="ExternalOutput").ap()
        dbg["d_R"] = nc.dram_tensor("d_R", [128, DI], F32, kind="ExternalOutput").ap()
        dbg["d_dt"] = nc.dram_tensor("d_dt", [32, SUP], F32, kind="ExternalOutput").ap()
        dbg["d_yg"] = nc.dram_tensor("d_yg", [SUP, DI], F32, kind="ExternalOutput").ap()
        dbg["d_h"] = nc.dram_tensor("d_h", [SUP, DM], F32, kind="ExternalOutput").ap()
        dbg["d_f1"] = nc.dram_tensor("d_f1", [SUP, DM], F32, kind="ExternalOutput").ap()
        dbg["d_ynb"] = nc.dram_tensor("d_ynb", [SUP, DI], BF16, kind="ExternalOutput").ap()
        dbg["d_xn32"] = nc.dram_tensor("d_xn32", [SUP, DM], F32, kind="ExternalOutput").ap()

    P = Prog()

    with contextlib.ExitStack() as ST:
        def sb(name, p, f, dt):
            return ST.enter_context(nc.sbuf_tensor(name, [p, f], dt))

        x_t = sb("x_t", 128, DM, F32)
        bnr = sb("bnr", 128, 24, F32)
        bna = sb("bna", 128, 2, F32)
        col1 = sb("col1", 128, 1, F32)
        xn_tmb = sb("xn_tmb", 128, DM, BF16)
        xn_fm = sb("xn_fm", 128, KD * SUP, BF16)
        dtb = sb("dtb", 32, 1, F32)
        alog = sb("alog", 32, 1, F32)
        acol = sb("acol", 32, 1, F32)
        dti = sb("dti", 128, 32, F32)
        rmsw = sb("rmsw", 128, 16, F32)
        ibf = sb("ibf", 128, 128, BF16)
        irp = sb("irp", 128, 512, BF16)
        alct = sb("alct", 128, 8, F32)
        gamt = sb("gamt", 128, 8, F32)
        igamt = sb("igamt", 128, 8, F32)
        cwt = sb("cwt", 128, NBC * 4, F32)
        cbt = sb("cbt", 128, NBC, F32)
        sig = sb("sig", 32, 1, F32)
        sig2 = sb("sig2", 32, 1, F32)
        cep = sb("cep", 128, 1, F32)
        col2 = sb("col2", 128, 1, F32)

        pA = ST.enter_context(nc.psum_tensor("pA", [128, DI], F32))
        pB = ST.enter_context(nc.psum_tensor("pB", [128, DI], F32))

        sems = {}
        for s in SEMS:
            sems[s] = ST.enter_context(nc.semaphore("s_" + s))

        def bcast_col(t, n_inner):
            b = t[:]
            return AP(b.tensor, b.offset, [[b.ap[0][0], b.ap[0][1]], [0, n_inner]])

        def hb(t, off, h_cnt, p_cnt, pdim=128):
            b = t[:]
            return AP(b.tensor, b.offset + off, [[b.ap[0][0], pdim], [1, h_cnt], [0, p_cnt]])

        def r3(t, off, pdim, dims):
            b = t[:]
            return AP(b.tensor, b.offset + off, [[b.ap[0][0], pdim]] + [list(d) for d in dims])

        def hq(ap_):
            return ap_.rearrange("p (h q) -> p h q", h=NH)

        def dmaf(out, in_, **kw):
            return lambda e: e.dma_start(out=out, in_=in_, **kw)

        with contextlib.ExitStack() as S1:
            def sb1(name, p, f, dt):
                return ST.enter_context(nc.sbuf_tensor(name, [p, f], dt))

            def sb1p(name, p, f, dt):
                return S1.enter_context(nc.sbuf_tensor(name, [p, f], dt))

            w_in_bf = sb1p("w_in_bf", 128, KD * 4384, BF16)
            wcst = sb1p("wcst", 128, 512, F32)
            wcbf = sb1p("wcbf", 128, 512, BF16)
            S0 = contextlib.ExitStack()
            wstage = S0.enter_context(nc.sbuf_tensor("wstage", [128, 4384], F32))
            wstage2 = S0.enter_context(nc.sbuf_tensor("wstage2", [128, 4384], F32))
            P.emit("gpsimd", dmaf(x_t[0:16, 0:DM], x_halo), inc="gp", step=16)
            halo_gp = P.v("gp")
            _emit_setup(P, nc, locals())
            S0.close()

            wrounds = []
            for k in range(16):
                for h in range(2):
                    wrounds.append((w_out, wout_bfh, k, h, True))
            for (wsrc, wdst) in ((wf, wf_bfh), (a1, a1_bfh), (a2, a2_bfh)):
                for k in range(KD):
                    for h in range(2):
                        wrounds.append((wsrc, wdst, k, h, False))
            wr_n = len(wrounds)
            wr_ld = [0]
            wr_cv = [0]
            wr_load_sc = {}
            wr_conv_ve = {}
            wr_store_gp = {}

            def emit_wround_load():
                i = wr_ld[0]
                if i >= wr_n:
                    return
                wr_ld[0] = i + 1
                src, dst, k, h, scl = wrounds[i]
                w = [("ve", wr_conv_ve[i - 1])] if i >= 1 else []
                P.emit("scalar", dmaf(
                    wcst[:], src[k * 128:(k + 1) * 128, h * 512:(h + 1) * 512]),
                    waits=w, inc="sc", step=16)
                wr_load_sc[i] = P.v("sc")

            def emit_wround_conv():
                i = wr_cv[0]
                if i >= wr_n or i >= wr_ld[0]:
                    return
                wr_cv[0] = i + 1
                src, dst, k, h, scl = wrounds[i]
                w = [("sc", wr_load_sc[i])]
                if i >= 1:
                    w.append(("gp", wr_store_gp[i - 1]))
                P.emit("vector", lambda e, k=k, scl=scl: e.tensor_scalar(
                    out=wcbf[:], in0=wcst[:],
                    scalar1=(rmsw[:, k:k + 1] if scl else 1.0),
                    scalar2=None, op0=ALU.mult),
                    waits=w, inc="ve")
                wr_conv_ve[i] = P.v("ve")
                P.emit("gpsimd", dmaf(
                    dst[k * 128:(k + 1) * 128, h * 512:(h + 1) * 512], wcbf[:]),
                    waits=[("ve", wr_conv_ve[i])], inc="gp", step=16)
                wr_store_gp[i] = P.v("gp")
            cbuf = sb1p("cbuf", 128, 2 * 520, F32)
            tail = sb1p("tail", 128, NBC * 4, F32)
            ca_v = sb1p("ca_v", 128, 2 * 512, F32)
            ca_p = sb1p("ca_p", 128, 3 * 512, F32)
            xbc_bf = sb1p("xbc_bf", 128, 4 * 512, BF16)
            t_dt = sb1p("t_dt", 32, SUP, F32)
            t_cs = t_dt
            t_u1 = sb1p("t_u1", 32, SUP, F32)
            t_u2 = sb1p("t_u2", 32, SUP, F32)
            t_A = t_u2
            w_bf = sb1p("w_bf", 32, SUP, BF16)
            t_g = sb1p("t_g", 32, SUP, F32)
            t_gsb = sb1p("t_gsb", 32, SUP, BF16)
            t_gcb = sb1p("t_gcb", 32, SUP, BF16)
            t_dsb = sb1p("t_dsb", 32, SUP, BF16)
            t_dtb = sb1p("t_dtb", 32, SUP, BF16)
            cd4 = sb1p("cd4", 32, CPS, F32)
            xtm = sb1p("xtm", 128, CPS * DI, BF16)
            btm = sb1p("btm", 128, CPS * 128, BF16)
            dttm = sb1p("dttm", 128, CPS * 32, BF16)
            gctm = sb1p("gctm", 128, CPS * 32, BF16)
            gstm = sb1p("gstm", 128, CPS * 32, BF16)
            dstm = sb1p("dstm", 128, CPS * 32, BF16)
            eab = sb1p("eab", 128, 4 * 1024, BF16)
            t_eab_b = sb1p("t_eab_b", 32, SUP, BF16)
            e_t = sb1p("e_t", 128, NH * Q, BF16)
            m_t = sb1p("m_t", 128, DI, BF16)
            g_sb = sb1p("g_sb", 128, 128, BF16)
            xt_t = sb1p("xt_t", 128, DI, BF16)
            xd_t = sb1p("xd_t", 128, DI, BF16)
            xtd = sb1p("xtd", 128, DI, BF16)
            r_bf = sb1p("r_bf", 128, DI, BF16)
            r_t = sb1p("r_t", 128, DI, F32)
            dcb = sb1p("dcb", 128, CPS * 32, F32)
            ypre = sb1p("ypre", 128, DI, F32)
            sz_t = sb1p("sz_t", 128, DI, F32)
            yp_b = sb1p("yp_b", 128, DI, BF16)

            P.emit("vector", lambda e: e.memset(r_t[:], 0.0), inc="ve")
            P.emit("vector", lambda e: e.memset(sig[:], 1.0), inc="ve")
            P.emit("vector", lambda e: e.memset(tail[:], 0.0), inc="ve")

            P.emit("vector", lambda e: e.bn_stats(out=bnr[0:16, 0:6], in_=x_t[0:16, 0:512]),
                   waits=[("gp", halo_gp)], inc="ve")
            P.emit("vector", lambda e: e.bn_stats(out=bnr[0:16, 6:12], in_=x_t[0:16, 512:1024]),
                   inc="ve")
            P.emit("vector", lambda e: e.bn_aggr(out=bna[0:16, :], in_=bnr[0:16, 0:12]),
                   waits=[("ve", P.v("ve"))], inc="ve")
            P.emit("scalar", lambda e: e.activation(out=col1[0:16, :], in_=bna[0:16, 1:2],
                                                    func=AF.Ln, bias=cep[0:16, :]),
                   waits=[("ve", P.v("ve"))], inc="se")
            P.emit("scalar", lambda e: e.activation(out=col2[0:16, :], in_=col1[0:16, :],
                                                    func=AF.Exp, scale=-0.5), inc="se")
            P.emit("vector", lambda e: e.scalar_tensor_tensor(
                out=xn_tmb[0:16, :], in0=x_t[0:16, 0:DM], scalar=bna[0:16, 0:1],
                in1=AP(col2[:].tensor, col2[:].offset, [[col2[:].ap[0][0], 16], [0, DM]]),
                op0=ALU.subtract, op1=ALU.mult),
                waits=[("se", P.v("se"))], inc="ve")
            P.emit("sync", lambda e: e.dma_start_transpose(
                r3(xn_fm, 0, 128, [(16, KD), (1, 16)]), xn_tmb[0:16, :]),
                waits=[("ve", P.v("ve"))], inc="sy", step=16)
            hsy = P.v("sy")
            for f in range(NBC):
                for k in range(KD):
                    w0 = ([("sy", hsy), ("ve", P.v("ve")), ("se", P.v("se"))]
                          if k == 0 else [])
                    P.emit("tensor", lambda e, f=f, k=k: e.matmul(
                        pB[:, 0:16],
                        w_in_bf[:, k * 4384 + 2048 + f * 128:k * 4384 + 2048 + (f + 1) * 128],
                        xn_fm[:, k * 16:(k + 1) * 16],
                        start=(k == 0), stop=(k == KD - 1)),
                        waits=w0, inc=("te" if k == KD - 1 else None))
                P.emit("scalar", lambda e, f=f: e.copy(out=tail[:, f * 4:f * 4 + 3],
                                                       in_=pB[:, 13:16]),
                       waits=[("te", P.v("te"))], inc="se")

            ln_sy = {}
            xnfm_free = [P.v("te")]

            def emit_ln_chunk(s2, c2):
                t0 = s2 * SUP + c2 * Q
                P.emit("scalar", dmaf(x_t[:], x_seg[t0:t0 + 128, :]),
                       waits=[("ve", P.v("ve"))], inc="sc", step=16)
                xs_ = P.v("sc")
                P.emit("vector", lambda e: e.bn_stats(
                    out=bnr[:, 0:6], in_=x_t[:, 0:512]),
                    waits=[("sc", xs_), ("se", P.v("se"))], inc="ve")
                P.emit("vector", lambda e: e.bn_stats(
                    out=bnr[:, 6:12], in_=x_t[:, 512:1024]),
                    inc="ve")
                P.emit("vector", lambda e: e.bn_aggr(out=bna[:], in_=bnr[:, 0:12]),
                       waits=[("ve", P.v("ve"))], inc="ve")
                P.emit("scalar", lambda e: e.activation(out=col1[:], in_=bna[:, 1:2],
                                                        func=AF.Ln, bias=cep[:]),
                       waits=[("ve", P.v("ve"))], inc="se")
                P.emit("scalar", lambda e: e.activation(out=col2[:], in_=col1[:],
                                                        func=AF.Exp, scale=-0.5),
                       inc="se")
                P.emit("vector", lambda e: e.scalar_tensor_tensor(
                    out=xn_tmb[:], in0=x_t[:],
                    scalar=bna[:, 0:1], in1=bcast_col(col2, DM),
                    op0=ALU.subtract, op1=ALU.mult),
                    waits=[("se", P.v("se")), ("sy", P.v("sy"))], inc="ve")
                P.emit("sync", lambda e, c2=c2: e.dma_start_transpose(
                    r3(xn_fm, c2 * 128, 128, [(SUP, KD), (1, 128)]), xn_tmb[:]),
                    waits=[("ve", P.v("ve")), ("te", xnfm_free[0])],
                    inc="sy", step=16)
                ln_sy[s2] = P.v("sy")

            for s in range(nsup):
                sup_off = s * SUP
                if s == 0:
                    for c in range(CPS):
                        emit_ln_chunk(0, c)
                xn_sy = ln_sy[s]

                tsy = {}
                copy_se = {}
                conv_po = {}
                for f in range(NBC):
                    cslot = f % 2
                    half = f % 2
                    xoff = (f % 2) * 512 if f < 16 else (1024 if f == 16 else 1536)
                    for k in range(KD):
                        if k != 0:
                            w0 = []
                        elif f < 2:
                            w0 = [("sy", xn_sy), ("se", P.v("se")),
                                  ("ve", P.v("ve"))]
                        else:
                            w0 = [("se", copy_se[f - 2])]
                        P.emit("tensor", lambda e, f=f, k=k, half=half: e.matmul(
                            pB[:, half * 512:(half + 1) * 512],
                            w_in_bf[:, k * 4384 + 2048 + f * 128:k * 4384 + 2048 + (f + 1) * 128],
                            xn_fm[:, k * SUP:(k + 1) * SUP],
                            start=(k == 0), stop=(k == KD - 1)),
                            waits=w0, inc=("te" if k == KD - 1 else None))
                    te_mm = P.v("te")
                    P.emit("vector", lambda e, cslot=cslot, f=f: e.tensor_copy(
                        cbuf[:, cslot * 520:cslot * 520 + 3], tail[:, f * 4:f * 4 + 3]),
                        waits=[("po", P.v("po")), ("se", P.v("se"))], inc="ve")
                    cp_w = [("te", te_mm), ("ve", P.v("ve"))]
                    if f >= 2 and (f % 2) == 1:
                        cp_w.append(("po", conv_po[f - 2]))
                    P.emit("scalar", lambda e, cslot=cslot, half=half: e.copy(
                        out=cbuf[:, cslot * 520 + 3:cslot * 520 + 515],
                        in_=pB[:, half * 512:(half + 1) * 512]),
                        waits=cp_w, inc="se")
                    ev_se = P.v("se")
                    copy_se[f] = ev_se
                    P.emit("vector", lambda e, cslot=cslot, f=f: e.tensor_copy(
                        tail[:, f * 4:f * 4 + 3],
                        cbuf[:, cslot * 520 + 512:cslot * 520 + 515]),
                        waits=[("se", ev_se)], inc="ve")
                    ceng = "vector" if f % 2 == 0 else "gpsimd"
                    cinc = "ve" if ceng == "vector" else "po"
                    cab = ca_v if f % 2 == 0 else ca_p
                    off = cslot * 520
                    if ceng == "vector":
                        P.emit(ceng, lambda e, off=off, f=f, cab=cab: e.tensor_scalar(
                            out=cab[:, 0:512], in0=cbuf[:, off:off + 512],
                            scalar1=cwt[:, f * 4:f * 4 + 1], scalar2=None, op0=ALU.mult),
                            waits=[("se", P.v("se"))], inc=cinc)
                        for j in (1, 2, 3):
                            src2 = 0 if j % 2 == 1 else 512
                            dst2 = 512 - src2
                            P.emit(ceng, lambda e, off=off, f=f, j=j, src2=src2, dst2=dst2,
                                   cab=cab: e.scalar_tensor_tensor(
                                       out=cab[:, dst2:dst2 + 512],
                                       in0=cbuf[:, off + j:off + j + 512],
                                       scalar=cwt[:, f * 4 + j:f * 4 + j + 1],
                                       in1=cab[:, src2:src2 + 512],
                                       op0=ALU.mult, op1=ALU.add), inc=cinc)
                    else:
                        def wcol(f, j):
                            b = cwt[:]
                            return AP(b.tensor, b.offset + f * 4 + j,
                                      [[b.ap[0][0], 128], [0, 512]])
                        P.emit(ceng, lambda e, off=off, f=f, cab=cab: e.tensor_tensor(
                            out=cab[:, 0:512], in0=cbuf[:, off:off + 512],
                            in1=wcol(f, 0), op=ALU.mult),
                            waits=[("se", P.v("se"))], inc=cinc)
                        plan = [(1, 512, 0, 512, 1024), (2, 0, 1024, 0, 512),
                                (3, 0, 512, 0, 1024)]
                        for (j, mdst, a0, a1_, adst) in plan:
                            P.emit(ceng, lambda e, off=off, f=f, j=j, mdst=mdst, cab=cab:
                                   e.tensor_tensor(
                                       out=cab[:, mdst:mdst + 512],
                                       in0=cbuf[:, off + j:off + j + 512],
                                       in1=wcol(f, j), op=ALU.mult), inc=cinc)
                            P.emit(ceng, lambda e, cab=cab, a0=a0, a1_=a1_, adst=adst:
                                   e.tensor_tensor(
                                       out=cab[:, adst:adst + 512], in0=cab[:, a0:a0 + 512],
                                       in1=cab[:, a1_:a1_ + 512], op=ALU.add), inc=cinc)
                    conv_po[f] = P.v("po")
                    sil_src = (512, 1024) if ceng == "vector" else (1024, 1536)
                    silw = [(cinc, P.v(cinc))]
                    if f >= 2 and f < 16:
                        silw.append(("sy", tsy[f - 2]))
                    if f >= 16:
                        silw.append(("te", P.v("te")))
                        silw.append(("gp", P.v("gp")))
                    P.emit("scalar", lambda e, f=f, cab=cab, sil_src=sil_src, xoff=xoff:
                           e.activation(
                               out=xbc_bf[:, xoff:xoff + 512],
                               in_=cab[:, sil_src[0]:sil_src[1]],
                               func=AF.Silu, bias=cbt[:, f:f + 1]),
                           waits=silw, inc="se")
                    sil_se = P.v("se")
                    if f < 16:
                        P.emit("sync", lambda e, f=f, xoff=xoff: e.dma_start_transpose(
                            r3(xtm, f * 128, 128, [(DI, CPS), (1, 128)]),
                            xbc_bf[:, xoff:xoff + 512]),
                            waits=[("se", sil_se)], inc="sy", step=16)
                        tsy[f] = P.v("sy")
                    elif f == 16:
                        P.emit("sync", lambda e: e.dma_start_transpose(
                            r3(btm, 0, 128, [(128, CPS), (1, 128)]),
                            xbc_bf[:, 1024:1536]),
                            waits=[("se", sil_se)], inc="sy", step=16)
                    else:
                        P.emit("gpsimd", lambda e, s=s: e.dma_start(
                            out=AP(st_c.tensor, st_c.offset + s * CPS * 128 * 128,
                                   [[128, 128], [128 * 128, CPS], [1, 128]]),
                            in_=r3(xbc_bf, 1536, 128, [(128, CPS), (1, 128)])),
                            waits=[("se", sil_se)], inc="gp", step=16)
                tm_sy = P.v("sy")
                conv_se = P.v("se")

                for c in range(CPS):
                    ch = s * CPS + c
                    for q in range(4):
                        for k in range(KD):
                            w0 = ([("ve", P.v("ve")), ("se", P.v("se"))]
                                  if (q == 0 and k == 0) else [])
                            P.emit("tensor", lambda e, q=q, k=k, c=c: e.matmul(
                                pA[:, q * 512:(q + 1) * 512],
                                xn_fm[:, k * SUP + c * 128:k * SUP + (c + 1) * 128],
                                w_in_bf[:, k * 4384 + q * 512:k * 4384 + (q + 1) * 512],
                                start=(k == 0), stop=(k == KD - 1)),
                                waits=w0,
                                inc=("te" if (k == KD - 1 and q == 3) else None))
                    z_te = P.v("te")
                    for q in range(4):
                        P.emit("scalar", lambda e, q=q: e.activation(
                            out=yp_b[:, q * 512:(q + 1) * 512],
                            in_=pA[:, q * 512:(q + 1) * 512], func=AF.Silu),
                            waits=([("te", z_te), ("gp", P.v("gp"))] if q == 0 else []),
                            inc="se")
                    z_se = P.v("se")
                    P.emit("gpsimd", lambda e, ch=ch: e.dma_start(
                        out=st_sz[ch * 128:(ch + 1) * 128, :], in_=yp_b[:]),
                        waits=[("se", z_se)], inc="gp", step=16)
                zphase_se = P.v("se")

                for k in range(KD):
                    w0 = [("se", conv_se), ("ve", P.v("ve"))] if k == 0 else []
                    P.emit("tensor", lambda e, k=k: e.matmul(
                        pB[0:32, 1024:1536],
                        w_in_bf[:, k * 4384 + 4352:k * 4384 + 4384],
                        xn_fm[:, k * SUP:(k + 1) * SUP],
                        start=(k == 0), stop=(k == KD - 1)),
                        waits=w0, inc=("te" if k == KD - 1 else None))
                dt_te = P.v("te")
                xnfm_free[0] = dt_te
                P.emit("vector", lambda e: e.tensor_scalar(
                    out=t_u1[:], in0=pB[0:32, 1024:1536], scalar1=dtb[:],
                    scalar2=0.0, op0=ALU.add, op1=ALU.max),
                    waits=[("te", dt_te)], inc="ve")
                P.emit("vector", lambda e: e.tensor_scalar(
                    out=t_u2[:], in0=pB[0:32, 1024:1536], scalar1=dtb[:],
                    scalar2=None, op0=ALU.add), inc="ve")
                P.emit("vector", lambda e: e.scalar_tensor_tensor(
                    out=t_g[:], in0=t_u1[:], scalar=-2.0, in1=t_u2[:],
                    op0=ALU.mult, op1=ALU.add), inc="ve")
                P.emit("scalar", lambda e: e.activation(out=t_u2[:], in_=t_g[:],
                                                        func=AF.Exp),
                       waits=[("ve", P.v("ve"))], inc="se")
                P.emit("scalar", lambda e: e.activation(out=t_g[:], in_=t_u2[:],
                                                        func=AF.Ln, bias=1.0), inc="se")
                P.emit("vector", lambda e: e.tensor_tensor(out=t_dt[:], in0=t_u1[:],
                                                           in1=t_g[:], op=ALU.add),
                       waits=[("se", P.v("se"))], inc="ve")
                P.emit("vector", lambda e: e.tensor_scalar(
                    out=t_A[:], in0=t_dt[:], scalar1=acol[:], scalar2=None,
                    op0=ALU.mult), inc="ve")
                P.emit("vector", lambda e: e.tensor_scalar(
                    out=t_dtb[:], in0=t_dt[:], scalar1=1.0, scalar2=None, op0=ALU.mult),
                    inc="ve")
                for c in range(CPS):
                    P.emit("vector", lambda e, c=c: e.tensor_tensor_scan(
                        out=t_cs[:, c * 128:(c + 1) * 128],
                        data0=t_A[:, c * 128:(c + 1) * 128],
                        data1=t_A[:, c * 128:(c + 1) * 128], initial=0.0,
                        op0=ALU.add, op1=ALU.bypass), inc="ve")
                cs_ve = P.v("ve")
                P.emit("scalar", lambda e: e.activation(out=t_eab_b[:], in_=t_A[:],
                                                        func=AF.Exp),
                       waits=[("ve", cs_ve), ("gp", P.v("gp"))], inc="se")
                P.emit("vector", lambda e: e.memset(
                    AP(t_eab_b[:].tensor, t_eab_b[:].offset,
                       [[t_eab_b[:].ap[0][0], 32], [128, CPS]]),
                    0.0),
                    waits=[("se", P.v("se"))], inc="ve")
                P.emit("gpsimd", lambda e, s=s: e.dma_start(
                    out=AP(expa_scr.tensor, expa_scr.offset + s * CPS * 32 * Q,
                           [[Q, 32], [32 * Q, CPS], [1, Q]]),
                    in_=t_eab_b[:]),
                    waits=[("ve", P.v("ve"))], inc="gp", step=16)
                expa_gp_s = P.v("gp")
                P.emit("scalar", lambda e: e.activation(out=t_g[:], in_=t_cs[:], func=AF.Exp),
                       waits=[("ve", cs_ve)], inc="se")
                g_se = P.v("se")
                P.emit("vector", lambda e: e.tensor_scalar(
                    out=t_gcb[:], in0=t_g[:], scalar1=1.0, scalar2=None, op0=ALU.mult),
                    waits=[("se", g_se), ("sy", P.v("sy"))], inc="ve")
                for c in range(CPS):
                    ch0 = s * CPS + c
                    scur = sig if ch0 % 2 == 0 else sig2
                    snxt = sig2 if ch0 % 2 == 0 else sig
                    P.emit("vector", lambda e, c=c, scur=scur: e.tensor_scalar(
                        out=t_gsb[:, c * 128:(c + 1) * 128],
                        in0=t_g[:, c * 128:(c + 1) * 128],
                        scalar1=scur[:], scalar2=None, op0=ALU.mult), inc="ve")
                    P.emit("scalar", lambda e, c=c: e.activation(
                        out=cd4[:, c:c + 1], in_=t_cs[:, c * 128 + 127:c * 128 + 128],
                        func=AF.Exp),
                        waits=[("ve", P.v("ve")), ("gp", P.v("gp"))], inc="se")
                    P.emit("vector", lambda e, c=c, scur=scur, snxt=snxt: e.tensor_scalar(
                        out=snxt[:], in0=scur[:], scalar1=cd4[:, c:c + 1], scalar2=None,
                        op0=ALU.mult),
                        waits=[("se", P.v("se"))], inc="ve")
                P.emit("vector", lambda e: e.tensor_tensor(
                    out=t_u1[:], in0=t_cs[:],
                    in1=AP(t_cs[:].tensor, t_cs[:].offset + 127,
                           [[t_cs[:].ap[0][0], 32], [128, CPS], [0, 128]]),
                    op=ALU.subtract), inc="ve")
                P.emit("scalar", lambda e: e.activation(out=t_dsb[:], in_=t_u1[:],
                                                        func=AF.Exp, scale=-1.0),
                       waits=[("ve", P.v("ve"))], inc="se")
                P.emit("vector", lambda e: e.tensor_tensor(
                    out=w_bf[:], in0=t_dtb[:], in1=t_dsb[:], op=ALU.mult),
                    waits=[("se", P.v("se"))], inc="ve")
                dt_ve = P.v("ve")
                dt_se = P.v("se")
                P.emit("gpsimd", lambda e, s=s: e.dma_start(
                    out=AP(cd_scr.tensor, cd_scr.offset + s * CPS * 32,
                           [[1, 32], [32, CPS]]),
                    in_=cd4[:], allow_slow_non_contiguous=True),
                    waits=[("ve", dt_ve), ("se", dt_se)], inc="gp", step=16)
                P.emit("scalar", lambda e, s=s: e.dma_start(
                    out=dcb[:], in_=AP(cd_scr.tensor, cd_scr.offset + s * CPS * 32,
                                       [[0, 128], [1, CPS * 32]])),
                    waits=[("gp", P.v("gp")), ("po", P.v("po"))], inc="dc", step=16)
                dc_s = P.v("dc")
                P.emit("sync", lambda e: e.dma_start_transpose(
                    r3(dttm, 0, 128, [(32, CPS), (1, 32)]), t_dtb[:]),
                    waits=[("ve", dt_ve), ("po", P.v("po")), ("te", P.v("te"))],
                    inc="sy", step=16)
                P.emit("sync", lambda e: e.dma_start_transpose(
                    r3(gctm, 0, 128, [(32, CPS), (1, 32)]), t_gcb[:]),
                    inc="sy", step=16)
                P.emit("sync", lambda e: e.dma_start_transpose(
                    r3(gstm, 0, 128, [(32, CPS), (1, 32)]), t_gsb[:]),
                    inc="sy", step=16)
                P.emit("sync", lambda e: e.dma_start_transpose(
                    r3(dstm, 0, 128, [(32, CPS), (1, 32)]), w_bf[:]),
                    waits=[("ve", dt_ve)], inc="sy", step=16)
                P.emit("gpsimd", lambda e, s=s: e.dma_start(
                    out=AP(st_g.tensor, st_g.offset + s * CPS * 128 * 32,
                           [[32, 128], [128 * 32, CPS], [1, 32]]),
                    in_=r3(gstm, 0, 128, [(32, CPS), (1, 32)])),
                    waits=[("sy", P.v("sy"))], inc="gp", step=16)
                if s + 1 < nsup:
                    emit_ln_chunk(s + 1, 0)

                def emit_G(c, wv):
                    P.emit("tensor", lambda e, c=c: e.matmul(
                        pA[:, 1024:1152],
                        xbc_bf[:, 1024 + c * 128:1024 + (c + 1) * 128],
                        xbc_bf[:, 1536 + c * 128:1536 + (c + 1) * 128],
                        start=True, stop=True),
                        waits=wv, inc="te")
                    P.emit("vector", lambda e: e.tensor_copy(g_sb[:], pA[:, 1024:1152]),
                           waits=[("te", P.v("te"))], inc="ve")
                    return P.v("ve")

                def emit_eab_loads(ch2):
                    s2 = ch2 // CPS
                    P.emit("scalar", lambda e, ch2=ch2: e.dma_start(
                        out=eab[:, 0:4096],
                        in_=AP(expa_scr.tensor, expa_scr.offset + ch2 * 32 * Q,
                               [[0, 128], [1, 32 * Q]])),
                        waits=list(eab_cons[0]) + [("gp", expa_gp[s2])],
                        inc="e0", step=16)
                    eab_load_e0[ch2] = P.v("e0")

                def emit_scans(ch2):
                    for ll in range(4):
                        for i4 in range(2):
                            i = ll * 2 + i4
                            w0 = []
                            if i == 0:
                                w0 = [("e0", eab_load_e0[ch2]),
                                      ("ve", mtil_done[0])]
                            P.emit("vector", lambda e, i=i, i4=i4, ll=ll:
                                   e.tensor_tensor_scan(
                                       out=e_t[:, i * 512:(i + 1) * 512],
                                       data0=eab[:, ll * 1024 + i4 * 512:ll * 1024 + (i4 + 1) * 512],
                                       data1=irp[:], initial=0.0,
                                       op0=ALU.mult, op1=ALU.add),
                                   waits=w0, inc="ve")
                    eab_cons[0] = [("ve", P.v("ve"))]
                    scan_cnt[0] = (P.v("ve"), 0)
                    return P.v("ve")

                if s == 0:
                    eab_cons = [[]]
                    eab_load_e0 = {}
                    expa_gp = {}
                    mtil_done = [0]
                    scan_cnt = [(0, 0)]
                    prev_ev2 = 0
                    prev_ev1 = 0
                    prev_add_po = 0
                    prev_s_te = 0
                    prev_yo_te = 0
                    prev_r2 = 0
                expa_gp[s] = expa_gp_s

                emit_eab_loads(s * CPS)
                gsb_ve = emit_G(0, [("ve", prev_ev2), ("se", zphase_se),
                                    ("te", P.v("te")), ("ve", P.v("ve"))])
                scan_done = emit_scans(s * CPS)
                P.emit("vector", lambda e: e.tensor_tensor(
                    out=hq(xt_t[:]), in0=hq(xtm[:, 0:DI]),
                    in1=r3(dttm, 0, 128, [(1, NH), (0, HP)]), op=ALU.mult),
                    waits=[("sy", tm_sy), ("te", P.v("te"))], inc="ve")
                xtil_done = ("ve", P.v("ve"))
                P.emit("gpsimd", lambda e: e.tensor_tensor(
                    out=hq(xd_t[:]), in0=hq(xtm[:, 0:DI]),
                    in1=r3(dti, 0, 128, [(1, NH), (0, HP)]), op=ALU.mult),
                    waits=[("sy", tm_sy), ("te", P.v("te"))], inc="po")
                xdt_done = ("po", P.v("po"))

                for c in range(CPS):
                    ch = s * CPS + c
                    for q in range(4):
                        w0 = ([("se", zphase_se), xdt_done, ("ve", P.v("ve"))]
                              if q == 0 else [])
                        P.emit("tensor", lambda e, q=q: e.matmul(
                            pA[:, q * 512:(q + 1) * 512], ibf[:],
                            xd_t[:, q * 512:(q + 1) * 512], start=True, stop=False),
                            waits=w0, inc=("te" if q == 3 else None))
                    imm_te = P.v("te")
                    for mh in range(2):
                        P.emit("vector", lambda e, mh=mh: e.tensor_tensor(
                            out=m_t[:].rearrange("p (h q) -> p h q", h=NH // 2),
                            in0=e_t[:, mh * DI:(mh + 1) * DI]
                                .rearrange("p (h q) -> p h q", h=NH // 2),
                            in1=r3(g_sb, 0, 128, [(0, NH // 2), (1, 128)]),
                            op=ALU.mult),
                            waits=([("ve", scan_cnt[0][0])] if mh == 0 else
                                   [("te", P.v("te"))]), inc="ve")
                        mtil_v = P.v("ve")
                        if mh == 1:
                            mtil_done[0] = mtil_v
                        for h2 in range(NH // 2):
                            h = mh * 16 + h2
                            w0 = ([("ve", mtil_v), xtil_done] if h2 == 0 else [])
                            P.emit("tensor", lambda e, h=h, h2=h2: e.matmul(
                                pA[:, h * 64:(h + 1) * 64],
                                m_t[:, h2 * 128:(h2 + 1) * 128],
                                xt_t[:, h * 64:(h + 1) * 64], start=False,
                                stop=(h % 8 == 7)),
                                waits=w0,
                                inc=("te" if h2 == NH // 2 - 1 else None))
                    yd_te = P.v("te")

                    P.emit("gpsimd", lambda e, c=c: e.tensor_tensor(
                        out=hq(xtd[:]), in0=hq(xtm[:, c * DI:(c + 1) * DI]),
                        in1=r3(dstm, c * 32, 128, [(1, NH), (0, HP)]),
                        op=ALU.mult),
                        waits=[("te", prev_s_te), ("sy", tm_sy)], inc="po")
                    xtd_done = P.v("po")
                    P.emit("vector", lambda e: e.tensor_copy(r_bf[:], r_t[:]),
                           waits=[("te", prev_yo_te)], inc="ve")
                    rbf_ve = P.v("ve")
                    P.emit("gpsimd", lambda e, c=c: e.tensor_tensor(
                        out=hq(sz_t[:]), in0=hq(r_t[:]),
                        in1=r3(dcb, c * 32, 128, [(1, NH), (0, HP)]),
                        op=ALU.mult),
                        waits=[("dc", dc_s), ("ve", prev_r2)],
                        inc="po")
                    r1_po = P.v("po")
                    for q in range(4):
                        w0 = ([("ve", max(prev_ev1, dt_ve) if c == 0 else prev_ev1),
                               ("po", xtd_done)] if q == 0 else [])
                        P.emit("tensor", lambda e, c=c, q=q: e.matmul(
                            pB[:, q * 512:(q + 1) * 512],
                            btm[:, c * 128:(c + 1) * 128],
                            xtd[:, q * 512:(q + 1) * 512], start=True, stop=True),
                            waits=w0, inc=("te" if q == 3 else None))
                    s_te = P.v("te")
                    P.emit("vector", lambda e: e.tensor_tensor(
                        out=r_t[:], in0=sz_t[:], in1=pB[:], op=ALU.add),
                        waits=[("po", r1_po), ("te", s_te)], inc="ve")
                    prev_r2 = P.v("ve")
                    for q in range(4):
                        w0 = [("ve", prev_r2)] if q == 0 else []
                        P.emit("tensor", lambda e, c=c, q=q: e.matmul(
                            pB[:, q * 512:(q + 1) * 512],
                            xbc_bf[:, 1536 + c * 128:1536 + (c + 1) * 128],
                            r_bf[:, q * 512:(q + 1) * 512], start=True, stop=True),
                            waits=w0, inc=("te" if q == 3 else None))
                    yo_te = P.v("te")
                    P.emit("vector", lambda e, c=c: e.tensor_tensor(
                        out=hq(ypre[:]), in0=hq(pB[:]),
                        in1=r3(gctm, c * 32, 128, [(1, NH), (0, HP)]),
                        op=ALU.mult),
                        waits=[("te", yo_te)], inc="ve")
                    ev1_ve = P.v("ve")
                    prev_ev1 = ev1_ve
                    if c < CPS - 1:
                        P.emit("gpsimd", lambda e, c=c: e.tensor_tensor(
                            out=hq(xt_t[:]), in0=hq(xtm[:, (c + 1) * DI:(c + 2) * DI]),
                            in1=r3(dttm, (c + 1) * 32, 128, [(1, NH), (0, HP)]),
                            op=ALU.mult),
                            waits=[("te", yd_te)], inc="po")
                        xtil_done = ("po", P.v("po"))
                        P.emit("gpsimd", lambda e, c=c: e.tensor_tensor(
                            out=hq(xd_t[:]), in0=hq(xtm[:, (c + 1) * DI:(c + 2) * DI]),
                            in1=r3(dti, 0, 128, [(1, NH), (0, HP)]),
                            op=ALU.mult),
                            waits=[("te", imm_te)], inc="po")
                        xdt_done = ("po", P.v("po"))
                    P.emit("vector", lambda e: e.tensor_tensor(
                        out=yp_b[:], in0=ypre[:], in1=pA[:], op=ALU.add),
                        waits=[("gp", P.v("gp")), ("te", yd_te)], inc="ve")
                    ev2_ve = P.v("ve")
                    prev_ev2 = ev2_ve
                    P.emit("gpsimd", lambda e, ch=ch: e.dma_start(
                        out=st_ypre[ch * 128:(ch + 1) * 128, :], in_=yp_b[:]),
                        waits=[("ve", ev2_ve)], inc="gp", step=16)
                    if c < CPS - 1:
                        gsb_ve = emit_G(c + 1, [("ve", ev2_ve)])
                    if c < CPS - 1:
                        emit_eab_loads(ch + 1)
                        scan_done = emit_scans(ch + 1)
                    for _wr in range(3 if ch < 16 else 2):
                        emit_wround_conv()
                        emit_wround_load()
                    if s + 1 < nsup and c < CPS - 1:
                        emit_ln_chunk(s + 1, c + 1)
                    prev_s_te = s_te
                    prev_yo_te = yo_te

            while wr_cv[0] < wr_n:
                emit_wround_conv()
                emit_wround_load()

            P.emit("gpsimd", dmaf(exch_in[0:128, :], r_t[:]),
                   waits=[("ve", P.v("ve"))], inc="gp", step=16)
            sig_fin = sig if nch % 2 == 0 else sig2
            P.emit("vector", lambda e: e.memset(ypre[0:32, :], 0.0),
                   waits=[("gp", P.v("gp"))], inc="ve")
            P.emit("gpsimd", lambda e: e.dma_start(
                out=exch_in[128:129, :], in_=ypre[0:1, :]),
                waits=[("ve", P.v("ve"))], inc="gp", step=16)
            P.emit("gpsimd", lambda e, sig_fin=sig_fin: e.dma_start(
                out=AP(exch_in.tensor, exch_in.offset + 128 * DI, [[1, 32], [1, 1]]),
                in_=sig_fin[:], allow_slow_non_contiguous=True),
                waits=[("gp", P.v("gp"))], inc="gp", step=16)
            P.emit("gpsimd", lambda e: e.collective_compute(
                "AllGather", ALU.bypass, replica_groups=[list(range(NCORE))],
                ins=[exch_in.opt()], outs=[gath.opt()]),
                waits=[("gp", P.v("gp"))], inc="cc")
            if debug:
                P.emit("gpsimd", dmaf(dbg["d_r"], exch_in),
                       waits=[("cc", P.v("cc"))], inc="gp", step=16)
                P.emit("gpsimd", dmaf(dbg["d_ypre"], st_ypre), inc="gp", step=16)
                P.emit("gpsimd", dmaf(dbg["d_sz"], st_sz), inc="gp", step=16)
                P.emit("gpsimd", lambda e: e.dma_start(out=dbg["d_dt"], in_=t_dt[:]),
                       inc="gp", step=16)

            barrier = P.barrier_vals()

        with contextlib.ExitStack() as S2:
            def sb2(name, p, f, dt):
                return S2.enter_context(nc.sbuf_tensor(name, [p, f], dt))

            woutb = sb2("woutb", 128, 16 * 1024, BF16)
            wffn = sb2("wffn", 128, KD * 1024, BF16)
            lngT = sb2("lngT", 128, DM, F32)
            lnbT = sb2("lnbT", 128, DM, F32)
            fbT = sb2("fbT", 128, DM, F32)
            R_t = sb2("R_t", 128, DI, F32)
            R_bf = sb2("R_bf", 128, DI, BF16)
            Dm_t = sb2("Dm_t", 128, 32, F32)
            Dm2_t = sb2("Dm2_t", 128, 32, F32)
            gr = sb2("gr", 128, DI, F32)
            yprd = sb2("yprd", 128, 3 * DI, BF16)
            szrd = sb2("szrd", 128, 3 * DI, BF16)
            crd = sb2("crd", 128, 3 * 128, BF16)
            grd = sb2("grd", 128, 3 * 32, BF16)
            y2 = sb2("y2", 128, DI, F32)
            ynb = sb2("ynb", 128, DI, BF16)
            ynfm = sb2("ynfm", 128, 2 * DI, BF16)
            bnrB = sb2("bnrB", 128, 12, F32)
            bnaB = sb2("bnaB", 128, 2, F32)
            col1B = sb2("col1B", 128, 1, F32)
            col2B = sb2("col2B", 128, 1, F32)
            muS = sb2("muS", 128, 5, F32)
            sigS = sb2("sigS", 128, 5, F32)
            xn32 = sb2("xn32", 128, 5 * DM, F32)
            h_t = sb2("h_t", 128, DM, F32)
            u1_t = sb2("u1_t", 128, DM, F32)
            f1_t = sb2("f1_t", 128, DM, F32)
            h1b = sb2("h1b", 128, KD * 512, BF16)
            h2b = sb2("h2b", 128, KD * 512, BF16)
            outst = sb2("outst", 128, 2 * DM, F32)

            first = {e: True for e in ENGS}

            def em(eng, fn, waits=(), inc=None, step=1):
                w = list(waits)
                if first[eng]:
                    w = barrier + w
                    first[eng] = False
                P.emit(eng, fn, w, inc, step)

            cc1 = P.v("cc")
            em("sync", dmaf(woutb[:].rearrange("p (a b) -> p a b", a=16),
                            wout_bfh.rearrange("(a p) b -> p a b", p=128)),
               inc="sy", step=16)
            em("sync", dmaf(lngT[:], lng_t), inc="sy", step=16)
            em("sync", dmaf(lnbT[:], lnb_t), inc="sy", step=16)
            em("sync", dmaf(fbT[:], fbias_t), inc="sy", step=16)
            w2_sy = P.v("sy")

            R2_t = y2
            em("vector", lambda e: e.memset(R_t[:], 0.0), inc="ve")
            for u in range(NCORE):
                em("scalar", lambda e, u=u: e.dma_start(
                    out=Dm_t[:], in_=AP(gath.tensor,
                                        gath.offset + (u * 129 + 128) * DI,
                                        [[0, 128], [1, 32]])),
                    waits=[("cc", cc1), ("ve", P.v("ve"))], inc="sc", step=16)
                em("vector", lambda e, u=u: e.tensor_scalar(
                    out=Dm2_t[:], in0=Dm_t[:], scalar1=gamt[:, u:u + 1],
                    scalar2=igamt[:, u:u + 1], op0=ALU.mult, op1=ALU.add),
                    waits=[("sc", P.v("sc"))], inc="ve")
                em("gpsimd", lambda e, u=u: e.dma_start(
                    out=gr[:], in_=gath[u * 129:u * 129 + 128, :]),
                    waits=[("cc", cc1), ("ve", P.v("ve"))], inc="gp", step=16)
                em("vector", lambda e: e.tensor_tensor(
                    out=hq(R2_t[:]), in0=hq(R_t[:]),
                    in1=r3(Dm2_t, 0, 128, [(1, NH), (0, HP)]), op=ALU.mult),
                    inc="ve")
                em("vector", lambda e, u=u: e.scalar_tensor_tensor(
                    out=R_t[:], in0=gr[:], scalar=alct[:, u:u + 1], in1=R2_t[:],
                    op0=ALU.mult, op1=ALU.add),
                    waits=[("gp", P.v("gp"))], inc="ve")
            em("vector", lambda e: e.tensor_scalar(
                out=R_bf[:], in0=R_t[:], scalar1=1.0, scalar2=None, op0=ALU.mult),
                inc="ve")
            R_ve = P.v("ve")
            if debug:
                em("gpsimd", dmaf(dbg["d_R"], R_t[:]), waits=[("ve", R_ve)],
                   inc="gp", step=16)

            yprd_free = {0: 0, 1: 0, 2: 0}
            szrd_free = {0: 0, 1: 0, 2: 0}
            crd_free = {0: 0, 1: 0, 2: 0}
            grd_free = {0: 0, 1: 0, 2: 0}
            ynfm_free = {0: 0, 1: 0}
            stash_sc = {}
            yn_sy_d = {}
            pa_free = [0]
            pb_lo_free = [0]
            pb_hi_free = [0]
            f1_free = [0]
            ln2_sy = {}
            ph2_xnfm_free = [P.v("te")]

            def emit_ln2_chunk(s2, c2):
                t0 = s2 * SUP + c2 * Q
                em("gpsimd", dmaf(x_t[:], x_seg[t0:t0 + 128, :]),
                   waits=[("ve", P.v("ve")), ("po", P.v("po"))], inc="gp", step=16)
                xs_ = P.v("gp")
                em("vector", lambda e: e.bn_stats(
                    out=bnr[:, 0:6], in_=x_t[:, 0:512]),
                    waits=[("gp", xs_), ("se", P.v("se"))], inc="ve")
                em("vector", lambda e: e.bn_stats(
                    out=bnr[:, 6:12], in_=x_t[:, 512:1024]),
                    inc="ve")
                em("vector", lambda e: e.bn_aggr(out=bna[:], in_=bnr[:, 0:12]),
                   waits=[("ve", P.v("ve"))], inc="ve")
                em("scalar", lambda e: e.activation(out=col1[:], in_=bna[:, 1:2],
                                                    func=AF.Ln, bias=cep[:]),
                   waits=[("ve", P.v("ve"))], inc="se")
                em("scalar", lambda e: e.activation(out=col2[:], in_=col1[:],
                                                    func=AF.Exp, scale=-0.5), inc="se")
                g5 = (s2 * CPS + c2) % 5
                em("scalar", lambda e, g5=g5: e.activation(
                    out=sigS[:, g5:g5 + 1], in_=col1[:], func=AF.Exp, scale=0.5),
                    inc="se")
                em("vector", lambda e, g5=g5: e.tensor_copy(
                    muS[:, g5:g5 + 1], bna[:, 0:1]),
                    waits=[("se", P.v("se"))], inc="ve")
                em("vector", lambda e, g5=g5: e.scalar_tensor_tensor(
                    out=xn32[:, g5 * DM:(g5 + 1) * DM],
                    in0=x_t[:], scalar=bna[:, 0:1],
                    in1=bcast_col(col2, DM), op0=ALU.subtract, op1=ALU.mult),
                    inc="ve")
                em("vector", lambda e, g5=g5: e.tensor_scalar(
                    out=xn_tmb[:], in0=xn32[:, g5 * DM:(g5 + 1) * DM],
                    scalar1=1.0, scalar2=None, op0=ALU.mult),
                    waits=[("sy", P.v("sy"))], inc="ve")
                em("sync", lambda e, c2=c2: e.dma_start_transpose(
                    r3(xn_fm, c2 * 128, 128, [(SUP, KD), (1, 128)]), xn_tmb[:]),
                    waits=[("ve", P.v("ve")), ("te", ph2_xnfm_free[0])],
                    inc="sy", step=16)
                ln2_sy[s2] = P.v("sy")

            for s in range(nsup):
                sup_off = s * SUP
                if s == 0:
                    for c in range(CPS):
                        emit_ln2_chunk(0, c)
                xn2_sy = ln2_sy[s]

                em("gpsimd", dmaf(wffn[:].rearrange("p (a b) -> p a b", a=KD),
                                  wf_bfh.rearrange("(a p) b -> p a b", p=128)),
                   waits=[("te", P.v("te"))], inc="gp", step=16)
                wl_gp = P.v("gp")
                for o in range(KD):
                    for k in range(KD):
                        w0 = ([("sy", xn2_sy), ("gp", wl_gp), ("ve", P.v("ve")),
                               ("se", P.v("se"))]
                              if (o == 0 and k == 0) else
                              ([("se", P.v("se"))] if k == 0 else []))
                        em("tensor", lambda e, o=o, k=k: e.matmul(
                            pB[:, 1024:1536],
                            wffn[:, k * 1024 + o * 128:k * 1024 + (o + 1) * 128],
                            xn_fm[:, k * SUP:(k + 1) * SUP],
                            start=(k == 0), stop=(k == KD - 1)),
                            waits=w0, inc=("te" if k == KD - 1 else None))
                    em("scalar", lambda e, o=o: e.copy(
                        out=h1b[:, o * 512:(o + 1) * 512], in_=pB[:, 1024:1536]),
                        waits=[("te", P.v("te"))], inc="se")
                h1_se = P.v("se")
                em("gpsimd", dmaf(wffn[:].rearrange("p (a b) -> p a b", a=KD),
                                  a1_bfh.rearrange("(a p) b -> p a b", p=128)),
                   waits=[("te", P.v("te"))], inc="gp", step=16)
                wl_gp = P.v("gp")
                for o in range(KD):
                    for k in range(KD):
                        w0 = ([("se", h1_se), ("gp", wl_gp)] if (o == 0 and k == 0)
                              else ([("se", P.v("se"))] if k == 0 else []))
                        em("tensor", lambda e, o=o, k=k: e.matmul(
                            pB[:, 1024:1536],
                            wffn[:, k * 1024 + o * 128:k * 1024 + (o + 1) * 128],
                            h1b[:, k * 512:(k + 1) * 512],
                            start=(k == 0), stop=(k == KD - 1)),
                            waits=w0, inc=("te" if k == KD - 1 else None))
                    em("scalar", lambda e, o=o: e.copy(
                        out=h2b[:, o * 512:(o + 1) * 512], in_=pB[:, 1024:1536]),
                        waits=[("te", P.v("te"))], inc="se")
                h2_se = P.v("se")
                ph2_xnfm_free[0] = P.v("te")
                em("gpsimd", dmaf(wffn[:].rearrange("p (a b) -> p a b", a=KD),
                                  a2_bfh.rearrange("(a p) b -> p a b", p=128)),
                   waits=[("te", P.v("te"))], inc="gp", step=16)
                a2_gp = P.v("gp")

                def emit_stash(ch2):
                    if ch2 >= nch or ch2 in stash_sc:
                        return
                    sl2 = ch2 % 3
                    em("scalar", lambda e, ch2=ch2, sl2=sl2: e.dma_start(
                        out=yprd[:, sl2 * DI:(sl2 + 1) * DI],
                        in_=st_ypre[ch2 * 128:(ch2 + 1) * 128, :]),
                        waits=[("ve", yprd_free[sl2])], inc="sc", step=16)
                    em("scalar", lambda e, ch2=ch2, sl2=sl2: e.dma_start(
                        out=szrd[:, sl2 * DI:(sl2 + 1) * DI],
                        in_=st_sz[ch2 * 128:(ch2 + 1) * 128, :]),
                        waits=[("ve", szrd_free[sl2])], inc="sc", step=16)
                    em("scalar", lambda e, ch2=ch2, sl2=sl2: e.dma_start(
                        out=crd[:, sl2 * 128:(sl2 + 1) * 128], in_=st_c[ch2]),
                        waits=[("te", crd_free[sl2])], inc="sc", step=16)
                    em("scalar", lambda e, ch2=ch2, sl2=sl2: e.dma_start(
                        out=grd[:, sl2 * 32:(sl2 + 1) * 32], in_=st_g[ch2]),
                        waits=[("ve", grd_free[sl2])], inc="sc", step=16)
                    stash_sc[ch2] = P.v("sc")

                if s == 0:
                    emit_stash(0)
                    emit_stash(1)

                def emit_A(c):
                    ch = s * CPS + c
                    sl = ch % 2
                    sl3 = ch % 3
                    for q in range(4):
                        w0 = ([("ve", R_ve), ("sc", stash_sc[ch]),
                               ("ve", pa_free[0])] if q == 0 else [])
                        em("tensor", lambda e, sl3=sl3, q=q: e.matmul(
                            pA[:, q * 512:(q + 1) * 512],
                            crd[:, sl3 * 128:(sl3 + 1) * 128],
                            R_bf[:, q * 512:(q + 1) * 512], start=True, stop=True),
                            waits=w0, inc=("te" if q == 3 else None))
                    dy_te = P.v("te")
                    crd_free[sl3] = dy_te
                    em("vector", lambda e, sl3=sl3: e.tensor_tensor(
                        out=hq(gr[:]), in0=hq(pA[:]),
                        in1=r3(grd, sl3 * 32, 128, [(1, NH), (0, HP)]), op=ALU.mult),
                        waits=[("te", dy_te)] + ([("gp", P.v("gp"))] if debug else []),
                        inc="ve")
                    gate_ve = P.v("ve")
                    pa_free[0] = gate_ve
                    grd_free[sl3] = gate_ve
                    em("vector", lambda e, sl3=sl3: e.tensor_tensor(
                        out=y2[:], in0=gr[:], in1=yprd[:, sl3 * DI:(sl3 + 1) * DI],
                        op=ALU.add), inc="ve")
                    yprd_free[sl3] = P.v("ve")
                    em("vector", lambda e, sl3=sl3: e.tensor_tensor(
                        out=gr[:], in0=y2[:], in1=szrd[:, sl3 * DI:(sl3 + 1) * DI],
                        op=ALU.mult), inc="ve")
                    yg_ve = P.v("ve")
                    szrd_free[sl3] = yg_ve
                    if debug and s == nsup - 1:
                        em("gpsimd", lambda e, c=c: e.dma_start(
                            out=dbg["d_yg"][c * 128:(c + 1) * 128, :], in_=gr[:]),
                            waits=[("ve", yg_ve)], inc="gp", step=16)
                    for t4 in range(4):
                        em("vector", lambda e, t4=t4: e.bn_stats(
                            out=bnr[:, t4 * 6:(t4 + 1) * 6],
                            in_=gr[:, t4 * 512:(t4 + 1) * 512]),
                            waits=([("se", P.v("se"))] if t4 == 0 else []), inc="ve")
                    em("vector", lambda e: e.bn_aggr(out=bna[:], in_=bnr[:, 0:24]),
                       waits=[("ve", P.v("ve"))], inc="ve")
                    em("vector", lambda e: e.scalar_tensor_tensor(
                        out=col1[:], in0=bna[:, 0:1], scalar=bna[:, 0:1],
                        in1=bna[:, 1:2], op0=ALU.mult, op1=ALU.add), inc="ve")
                    em("scalar", lambda e: e.activation(out=col1[:], in_=col1[:],
                                                        func=AF.Ln, bias=cep[:]),
                       waits=[("ve", P.v("ve"))], inc="se")
                    em("scalar", lambda e: e.activation(out=col2[:], in_=col1[:],
                                                        func=AF.Exp, scale=-0.5), inc="se")
                    em("vector", lambda e: e.tensor_scalar(
                        out=ynb[:, 0:1024], in0=gr[:, 0:1024], scalar1=col2[:],
                        scalar2=None, op0=ALU.mult),
                        waits=[("se", P.v("se")), ("sy", P.v("sy"))] +
                              ([("gp", P.v("gp"))] if debug else []), inc="ve")
                    em("sync", lambda e, sl=sl: e.dma_start_transpose(
                        r3(ynfm, sl * DI, 128, [(128, NXS // 2), (1, 128)]),
                        ynb[:, 0:1024]),
                        waits=[("ve", P.v("ve")), ("te", ynfm_free[sl])],
                        inc="sy", step=16)
                    yn_h1 = P.v("sy")
                    em("vector", lambda e: e.tensor_scalar(
                        out=ynb[:, 1024:2048], in0=gr[:, 1024:2048], scalar1=col2[:],
                        scalar2=None, op0=ALU.mult), inc="ve")
                    if debug and s == nsup - 1:
                        em("gpsimd", lambda e, c=c: e.dma_start(
                            out=dbg["d_ynb"][c * 128:(c + 1) * 128, :], in_=ynb[:]),
                            waits=[("ve", P.v("ve"))], inc="gp", step=16)
                        em("gpsimd", lambda e, c=c: e.dma_start(
                            out=dbg["d_xn32"][c * 128:(c + 1) * 128, :],
                            in_=xn32[:, c * DM:(c + 1) * DM]),
                            inc="gp", step=16)
                    em("sync", lambda e, sl=sl: e.dma_start_transpose(
                        r3(ynfm, sl * DI + 1024, 128, [(128, NXS // 2), (1, 128)]),
                        ynb[:, 1024:2048]),
                        waits=[("ve", P.v("ve"))] +
                              ([("gp", P.v("gp"))] if debug else []),
                        inc="sy", step=16)
                    yn_sy_d[ch] = (yn_h1, P.v("sy"))
                    emit_stash(ch + 2)

                def emit_B(c):
                    ch = s * CPS + c
                    sl = ch % 2
                    g5 = ch % 5
                    mcol = muS[:, g5:g5 + 1]
                    em("vector", lambda e, g5=g5, mcol=mcol: e.scalar_tensor_tensor(
                        out=f1_t[:], in0=xn32[:, g5 * DM:(g5 + 1) * DM],
                        scalar=sigS[:, g5:g5 + 1],
                        in1=AP(mcol.tensor, mcol.offset,
                               [[mcol.ap[0][0], 128], [0, DM]]),
                        op0=ALU.mult, op1=ALU.add),
                        waits=[("se", f1_free[0])], inc="ve")
                    xrec_ve = P.v("ve")
                    for q in range(2):
                        for k in range(16):
                            if q == 0 and k == 0:
                                w0 = [("sy", yn_sy_d[ch][0]),
                                      ("ve", pb_lo_free[0])]
                            elif q == 0 and k == 8:
                                w0 = [("sy", yn_sy_d[ch][1])]
                            else:
                                w0 = []
                            em("tensor", lambda e, sl=sl, q=q, k=k: e.matmul(
                                pB[:, q * 512:(q + 1) * 512],
                                ynfm[:, sl * DI + k * 128:sl * DI + (k + 1) * 128],
                                woutb[:, k * 1024 + q * 512:k * 1024 + (q + 1) * 512],
                                start=(k == 0), stop=(k == 15)),
                                waits=w0,
                                inc=("te" if (k == 15 and q == 1) else None))
                    wo_te = P.v("te")
                    ynfm_free[sl] = wo_te
                    em("vector", lambda e, g5=g5: e.tensor_tensor(
                        out=h_t[:], in0=xn32[:, g5 * DM:(g5 + 1) * DM],
                        in1=pB[:, 0:1024], op=ALU.add),
                        waits=[("te", wo_te)], inc="ve")
                    pb_lo_free[0] = P.v("ve")
                    if debug and s == nsup - 1:
                        em("gpsimd", lambda e, c=c: e.dma_start(
                            out=dbg["d_h"][c * 128:(c + 1) * 128, :], in_=h_t[:]),
                            waits=[("ve", P.v("ve"))], inc="gp", step=16)
                    em("vector", lambda e: e.bn_stats(out=bnrB[:, 0:6],
                                                      in_=h_t[:, 0:512]),
                       waits=[("se", P.v("se"))], inc="ve")
                    em("vector", lambda e: e.bn_stats(out=bnrB[:, 6:12],
                                                      in_=h_t[:, 512:1024]),
                       inc="ve")
                    em("vector", lambda e: e.bn_aggr(out=bnaB[:], in_=bnrB[:, 0:12]),
                       waits=[("ve", P.v("ve"))], inc="ve")
                    em("scalar", lambda e: e.activation(out=col1B[:], in_=bnaB[:, 1:2],
                                                        func=AF.Ln, bias=cep[:]),
                       waits=[("ve", P.v("ve"))], inc="se")
                    em("scalar", lambda e: e.activation(out=col2B[:], in_=col1B[:],
                                                        func=AF.Exp, scale=-0.5),
                       inc="se")
                    em("vector", lambda e: e.scalar_tensor_tensor(
                        out=u1_t[:], in0=h_t[:], scalar=bnaB[:, 0:1],
                        in1=bcast_col(col2B, DM), op0=ALU.subtract, op1=ALU.mult),
                        waits=[("se", P.v("se"))] + ([("gp", P.v("gp"))] if debug else []),
                        inc="ve")
                    em("vector", lambda e: e.tensor_tensor(
                        out=h_t[:], in0=u1_t[:], in1=lngT[:], op=ALU.mult),
                        waits=([("gp", P.v("gp"))] if debug else []), inc="ve")
                    em("gpsimd", lambda e: e.tensor_tensor(
                        out=u1_t[:], in0=h_t[:], in1=lnbT[:], op=ALU.add),
                        waits=[("ve", P.v("ve"))], inc="po")
                    em("gpsimd", lambda e: e.tensor_tensor(
                        out=h_t[:], in0=u1_t[:], in1=f1_t[:],
                        op=ALU.add),
                        waits=[("ve", xrec_ve)], inc="po")
                    u1_po = P.v("po")
                    for q in range(2):
                        for k in range(KD):
                            w0 = ([("se", h2_se), ("gp", a2_gp),
                                   ("ve", pb_hi_free[0])]
                                  if (q == 0 and k == 0) else [])
                            em("tensor", lambda e, c=c, q=q, k=k: e.matmul(
                                pB[:, 1024 + q * 512:1024 + (q + 1) * 512],
                                h2b[:, k * 512 + c * 128:k * 512 + (c + 1) * 128],
                                wffn[:, k * 1024 + q * 512:k * 1024 + (q + 1) * 512],
                                start=(k == 0), stop=(k == KD - 1)),
                                waits=w0,
                                inc=("te" if (k == KD - 1 and q == 1) else None))
                    f3_te = P.v("te")
                    em("vector", lambda e: e.tensor_tensor(
                        out=f1_t[:], in0=pB[:, 1024:2048], in1=fbT[:], op=ALU.add),
                        waits=[("te", f3_te), ("po", u1_po)], inc="ve")
                    pb_hi_free[0] = P.v("ve")
                    em("scalar", lambda e: e.activation(out=u1_t[:], in_=f1_t[:],
                                                        func=AF.Tanh),
                       waits=[("ve", P.v("ve")), ("po", u1_po)] +
                             ([("gp", P.v("gp"))] if debug else []), inc="se")
                    f1_free[0] = P.v("se")
                    if debug and s == nsup - 1:
                        em("gpsimd", lambda e, c=c: e.dma_start(
                            out=dbg["d_f1"][c * 128:(c + 1) * 128, :], in_=u1_t[:]),
                            waits=[("ve", P.v("ve"))], inc="gp", step=16)
                    em("vector", lambda e, sl=sl: e.tensor_tensor(
                        out=outst[:, sl * DM:(sl + 1) * DM], in0=h_t[:], in1=u1_t[:],
                        op=ALU.add),
                        waits=[("se", P.v("se")), ("po", u1_po), ("gp", P.v("gp"))],
                        inc="ve")
                    em("gpsimd", lambda e, ch=ch, sl=sl: e.dma_start(
                        out=out_seg[ch * 128:(ch + 1) * 128, :],
                        in_=outst[:, sl * DM:(sl + 1) * DM]),
                        waits=[("ve", P.v("ve"))], inc="gp", step=16)

                for c in range(CPS):
                    emit_A(c)
                    if c > 0:
                        emit_B(c - 1)
                    if s + 1 < nsup:
                        emit_ln2_chunk(s + 1, c)
                emit_B(CPS - 1)

            P.emit("gpsimd", lambda e: e.nop(), waits=[("gp", P.v("gp"))])

            OWN = {"vector": "ve", "scalar": "se", "gpsimd": "po", "sync": "sy"}
            with nc.Block() as block:
                for eng_name in ENGS:
                    def make(eng_name):
                        def body(e):
                            own = OWN.get(eng_name)
                            own_cnt = 0
                            last_wait = 0
                            for (fn, waits, inc, step) in P.items[eng_name]:
                                if own is not None and own_cnt > last_wait:
                                    e.wait_ge(sems[own], own_cnt)
                                    last_wait = own_cnt
                                for (sname, val) in waits:
                                    e.wait_ge(sems[sname], val)
                                r = fn(e)
                                if inc is not None:
                                    r.then_inc(sems[inc], step)
                                    if sname_is_own := (inc == own):
                                        own_cnt += step
                        return body
                    getattr(block, eng_name)(make(eng_name))

    return nc, dbg


def kernel(**inputs):
    return _run(inputs, debug=False)[0]


def _run(inputs, debug=False):
    x = np.asarray(inputs["L0_attn_hidden_states"])
    B, T, _ = x.shape
    seg_len = T // NSEG
    w_in = np.ascontiguousarray(inputs["L0_attn_W_in"])
    conv_w = np.asarray(inputs["L0_attn_conv_w"])
    conv_b = np.asarray(inputs["L0_attn_conv_b"])
    dt_bias = np.asarray(inputs["L0_attn_dt_bias"])
    A_log = np.asarray(inputs["L0_attn_A_log"])
    D = np.asarray(inputs["L0_attn_D"])
    rms_w = np.asarray(inputs["L0_attn_rms_w"])
    w_out = np.ascontiguousarray(inputs["L0_attn_W_out"])
    ln_g = np.asarray(inputs["L0_attn_ln_g"])
    ln_b = np.asarray(inputs["L0_attn_ln_b"])
    wf = np.ascontiguousarray(inputs["L0_ffn_weight"])
    a1 = np.ascontiguousarray(inputs["L0_ffn_aux_weight1"])
    a2 = np.ascontiguousarray(inputs["L0_ffn_aux_weight2"])
    fb = np.asarray(inputs["L0_ffn_bias"])

    conv_wt = np.ascontiguousarray(
        conv_w.reshape(NBC, 128, 4).transpose(1, 0, 2).reshape(128, NBC * 4))
    conv_bt = np.ascontiguousarray(conv_b.reshape(NBC, 128).T)
    dtb_col = np.ascontiguousarray(dt_bias.reshape(32, 1))
    alog_col = np.ascontiguousarray(A_log.reshape(32, 1))
    d_tile = np.ascontiguousarray(np.tile(D.reshape(1, 32), (128, 1)))
    rms_t = np.ascontiguousarray(rms_w.reshape(16, 128).T)
    lng_t = np.ascontiguousarray(np.tile(ln_g.reshape(1, DM), (128, 1)))
    lnb_t = np.ascontiguousarray(np.tile(ln_b.reshape(1, DM), (128, 1)))
    fbias_t = np.ascontiguousarray(np.tile(fb.reshape(1, DM), (128, 1)))
    i_bf = np.eye(128, dtype=BF)
    irep = np.ascontiguousarray(np.tile(np.eye(128, dtype=BF), (1, 4)))

    nc, _ = build_nc(seg_len, debug=debug)

    in_maps = []
    for core in range(NCORE):
        b, s = core // NSEG, core % NSEG
        t0 = s * seg_len
        xs = np.ascontiguousarray(x[b, t0:t0 + seg_len])
        if s == 0:
            xh = np.zeros((16, DM), np.float32)
        else:
            xh = np.ascontiguousarray(x[b, t0 - 16:t0])
        alpha = np.zeros(8, np.float32)
        gam = np.zeros(8, np.float32)
        for u in range(NCORE):
            if u // NSEG == b and (u % NSEG) < s:
                alpha[u] = 1.0
                gam[u] = 1.0
        alpha_t = np.ascontiguousarray(np.tile(alpha.reshape(1, 8), (128, 1)))
        gam_t = np.ascontiguousarray(np.tile(gam.reshape(1, 8), (128, 1)))
        igam_t = np.ascontiguousarray(1.0 - gam_t)
        in_maps.append(dict(
            x_seg=xs, x_halo=xh, w_in=w_in, conv_wt=conv_wt, conv_bt=conv_bt,
            dtb_col=dtb_col, alog_col=alog_col, d_tile=d_tile, rms_t=rms_t,
            w_out=w_out, wf=wf, a1=a1, a2=a2, lng_t=lng_t, lnb_t=lnb_t,
            fbias_t=fbias_t, i_bf=i_bf, irep=irep, alpha_t=alpha_t,
            gam_t=gam_t, igam_t=igam_t,
            eps_c=np.full((128, 1), EPS, np.float32)))

    res = run_bass_kernel_spmd(nc, in_maps, core_ids=list(range(NCORE)))
    out = np.empty((B, T, DM), np.float32)
    for core in range(NCORE):
        b, s = core // NSEG, core % NSEG
        out[b, s * seg_len:(s + 1) * seg_len] = res.results[core]["out_seg"]
    return out, res.results


# version 15
# speedup vs baseline: 1.0775x; 1.0235x over previous
import numpy as np
import ml_dtypes
import contextlib
import concourse.bass as bass
import concourse.mybir as mybir
from concourse.bass_utils import run_bass_kernel_spmd
from concourse.alu_op_type import AluOpType as ALU

F32 = mybir.dt.float32
BF16 = mybir.dt.bfloat16
AF = mybir.ActivationFunctionType
AP = bass.AP

NCORE = 8
NSEG = 4
DM = 1024
DI = 2048
NH = 32
HP = 64
Q = 128
SUP = 512
CPS = SUP // Q
NXS = 16
NBC = 18
KD = 8
EPS = 1e-5
BF = ml_dtypes.bfloat16

ENGS = ("sync", "vector", "scalar", "tensor", "gpsimd")
SEMS = ("sy", "sc", "gp", "te", "ve", "se", "po", "cc", "e0", "e1", "dc")


class Prog:
    def __init__(self):
        self.items = {e: [] for e in ENGS}
        self.cnt = {s: 0 for s in SEMS}

    def emit(self, eng, fn, waits=(), inc=None, step=1):
        if inc is not None:
            self.cnt[inc] += step
        self.items[eng].append((fn, tuple((s, v) for (s, v) in waits if v > 0), inc, step))

    def v(self, name):
        return self.cnt[name]

    def barrier_vals(self):
        return [(s, self.cnt[s]) for s in SEMS if self.cnt[s] > 0]


def _emit_setup(P, nc, env):
    (w_in_bf, wstage, wstage2, cwt, cbt, dtb, alog, acol, dti, rmsw, ibf, irp,
     alct, gamt, igamt, conv_wt, conv_bt, dtb_col, alog_col, d_tile, rms_t,
     i_bf, irep, alpha_t, gam_t, igam_t, w_in, dmaf) = (
        env["w_in_bf"], env["wstage"], env["wstage2"], env["cwt"], env["cbt"],
        env["dtb"], env["alog"], env["acol"], env["dti"], env["rmsw"], env["ibf"],
        env["irp"], env["alct"], env["gamt"], env["igamt"], env["conv_wt"],
        env["conv_bt"], env["dtb_col"], env["alog_col"], env["d_tile"], env["rms_t"],
        env["i_bf"], env["irep"], env["alpha_t"], env["gam_t"], env["igam_t"],
        env["w_in"], env["dmaf"])
    cep, eps_c = env["cep"], env["eps_c"]
    for (dst, src) in ((cwt, conv_wt), (cbt, conv_bt), (dtb, dtb_col),
                       (alog, alog_col), (dti, d_tile), (rmsw, rms_t),
                       (ibf, i_bf), (irp, irep), (alct, alpha_t),
                       (gamt, gam_t), (igamt, igam_t), (cep, eps_c)):
        P.emit("sync", dmaf(dst[:], src), inc="sy", step=16)
    setup_sy = P.v("sy")
    col1 = env["col1"]
    P.emit("scalar", lambda e: e.activation(out=col1[0:32, :], in_=alog[:], func=AF.Exp),
           waits=[("sy", setup_sy)], inc="se")
    P.emit("vector", lambda e: e.tensor_scalar(out=acol[:], in0=col1[0:32, :],
                                               scalar1=-1.0, scalar2=None,
                                               op0=ALU.mult),
           waits=[("se", P.v("se"))], inc="ve")
    QS = (("sync", "sy"), ("scalar", "sc"), ("gpsimd", "gp"))
    stg = (wstage, wstage2)
    conv_done = [0] * KD
    for k in range(KD):
        qeng, qsem = QS[k % 3]
        w = [("ve", conv_done[k - 2])] if k >= 2 else []
        P.emit(qeng, dmaf(stg[k % 2][:], w_in[k * 128:(k + 1) * 128, :]),
               waits=w, inc=qsem, step=16)
        ld = (qsem, P.v(qsem))
        P.emit("vector", lambda e, k=k: e.tensor_scalar(
            out=w_in_bf[:, k * 4384:(k + 1) * 4384], in0=stg[k % 2][:],
            scalar1=1.0, scalar2=None, op0=ALU.mult),
            waits=[ld], inc="ve")
        conv_done[k] = P.v("ve")


def build_nc(seg_len, debug=False):
    assert seg_len % SUP == 0
    nsup = seg_len // SUP
    nch = seg_len // Q

    nc = bass.Bass("TRN2", target_bir_lowering=False)

    x_seg = nc.dram_tensor("x_seg", [seg_len, DM], F32, kind="ExternalInput").ap()
    x_halo = nc.dram_tensor("x_halo", [16, DM], F32, kind="ExternalInput").ap()
    w_in = nc.dram_tensor("w_in", [DM, 4384], F32, kind="ExternalInput").ap()
    conv_wt = nc.dram_tensor("conv_wt", [128, NBC * 4], F32, kind="ExternalInput").ap()
    conv_bt = nc.dram_tensor("conv_bt", [128, NBC], F32, kind="ExternalInput").ap()
    dtb_col = nc.dram_tensor("dtb_col", [32, 1], F32, kind="ExternalInput").ap()
    alog_col = nc.dram_tensor("alog_col", [32, 1], F32, kind="ExternalInput").ap()
    d_tile = nc.dram_tensor("d_tile", [128, 32], F32, kind="ExternalInput").ap()
    rms_t = nc.dram_tensor("rms_t", [128, 16], F32, kind="ExternalInput").ap()
    w_out = nc.dram_tensor("w_out", [DI, DM], F32, kind="ExternalInput").ap()
    wf = nc.dram_tensor("wf", [DM, DM], F32, kind="ExternalInput").ap()
    a1 = nc.dram_tensor("a1", [DM, DM], F32, kind="ExternalInput").ap()
    a2 = nc.dram_tensor("a2", [DM, DM], F32, kind="ExternalInput").ap()
    lng_t = nc.dram_tensor("lng_t", [128, DM], F32, kind="ExternalInput").ap()
    lnb_t = nc.dram_tensor("lnb_t", [128, DM], F32, kind="ExternalInput").ap()
    fbias_t = nc.dram_tensor("fbias_t", [128, DM], F32, kind="ExternalInput").ap()
    i_bf = nc.dram_tensor("i_bf", [128, 128], BF16, kind="ExternalInput").ap()
    irep = nc.dram_tensor("irep", [128, 512], BF16, kind="ExternalInput").ap()
    alpha_t = nc.dram_tensor("alpha_t", [128, 8], F32, kind="ExternalInput").ap()
    gam_t = nc.dram_tensor("gam_t", [128, 8], F32, kind="ExternalInput").ap()
    igam_t = nc.dram_tensor("igam_t", [128, 8], F32, kind="ExternalInput").ap()
    eps_c = nc.dram_tensor("eps_c", [128, 1], F32, kind="ExternalInput").ap()

    out_seg = nc.dram_tensor("out_seg", [seg_len, DM], F32, kind="ExternalOutput").ap()

    st_ypre = nc.dram_tensor("st_ypre", [seg_len, DI], BF16).ap()
    st_sz = nc.dram_tensor("st_sz", [seg_len, DI], BF16).ap()
    st_c = nc.dram_tensor("st_c", [nch, 128, 128], BF16).ap()
    st_g = nc.dram_tensor("st_g", [nch, 128, 32], BF16).ap()
    expa_scr = nc.dram_tensor("expa_scr", [seg_len // Q, 32 * Q], BF16).ap()
    cd_scr = nc.dram_tensor("cd_scr", [nch, 32], F32).ap()
    wout_bfh = nc.dram_tensor("wout_bfh", [DI, DM], BF16).ap()
    wf_bfh = nc.dram_tensor("wf_bfh", [DM, DM], BF16).ap()
    a1_bfh = nc.dram_tensor("a1_bfh", [DM, DM], BF16).ap()
    a2_bfh = nc.dram_tensor("a2_bfh", [DM, DM], BF16).ap()
    exch_in = nc.dram_tensor("exch_in", [129, DI], F32).ap()
    gath = nc.dram_tensor("gath", [NCORE * 129, DI], F32).ap()

    dbg = {}
    if debug:
        dbg["d_ypre"] = nc.dram_tensor("d_ypre", [seg_len, DI], F32, kind="ExternalOutput").ap()
        dbg["d_sz"] = nc.dram_tensor("d_sz", [seg_len, DI], F32, kind="ExternalOutput").ap()
        dbg["d_r"] = nc.dram_tensor("d_r", [129, DI], F32, kind="ExternalOutput").ap()
        dbg["d_R"] = nc.dram_tensor("d_R", [128, DI], F32, kind="ExternalOutput").ap()
        dbg["d_dt"] = nc.dram_tensor("d_dt", [32, SUP], F32, kind="ExternalOutput").ap()
        dbg["d_yg"] = nc.dram_tensor("d_yg", [SUP, DI], F32, kind="ExternalOutput").ap()
        dbg["d_h"] = nc.dram_tensor("d_h", [SUP, DM], F32, kind="ExternalOutput").ap()
        dbg["d_f1"] = nc.dram_tensor("d_f1", [SUP, DM], F32, kind="ExternalOutput").ap()
        dbg["d_ynb"] = nc.dram_tensor("d_ynb", [SUP, DI], BF16, kind="ExternalOutput").ap()
        dbg["d_xn32"] = nc.dram_tensor("d_xn32", [SUP, DM], F32, kind="ExternalOutput").ap()

    P = Prog()

    with contextlib.ExitStack() as ST:
        def sb(name, p, f, dt):
            return ST.enter_context(nc.sbuf_tensor(name, [p, f], dt))

        x_t = sb("x_t", 128, DM, F32)
        bnr = sb("bnr", 128, 24, F32)
        bna = sb("bna", 128, 2, F32)
        col1 = sb("col1", 128, 1, F32)
        xn_tmb = sb("xn_tmb", 128, DM, BF16)
        xn_fm = sb("xn_fm", 128, KD * SUP, BF16)
        dtb = sb("dtb", 32, 1, F32)
        alog = sb("alog", 32, 1, F32)
        acol = sb("acol", 32, 1, F32)
        dti = sb("dti", 128, 32, F32)
        rmsw = sb("rmsw", 128, 16, F32)
        ibf = sb("ibf", 128, 128, BF16)
        irp = sb("irp", 128, 512, BF16)
        alct = sb("alct", 128, 8, F32)
        gamt = sb("gamt", 128, 8, F32)
        igamt = sb("igamt", 128, 8, F32)
        cwt = sb("cwt", 128, NBC * 4, F32)
        cbt = sb("cbt", 128, NBC, F32)
        sig = sb("sig", 32, 1, F32)
        sig2 = sb("sig2", 32, 1, F32)
        cep = sb("cep", 128, 1, F32)
        col2 = sb("col2", 128, 1, F32)

        pA = ST.enter_context(nc.psum_tensor("pA", [128, DI], F32))
        pB = ST.enter_context(nc.psum_tensor("pB", [128, DI], F32))

        sems = {}
        for s in SEMS:
            sems[s] = ST.enter_context(nc.semaphore("s_" + s))

        def bcast_col(t, n_inner):
            b = t[:]
            return AP(b.tensor, b.offset, [[b.ap[0][0], b.ap[0][1]], [0, n_inner]])

        def hb(t, off, h_cnt, p_cnt, pdim=128):
            b = t[:]
            return AP(b.tensor, b.offset + off, [[b.ap[0][0], pdim], [1, h_cnt], [0, p_cnt]])

        def r3(t, off, pdim, dims):
            b = t[:]
            return AP(b.tensor, b.offset + off, [[b.ap[0][0], pdim]] + [list(d) for d in dims])

        def hq(ap_):
            return ap_.rearrange("p (h q) -> p h q", h=NH)

        def dmaf(out, in_, **kw):
            return lambda e: e.dma_start(out=out, in_=in_, **kw)

        with contextlib.ExitStack() as S1:
            def sb1(name, p, f, dt):
                return ST.enter_context(nc.sbuf_tensor(name, [p, f], dt))

            def sb1p(name, p, f, dt):
                return S1.enter_context(nc.sbuf_tensor(name, [p, f], dt))

            w_in_bf = sb1p("w_in_bf", 128, KD * 4384, BF16)
            wcst = sb1p("wcst", 128, 512, F32)
            wcbf = sb1p("wcbf", 128, 512, BF16)
            S0 = contextlib.ExitStack()
            wstage = S0.enter_context(nc.sbuf_tensor("wstage", [128, 4384], F32))
            wstage2 = S0.enter_context(nc.sbuf_tensor("wstage2", [128, 4384], F32))
            P.emit("gpsimd", dmaf(x_t[0:16, 0:DM], x_halo), inc="gp", step=16)
            halo_gp = P.v("gp")
            _emit_setup(P, nc, locals())
            S0.close()

            wrounds = []
            for k in range(16):
                for h in range(2):
                    wrounds.append((w_out, wout_bfh, k, h, True))
            for (wsrc, wdst) in ((wf, wf_bfh), (a1, a1_bfh), (a2, a2_bfh)):
                for k in range(KD):
                    for h in range(2):
                        wrounds.append((wsrc, wdst, k, h, False))
            wr_n = len(wrounds)
            wr_ld = [0]
            wr_cv = [0]
            wr_load_sc = {}
            wr_conv_ve = {}
            wr_store_gp = {}

            def emit_wround_load():
                i = wr_ld[0]
                if i >= wr_n:
                    return
                wr_ld[0] = i + 1
                src, dst, k, h, scl = wrounds[i]
                w = [("ve", wr_conv_ve[i - 1])] if i >= 1 else []
                P.emit("scalar", dmaf(
                    wcst[:], src[k * 128:(k + 1) * 128, h * 512:(h + 1) * 512]),
                    waits=w, inc="sc", step=16)
                wr_load_sc[i] = P.v("sc")

            def emit_wround_conv():
                i = wr_cv[0]
                if i >= wr_n or i >= wr_ld[0]:
                    return
                wr_cv[0] = i + 1
                src, dst, k, h, scl = wrounds[i]
                w = [("sc", wr_load_sc[i])]
                if i >= 1:
                    w.append(("gp", wr_store_gp[i - 1]))
                P.emit("vector", lambda e, k=k, scl=scl: e.tensor_scalar(
                    out=wcbf[:], in0=wcst[:],
                    scalar1=(rmsw[:, k:k + 1] if scl else 1.0),
                    scalar2=None, op0=ALU.mult),
                    waits=w, inc="ve")
                wr_conv_ve[i] = P.v("ve")
                P.emit("gpsimd", dmaf(
                    dst[k * 128:(k + 1) * 128, h * 512:(h + 1) * 512], wcbf[:]),
                    waits=[("ve", wr_conv_ve[i])], inc="gp", step=16)
                wr_store_gp[i] = P.v("gp")
            cbuf = sb1p("cbuf", 128, 2 * 520, F32)
            tail = sb1p("tail", 128, NBC * 4, F32)
            ca_v = sb1p("ca_v", 128, 2 * 512, F32)
            ca_p = sb1p("ca_p", 128, 3 * 512, F32)
            xbc_bf = sb1p("xbc_bf", 128, 4 * 512, BF16)
            t_dt = sb1p("t_dt", 32, SUP, F32)
            t_cs = t_dt
            t_u1 = sb1p("t_u1", 32, SUP, F32)
            t_u2 = sb1p("t_u2", 32, SUP, F32)
            t_A = t_u2
            w_bf = sb1p("w_bf", 32, SUP, BF16)
            t_g = sb1p("t_g", 32, SUP, F32)
            t_gsb = sb1p("t_gsb", 32, SUP, BF16)
            t_gcb = sb1p("t_gcb", 32, SUP, BF16)
            t_dsb = sb1p("t_dsb", 32, SUP, BF16)
            t_dtb = sb1p("t_dtb", 32, SUP, BF16)
            cd4 = sb1p("cd4", 32, CPS, F32)
            xtm = sb1p("xtm", 128, CPS * DI, BF16)
            btm = sb1p("btm", 128, CPS * 128, BF16)
            dttm = sb1p("dttm", 128, CPS * 32, BF16)
            gctm = sb1p("gctm", 128, CPS * 32, BF16)
            gstm = sb1p("gstm", 128, CPS * 32, BF16)
            dstm = sb1p("dstm", 128, CPS * 32, BF16)
            eab = sb1p("eab", 128, 4 * 1024, BF16)
            t_eab_b = sb1p("t_eab_b", 32, SUP, BF16)
            e_t = sb1p("e_t", 128, NH * Q, BF16)
            m_t = sb1p("m_t", 128, DI, BF16)
            g_sb = sb1p("g_sb", 128, 128, BF16)
            xt_t = sb1p("xt_t", 128, DI, BF16)
            xd_t = sb1p("xd_t", 128, DI, BF16)
            xtd = sb1p("xtd", 128, DI, BF16)
            r_bf = sb1p("r_bf", 128, DI, BF16)
            r_t = sb1p("r_t", 128, DI, F32)
            dcb = sb1p("dcb", 128, CPS * 32, F32)
            ypre = sb1p("ypre", 128, DI, F32)
            sz_t = sb1p("sz_t", 128, DI, F32)
            yp_b = sb1p("yp_b", 128, DI, BF16)

            P.emit("vector", lambda e: e.memset(r_t[:], 0.0), inc="ve")
            P.emit("vector", lambda e: e.memset(sig[:], 1.0), inc="ve")
            P.emit("vector", lambda e: e.memset(tail[:], 0.0), inc="ve")

            P.emit("vector", lambda e: e.bn_stats(out=bnr[0:16, 0:6], in_=x_t[0:16, 0:512]),
                   waits=[("gp", halo_gp)], inc="ve")
            P.emit("vector", lambda e: e.bn_stats(out=bnr[0:16, 6:12], in_=x_t[0:16, 512:1024]),
                   inc="ve")
            P.emit("vector", lambda e: e.bn_aggr(out=bna[0:16, :], in_=bnr[0:16, 0:12]),
                   waits=[("ve", P.v("ve"))], inc="ve")
            P.emit("scalar", lambda e: e.activation(out=col1[0:16, :], in_=bna[0:16, 1:2],
                                                    func=AF.Ln, bias=cep[0:16, :]),
                   waits=[("ve", P.v("ve"))], inc="se")
            P.emit("scalar", lambda e: e.activation(out=col2[0:16, :], in_=col1[0:16, :],
                                                    func=AF.Exp, scale=-0.5), inc="se")
            P.emit("vector", lambda e: e.scalar_tensor_tensor(
                out=xn_tmb[0:16, :], in0=x_t[0:16, 0:DM], scalar=bna[0:16, 0:1],
                in1=AP(col2[:].tensor, col2[:].offset, [[col2[:].ap[0][0], 16], [0, DM]]),
                op0=ALU.subtract, op1=ALU.mult),
                waits=[("se", P.v("se"))], inc="ve")
            P.emit("sync", lambda e: e.dma_start_transpose(
                r3(xn_fm, 0, 128, [(16, KD), (1, 16)]), xn_tmb[0:16, :]),
                waits=[("ve", P.v("ve"))], inc="sy", step=16)
            hsy = P.v("sy")
            for f in range(NBC):
                for k in range(KD):
                    w0 = ([("sy", hsy), ("ve", P.v("ve")), ("se", P.v("se"))]
                          if k == 0 else [])
                    P.emit("tensor", lambda e, f=f, k=k: e.matmul(
                        pB[:, 0:16],
                        w_in_bf[:, k * 4384 + 2048 + f * 128:k * 4384 + 2048 + (f + 1) * 128],
                        xn_fm[:, k * 16:(k + 1) * 16],
                        start=(k == 0), stop=(k == KD - 1)),
                        waits=w0, inc=("te" if k == KD - 1 else None))
                P.emit("scalar", lambda e, f=f: e.copy(out=tail[:, f * 4:f * 4 + 3],
                                                       in_=pB[:, 13:16]),
                       waits=[("te", P.v("te"))], inc="se")

            ln_sy = {}
            xnfm_free = [P.v("te")]

            def emit_ln_chunk(s2, c2):
                t0 = s2 * SUP + c2 * Q
                P.emit("scalar", dmaf(x_t[:], x_seg[t0:t0 + 128, :]),
                       waits=[("ve", P.v("ve"))], inc="sc", step=16)
                xs_ = P.v("sc")
                P.emit("vector", lambda e: e.bn_stats(
                    out=bnr[:, 0:6], in_=x_t[:, 0:512]),
                    waits=[("sc", xs_), ("se", P.v("se"))], inc="ve")
                P.emit("vector", lambda e: e.bn_stats(
                    out=bnr[:, 6:12], in_=x_t[:, 512:1024]),
                    inc="ve")
                P.emit("vector", lambda e: e.bn_aggr(out=bna[:], in_=bnr[:, 0:12]),
                       waits=[("ve", P.v("ve"))], inc="ve")
                P.emit("scalar", lambda e: e.activation(out=col1[:], in_=bna[:, 1:2],
                                                        func=AF.Ln, bias=cep[:]),
                       waits=[("ve", P.v("ve"))], inc="se")
                P.emit("scalar", lambda e: e.activation(out=col2[:], in_=col1[:],
                                                        func=AF.Exp, scale=-0.5),
                       inc="se")
                P.emit("vector", lambda e: e.scalar_tensor_tensor(
                    out=xn_tmb[:], in0=x_t[:],
                    scalar=bna[:, 0:1], in1=bcast_col(col2, DM),
                    op0=ALU.subtract, op1=ALU.mult),
                    waits=[("se", P.v("se")), ("sy", P.v("sy"))], inc="ve")
                P.emit("sync", lambda e, c2=c2: e.dma_start_transpose(
                    r3(xn_fm, c2 * 128, 128, [(SUP, KD), (1, 128)]), xn_tmb[:]),
                    waits=[("ve", P.v("ve")), ("te", xnfm_free[0])],
                    inc="sy", step=16)
                ln_sy[s2] = P.v("sy")

            for s in range(nsup):
                sup_off = s * SUP
                if s == 0:
                    for c in range(CPS):
                        emit_ln_chunk(0, c)
                xn_sy = ln_sy[s]

                tsy = {}
                copy_se = {}
                conv_po = {}
                for f in range(NBC):
                    cslot = f % 2
                    half = f % 2
                    xoff = (f % 2) * 512 if f < 16 else (1024 if f == 16 else 1536)
                    for k in range(KD):
                        if k != 0:
                            w0 = []
                        elif f < 2:
                            w0 = [("sy", xn_sy), ("se", P.v("se")),
                                  ("ve", P.v("ve"))]
                        else:
                            w0 = [("se", copy_se[f - 2])]
                        P.emit("tensor", lambda e, f=f, k=k, half=half: e.matmul(
                            pB[:, half * 512:(half + 1) * 512],
                            w_in_bf[:, k * 4384 + 2048 + f * 128:k * 4384 + 2048 + (f + 1) * 128],
                            xn_fm[:, k * SUP:(k + 1) * SUP],
                            start=(k == 0), stop=(k == KD - 1)),
                            waits=w0, inc=("te" if k == KD - 1 else None))
                    te_mm = P.v("te")
                    P.emit("vector", lambda e, cslot=cslot, f=f: e.tensor_copy(
                        cbuf[:, cslot * 520:cslot * 520 + 3], tail[:, f * 4:f * 4 + 3]),
                        waits=[("po", P.v("po")), ("se", P.v("se"))], inc="ve")
                    cp_w = [("te", te_mm), ("ve", P.v("ve"))]
                    if f >= 2 and (f % 2) == 1:
                        cp_w.append(("po", conv_po[f - 2]))
                    P.emit("scalar", lambda e, cslot=cslot, half=half: e.copy(
                        out=cbuf[:, cslot * 520 + 3:cslot * 520 + 515],
                        in_=pB[:, half * 512:(half + 1) * 512]),
                        waits=cp_w, inc="se")
                    ev_se = P.v("se")
                    copy_se[f] = ev_se
                    P.emit("vector", lambda e, cslot=cslot, f=f: e.tensor_copy(
                        tail[:, f * 4:f * 4 + 3],
                        cbuf[:, cslot * 520 + 512:cslot * 520 + 515]),
                        waits=[("se", ev_se)], inc="ve")
                    ceng = "vector" if f % 2 == 0 else "gpsimd"
                    cinc = "ve" if ceng == "vector" else "po"
                    cab = ca_v if f % 2 == 0 else ca_p
                    off = cslot * 520
                    if ceng == "vector":
                        P.emit(ceng, lambda e, off=off, f=f, cab=cab: e.tensor_scalar(
                            out=cab[:, 0:512], in0=cbuf[:, off:off + 512],
                            scalar1=cwt[:, f * 4:f * 4 + 1], scalar2=None, op0=ALU.mult),
                            waits=[("se", P.v("se"))], inc=cinc)
                        for j in (1, 2, 3):
                            src2 = 0 if j % 2 == 1 else 512
                            dst2 = 512 - src2
                            P.emit(ceng, lambda e, off=off, f=f, j=j, src2=src2, dst2=dst2,
                                   cab=cab: e.scalar_tensor_tensor(
                                       out=cab[:, dst2:dst2 + 512],
                                       in0=cbuf[:, off + j:off + j + 512],
                                       scalar=cwt[:, f * 4 + j:f * 4 + j + 1],
                                       in1=cab[:, src2:src2 + 512],
                                       op0=ALU.mult, op1=ALU.add), inc=cinc)
                    else:
                        def wcol(f, j):
                            b = cwt[:]
                            return AP(b.tensor, b.offset + f * 4 + j,
                                      [[b.ap[0][0], 128], [0, 512]])
                        P.emit(ceng, lambda e, off=off, f=f, cab=cab: e.tensor_tensor(
                            out=cab[:, 0:512], in0=cbuf[:, off:off + 512],
                            in1=wcol(f, 0), op=ALU.mult),
                            waits=[("se", P.v("se"))], inc=cinc)
                        plan = [(1, 512, 0, 512, 1024), (2, 0, 1024, 0, 512),
                                (3, 0, 512, 0, 1024)]
                        for (j, mdst, a0, a1_, adst) in plan:
                            P.emit(ceng, lambda e, off=off, f=f, j=j, mdst=mdst, cab=cab:
                                   e.tensor_tensor(
                                       out=cab[:, mdst:mdst + 512],
                                       in0=cbuf[:, off + j:off + j + 512],
                                       in1=wcol(f, j), op=ALU.mult), inc=cinc)
                            P.emit(ceng, lambda e, cab=cab, a0=a0, a1_=a1_, adst=adst:
                                   e.tensor_tensor(
                                       out=cab[:, adst:adst + 512], in0=cab[:, a0:a0 + 512],
                                       in1=cab[:, a1_:a1_ + 512], op=ALU.add), inc=cinc)
                    conv_po[f] = P.v("po")
                    sil_src = (512, 1024) if ceng == "vector" else (1024, 1536)
                    silw = [(cinc, P.v(cinc))]
                    if f >= 2 and f < 16:
                        silw.append(("sy", tsy[f - 2]))
                    if f >= 16:
                        silw.append(("te", P.v("te")))
                        silw.append(("gp", P.v("gp")))
                    P.emit("scalar", lambda e, f=f, cab=cab, sil_src=sil_src, xoff=xoff:
                           e.activation(
                               out=xbc_bf[:, xoff:xoff + 512],
                               in_=cab[:, sil_src[0]:sil_src[1]],
                               func=AF.Silu, bias=cbt[:, f:f + 1]),
                           waits=silw, inc="se")
                    sil_se = P.v("se")
                    if f < 16:
                        P.emit("sync", lambda e, f=f, xoff=xoff: e.dma_start_transpose(
                            r3(xtm, f * 128, 128, [(DI, CPS), (1, 128)]),
                            xbc_bf[:, xoff:xoff + 512]),
                            waits=[("se", sil_se)], inc="sy", step=16)
                        tsy[f] = P.v("sy")
                    elif f == 16:
                        P.emit("sync", lambda e: e.dma_start_transpose(
                            r3(btm, 0, 128, [(128, CPS), (1, 128)]),
                            xbc_bf[:, 1024:1536]),
                            waits=[("se", sil_se)], inc="sy", step=16)
                    else:
                        P.emit("gpsimd", lambda e, s=s: e.dma_start(
                            out=AP(st_c.tensor, st_c.offset + s * CPS * 128 * 128,
                                   [[128, 128], [128 * 128, CPS], [1, 128]]),
                            in_=r3(xbc_bf, 1536, 128, [(128, CPS), (1, 128)])),
                            waits=[("se", sil_se)], inc="gp", step=16)
                tm_sy = P.v("sy")
                conv_se = P.v("se")

                for c in range(CPS):
                    ch = s * CPS + c
                    for q in range(4):
                        for k in range(KD):
                            w0 = ([("ve", P.v("ve")), ("se", P.v("se"))]
                                  if (q == 0 and k == 0) else [])
                            P.emit("tensor", lambda e, q=q, k=k, c=c: e.matmul(
                                pA[:, q * 512:(q + 1) * 512],
                                xn_fm[:, k * SUP + c * 128:k * SUP + (c + 1) * 128],
                                w_in_bf[:, k * 4384 + q * 512:k * 4384 + (q + 1) * 512],
                                start=(k == 0), stop=(k == KD - 1)),
                                waits=w0,
                                inc=("te" if (k == KD - 1 and q == 3) else None))
                    z_te = P.v("te")
                    for q in range(4):
                        P.emit("scalar", lambda e, q=q: e.activation(
                            out=yp_b[:, q * 512:(q + 1) * 512],
                            in_=pA[:, q * 512:(q + 1) * 512], func=AF.Silu),
                            waits=([("te", z_te), ("gp", P.v("gp"))] if q == 0 else []),
                            inc="se")
                    z_se = P.v("se")
                    P.emit("gpsimd", lambda e, ch=ch: e.dma_start(
                        out=st_sz[ch * 128:(ch + 1) * 128, :], in_=yp_b[:]),
                        waits=[("se", z_se)], inc="gp", step=16)
                zphase_se = P.v("se")

                for k in range(KD):
                    w0 = [("se", conv_se), ("ve", P.v("ve"))] if k == 0 else []
                    P.emit("tensor", lambda e, k=k: e.matmul(
                        pB[0:32, 1024:1536],
                        w_in_bf[:, k * 4384 + 4352:k * 4384 + 4384],
                        xn_fm[:, k * SUP:(k + 1) * SUP],
                        start=(k == 0), stop=(k == KD - 1)),
                        waits=w0, inc=("te" if k == KD - 1 else None))
                dt_te = P.v("te")
                xnfm_free[0] = dt_te
                P.emit("vector", lambda e: e.tensor_scalar(
                    out=t_u1[:], in0=pB[0:32, 1024:1536], scalar1=dtb[:],
                    scalar2=0.0, op0=ALU.add, op1=ALU.max),
                    waits=[("te", dt_te)], inc="ve")
                P.emit("vector", lambda e: e.tensor_scalar(
                    out=t_u2[:], in0=pB[0:32, 1024:1536], scalar1=dtb[:],
                    scalar2=None, op0=ALU.add), inc="ve")
                P.emit("vector", lambda e: e.scalar_tensor_tensor(
                    out=t_g[:], in0=t_u1[:], scalar=-2.0, in1=t_u2[:],
                    op0=ALU.mult, op1=ALU.add), inc="ve")
                P.emit("scalar", lambda e: e.activation(out=t_u2[:], in_=t_g[:],
                                                        func=AF.Exp),
                       waits=[("ve", P.v("ve"))], inc="se")
                P.emit("scalar", lambda e: e.activation(out=t_g[:], in_=t_u2[:],
                                                        func=AF.Ln, bias=1.0), inc="se")
                P.emit("vector", lambda e: e.tensor_tensor(out=t_dt[:], in0=t_u1[:],
                                                           in1=t_g[:], op=ALU.add),
                       waits=[("se", P.v("se"))], inc="ve")
                P.emit("vector", lambda e: e.tensor_scalar(
                    out=t_A[:], in0=t_dt[:], scalar1=acol[:], scalar2=None,
                    op0=ALU.mult), inc="ve")
                P.emit("vector", lambda e: e.tensor_scalar(
                    out=t_dtb[:], in0=t_dt[:], scalar1=1.0, scalar2=None, op0=ALU.mult),
                    inc="ve")
                for c in range(CPS):
                    P.emit("vector", lambda e, c=c: e.tensor_tensor_scan(
                        out=t_cs[:, c * 128:(c + 1) * 128],
                        data0=t_A[:, c * 128:(c + 1) * 128],
                        data1=t_A[:, c * 128:(c + 1) * 128], initial=0.0,
                        op0=ALU.add, op1=ALU.bypass), inc="ve")
                cs_ve = P.v("ve")
                P.emit("scalar", lambda e: e.activation(out=t_eab_b[:], in_=t_A[:],
                                                        func=AF.Exp),
                       waits=[("ve", cs_ve), ("gp", P.v("gp"))], inc="se")
                P.emit("vector", lambda e: e.memset(
                    AP(t_eab_b[:].tensor, t_eab_b[:].offset,
                       [[t_eab_b[:].ap[0][0], 32], [128, CPS]]),
                    0.0),
                    waits=[("se", P.v("se"))], inc="ve")
                P.emit("gpsimd", lambda e, s=s: e.dma_start(
                    out=AP(expa_scr.tensor, expa_scr.offset + s * CPS * 32 * Q,
                           [[Q, 32], [32 * Q, CPS], [1, Q]]),
                    in_=t_eab_b[:]),
                    waits=[("ve", P.v("ve"))], inc="gp", step=16)
                expa_gp_s = P.v("gp")
                P.emit("scalar", lambda e: e.activation(out=t_g[:], in_=t_cs[:], func=AF.Exp),
                       waits=[("ve", cs_ve)], inc="se")
                g_se = P.v("se")
                P.emit("vector", lambda e: e.tensor_scalar(
                    out=t_gcb[:], in0=t_g[:], scalar1=1.0, scalar2=None, op0=ALU.mult),
                    waits=[("se", g_se), ("sy", P.v("sy"))], inc="ve")
                for c in range(CPS):
                    ch0 = s * CPS + c
                    scur = sig if ch0 % 2 == 0 else sig2
                    snxt = sig2 if ch0 % 2 == 0 else sig
                    P.emit("vector", lambda e, c=c, scur=scur: e.tensor_scalar(
                        out=t_gsb[:, c * 128:(c + 1) * 128],
                        in0=t_g[:, c * 128:(c + 1) * 128],
                        scalar1=scur[:], scalar2=None, op0=ALU.mult), inc="ve")
                    P.emit("scalar", lambda e, c=c: e.activation(
                        out=cd4[:, c:c + 1], in_=t_cs[:, c * 128 + 127:c * 128 + 128],
                        func=AF.Exp),
                        waits=[("ve", P.v("ve")), ("gp", P.v("gp"))], inc="se")
                    P.emit("vector", lambda e, c=c, scur=scur, snxt=snxt: e.tensor_scalar(
                        out=snxt[:], in0=scur[:], scalar1=cd4[:, c:c + 1], scalar2=None,
                        op0=ALU.mult),
                        waits=[("se", P.v("se"))], inc="ve")
                P.emit("vector", lambda e: e.tensor_tensor(
                    out=t_u1[:], in0=t_cs[:],
                    in1=AP(t_cs[:].tensor, t_cs[:].offset + 127,
                           [[t_cs[:].ap[0][0], 32], [128, CPS], [0, 128]]),
                    op=ALU.subtract), inc="ve")
                P.emit("scalar", lambda e: e.activation(out=t_dsb[:], in_=t_u1[:],
                                                        func=AF.Exp, scale=-1.0),
                       waits=[("ve", P.v("ve"))], inc="se")
                P.emit("vector", lambda e: e.tensor_tensor(
                    out=w_bf[:], in0=t_dtb[:], in1=t_dsb[:], op=ALU.mult),
                    waits=[("se", P.v("se"))], inc="ve")
                dt_ve = P.v("ve")
                dt_se = P.v("se")
                P.emit("gpsimd", lambda e, s=s: e.dma_start(
                    out=AP(cd_scr.tensor, cd_scr.offset + s * CPS * 32,
                           [[1, 32], [32, CPS]]),
                    in_=cd4[:], allow_slow_non_contiguous=True),
                    waits=[("ve", dt_ve), ("se", dt_se)], inc="gp", step=16)
                P.emit("scalar", lambda e, s=s: e.dma_start(
                    out=dcb[:], in_=AP(cd_scr.tensor, cd_scr.offset + s * CPS * 32,
                                       [[0, 128], [1, CPS * 32]])),
                    waits=[("gp", P.v("gp")), ("po", P.v("po"))], inc="dc", step=16)
                dc_s = P.v("dc")
                P.emit("sync", lambda e: e.dma_start_transpose(
                    r3(dttm, 0, 128, [(32, CPS), (1, 32)]), t_dtb[:]),
                    waits=[("ve", dt_ve), ("po", P.v("po")), ("te", P.v("te"))],
                    inc="sy", step=16)
                P.emit("sync", lambda e: e.dma_start_transpose(
                    r3(gctm, 0, 128, [(32, CPS), (1, 32)]), t_gcb[:]),
                    inc="sy", step=16)
                P.emit("sync", lambda e: e.dma_start_transpose(
                    r3(gstm, 0, 128, [(32, CPS), (1, 32)]), t_gsb[:]),
                    inc="sy", step=16)
                P.emit("sync", lambda e: e.dma_start_transpose(
                    r3(dstm, 0, 128, [(32, CPS), (1, 32)]), w_bf[:]),
                    waits=[("ve", dt_ve)], inc="sy", step=16)
                P.emit("gpsimd", lambda e, s=s: e.dma_start(
                    out=AP(st_g.tensor, st_g.offset + s * CPS * 128 * 32,
                           [[32, 128], [128 * 32, CPS], [1, 32]]),
                    in_=r3(gstm, 0, 128, [(32, CPS), (1, 32)])),
                    waits=[("sy", P.v("sy"))], inc="gp", step=16)
                if s + 1 < nsup:
                    emit_ln_chunk(s + 1, 0)

                def emit_G(c, wv):
                    P.emit("tensor", lambda e, c=c: e.matmul(
                        pA[:, 1024:1152],
                        xbc_bf[:, 1024 + c * 128:1024 + (c + 1) * 128],
                        xbc_bf[:, 1536 + c * 128:1536 + (c + 1) * 128],
                        start=True, stop=True),
                        waits=wv, inc="te")
                    P.emit("vector", lambda e: e.tensor_copy(g_sb[:], pA[:, 1024:1152]),
                           waits=[("te", P.v("te"))], inc="ve")
                    return P.v("ve")

                def emit_eab_loads(ch2):
                    s2 = ch2 // CPS
                    P.emit("scalar", lambda e, ch2=ch2: e.dma_start(
                        out=eab[:, 0:4096],
                        in_=AP(expa_scr.tensor, expa_scr.offset + ch2 * 32 * Q,
                               [[0, 128], [1, 32 * Q]])),
                        waits=list(eab_cons[0]) + [("gp", expa_gp[s2])],
                        inc="e0", step=16)
                    eab_load_e0[ch2] = P.v("e0")

                def emit_scans(ch2):
                    for ll in range(4):
                        for i4 in range(2):
                            i = ll * 2 + i4
                            w0 = []
                            if i == 0:
                                w0 = [("e0", eab_load_e0[ch2]),
                                      ("ve", mtil_done[0])]
                            P.emit("vector", lambda e, i=i, i4=i4, ll=ll:
                                   e.tensor_tensor_scan(
                                       out=e_t[:, i * 512:(i + 1) * 512],
                                       data0=eab[:, ll * 1024 + i4 * 512:ll * 1024 + (i4 + 1) * 512],
                                       data1=irp[:], initial=0.0,
                                       op0=ALU.mult, op1=ALU.add),
                                   waits=w0, inc="ve")
                    eab_cons[0] = [("ve", P.v("ve"))]
                    scan_cnt[0] = (P.v("ve"), 0)
                    return P.v("ve")

                if s == 0:
                    eab_cons = [[]]
                    eab_load_e0 = {}
                    expa_gp = {}
                    mtil_done = [0]
                    scan_cnt = [(0, 0)]
                    prev_ev2 = 0
                    prev_ev1 = 0
                    prev_add_po = 0
                    prev_s_te = 0
                    prev_yo_te = 0
                    prev_r2 = 0
                expa_gp[s] = expa_gp_s

                emit_eab_loads(s * CPS)
                gsb_ve = emit_G(0, [("ve", prev_ev2), ("se", zphase_se),
                                    ("te", P.v("te")), ("ve", P.v("ve"))])
                scan_done = emit_scans(s * CPS)
                P.emit("vector", lambda e: e.tensor_tensor(
                    out=hq(xt_t[:]), in0=hq(xtm[:, 0:DI]),
                    in1=r3(dttm, 0, 128, [(1, NH), (0, HP)]), op=ALU.mult),
                    waits=[("sy", tm_sy), ("te", P.v("te"))], inc="ve")
                xtil_done = ("ve", P.v("ve"))
                P.emit("gpsimd", lambda e: e.tensor_tensor(
                    out=hq(xd_t[:]), in0=hq(xtm[:, 0:DI]),
                    in1=r3(dti, 0, 128, [(1, NH), (0, HP)]), op=ALU.mult),
                    waits=[("sy", tm_sy), ("te", P.v("te"))], inc="po")
                xdt_done = ("po", P.v("po"))

                for c in range(CPS):
                    ch = s * CPS + c
                    for q in range(4):
                        w0 = ([("se", zphase_se), xdt_done, ("ve", P.v("ve"))]
                              if q == 0 else [])
                        P.emit("tensor", lambda e, q=q: e.matmul(
                            pA[:, q * 512:(q + 1) * 512], ibf[:],
                            xd_t[:, q * 512:(q + 1) * 512], start=True, stop=False),
                            waits=w0, inc=("te" if q == 3 else None))
                    imm_te = P.v("te")
                    for mh in range(2):
                        P.emit("vector", lambda e, mh=mh: e.tensor_tensor(
                            out=m_t[:].rearrange("p (h q) -> p h q", h=NH // 2),
                            in0=e_t[:, mh * DI:(mh + 1) * DI]
                                .rearrange("p (h q) -> p h q", h=NH // 2),
                            in1=r3(g_sb, 0, 128, [(0, NH // 2), (1, 128)]),
                            op=ALU.mult),
                            waits=([("ve", scan_cnt[0][0])] if mh == 0 else
                                   [("te", P.v("te"))]), inc="ve")
                        mtil_v = P.v("ve")
                        if mh == 1:
                            mtil_done[0] = mtil_v
                        for h2 in range(NH // 2):
                            h = mh * 16 + h2
                            w0 = ([("ve", mtil_v), xtil_done] if h2 == 0 else [])
                            P.emit("tensor", lambda e, h=h, h2=h2: e.matmul(
                                pA[:, h * 64:(h + 1) * 64],
                                m_t[:, h2 * 128:(h2 + 1) * 128],
                                xt_t[:, h * 64:(h + 1) * 64], start=False,
                                stop=(h % 8 == 7)),
                                waits=w0,
                                inc=("te" if h2 == NH // 2 - 1 else None))
                    yd_te = P.v("te")

                    P.emit("gpsimd", lambda e, c=c: e.tensor_tensor(
                        out=hq(xtd[:]), in0=hq(xtm[:, c * DI:(c + 1) * DI]),
                        in1=r3(dstm, c * 32, 128, [(1, NH), (0, HP)]),
                        op=ALU.mult),
                        waits=[("te", prev_s_te), ("sy", tm_sy)], inc="po")
                    xtd_done = P.v("po")
                    P.emit("vector", lambda e: e.tensor_copy(r_bf[:], r_t[:]),
                           waits=[("te", prev_yo_te)], inc="ve")
                    rbf_ve = P.v("ve")
                    P.emit("gpsimd", lambda e, c=c: e.tensor_tensor(
                        out=hq(sz_t[:]), in0=hq(r_t[:]),
                        in1=r3(dcb, c * 32, 128, [(1, NH), (0, HP)]),
                        op=ALU.mult),
                        waits=[("dc", dc_s), ("ve", prev_r2)],
                        inc="po")
                    r1_po = P.v("po")
                    for q in range(4):
                        w0 = ([("ve", max(prev_ev1, dt_ve) if c == 0 else prev_ev1),
                               ("po", xtd_done)] if q == 0 else [])
                        P.emit("tensor", lambda e, c=c, q=q: e.matmul(
                            pB[:, q * 512:(q + 1) * 512],
                            btm[:, c * 128:(c + 1) * 128],
                            xtd[:, q * 512:(q + 1) * 512], start=True, stop=True),
                            waits=w0, inc=("te" if q == 3 else None))
                    s_te = P.v("te")
                    P.emit("vector", lambda e: e.tensor_tensor(
                        out=r_t[:], in0=sz_t[:], in1=pB[:], op=ALU.add),
                        waits=[("po", r1_po), ("te", s_te)], inc="ve")
                    prev_r2 = P.v("ve")
                    for q in range(4):
                        w0 = [("ve", prev_r2)] if q == 0 else []
                        P.emit("tensor", lambda e, c=c, q=q: e.matmul(
                            pB[:, q * 512:(q + 1) * 512],
                            xbc_bf[:, 1536 + c * 128:1536 + (c + 1) * 128],
                            r_bf[:, q * 512:(q + 1) * 512], start=True, stop=True),
                            waits=w0, inc=("te" if q == 3 else None))
                    yo_te = P.v("te")
                    P.emit("vector", lambda e, c=c: e.tensor_tensor(
                        out=hq(ypre[:]), in0=hq(pB[:]),
                        in1=r3(gctm, c * 32, 128, [(1, NH), (0, HP)]),
                        op=ALU.mult),
                        waits=[("te", yo_te)], inc="ve")
                    ev1_ve = P.v("ve")
                    prev_ev1 = ev1_ve
                    if c < CPS - 1:
                        P.emit("gpsimd", lambda e, c=c: e.tensor_tensor(
                            out=hq(xt_t[:]), in0=hq(xtm[:, (c + 1) * DI:(c + 2) * DI]),
                            in1=r3(dttm, (c + 1) * 32, 128, [(1, NH), (0, HP)]),
                            op=ALU.mult),
                            waits=[("te", yd_te)], inc="po")
                        xtil_done = ("po", P.v("po"))
                        P.emit("gpsimd", lambda e, c=c: e.tensor_tensor(
                            out=hq(xd_t[:]), in0=hq(xtm[:, (c + 1) * DI:(c + 2) * DI]),
                            in1=r3(dti, 0, 128, [(1, NH), (0, HP)]),
                            op=ALU.mult),
                            waits=[("te", imm_te)], inc="po")
                        xdt_done = ("po", P.v("po"))
                    P.emit("vector", lambda e: e.tensor_tensor(
                        out=yp_b[:], in0=ypre[:], in1=pA[:], op=ALU.add),
                        waits=[("gp", P.v("gp")), ("te", yd_te)], inc="ve")
                    ev2_ve = P.v("ve")
                    prev_ev2 = ev2_ve
                    P.emit("gpsimd", lambda e, ch=ch: e.dma_start(
                        out=st_ypre[ch * 128:(ch + 1) * 128, :], in_=yp_b[:]),
                        waits=[("ve", ev2_ve)], inc="gp", step=16)
                    if c < CPS - 1:
                        gsb_ve = emit_G(c + 1, [("ve", ev2_ve)])
                    if c < CPS - 1:
                        emit_eab_loads(ch + 1)
                        scan_done = emit_scans(ch + 1)
                    for _wr in range(3 if ch < 16 else 2):
                        emit_wround_conv()
                        emit_wround_load()
                    if s + 1 < nsup and c < CPS - 1:
                        emit_ln_chunk(s + 1, c + 1)
                    prev_s_te = s_te
                    prev_yo_te = yo_te

            while wr_cv[0] < wr_n:
                emit_wround_conv()
                emit_wround_load()

            P.emit("gpsimd", dmaf(exch_in[0:128, :], r_t[:]),
                   waits=[("ve", P.v("ve"))], inc="gp", step=16)
            sig_fin = sig if nch % 2 == 0 else sig2
            P.emit("vector", lambda e: e.memset(ypre[0:32, :], 0.0),
                   waits=[("gp", P.v("gp"))], inc="ve")
            P.emit("gpsimd", lambda e: e.dma_start(
                out=exch_in[128:129, :], in_=ypre[0:1, :]),
                waits=[("ve", P.v("ve"))], inc="gp", step=16)
            P.emit("gpsimd", lambda e, sig_fin=sig_fin: e.dma_start(
                out=AP(exch_in.tensor, exch_in.offset + 128 * DI, [[1, 32], [1, 1]]),
                in_=sig_fin[:], allow_slow_non_contiguous=True),
                waits=[("gp", P.v("gp"))], inc="gp", step=16)
            P.emit("gpsimd", lambda e: e.collective_compute(
                "AllGather", ALU.bypass, replica_groups=[list(range(NCORE))],
                ins=[exch_in.opt()], outs=[gath.opt()]),
                waits=[("gp", P.v("gp"))], inc="cc")
            if debug:
                P.emit("gpsimd", dmaf(dbg["d_r"], exch_in),
                       waits=[("cc", P.v("cc"))], inc="gp", step=16)
                P.emit("gpsimd", dmaf(dbg["d_ypre"], st_ypre), inc="gp", step=16)
                P.emit("gpsimd", dmaf(dbg["d_sz"], st_sz), inc="gp", step=16)
                P.emit("gpsimd", lambda e: e.dma_start(out=dbg["d_dt"], in_=t_dt[:]),
                       inc="gp", step=16)

            barrier = P.barrier_vals()

        with contextlib.ExitStack() as S2:
            def sb2(name, p, f, dt):
                return S2.enter_context(nc.sbuf_tensor(name, [p, f], dt))

            woutb = sb2("woutb", 128, 16 * 1024, BF16)
            wffn = sb2("wffn", 128, KD * 1024, BF16)
            lngT = sb2("lngT", 128, DM, F32)
            lnbT = sb2("lnbT", 128, DM, F32)
            fbT = sb2("fbT", 128, DM, F32)
            R_t = sb2("R_t", 128, DI, F32)
            R_bf = sb2("R_bf", 128, DI, BF16)
            Dm_t = sb2("Dm_t", 128, 32, F32)
            Dm2_t = sb2("Dm2_t", 128, 32, F32)
            gr = sb2("gr", 128, DI, F32)
            yprd = sb2("yprd", 128, 3 * DI, BF16)
            szrd = sb2("szrd", 128, 3 * DI, BF16)
            crd = sb2("crd", 128, 3 * 128, BF16)
            grd = sb2("grd", 128, 3 * 32, BF16)
            y2 = sb2("y2", 128, DI, F32)
            ynb = sb2("ynb", 128, DI, BF16)
            ynfm = sb2("ynfm", 128, 2 * DI, BF16)
            bnrB = sb2("bnrB", 128, 12, F32)
            bnaB = sb2("bnaB", 128, 2, F32)
            col1B = sb2("col1B", 128, 1, F32)
            col2B = sb2("col2B", 128, 1, F32)
            muS = sb2("muS", 128, CPS, F32)
            sigS = sb2("sigS", 128, CPS, F32)
            xn32 = sb2("xn32", 128, 4 * DM, F32)
            h_t = sb2("h_t", 128, DM, F32)
            u1_t = sb2("u1_t", 128, DM, F32)
            f1_t = sb2("f1_t", 128, DM, F32)
            h1b = sb2("h1b", 128, KD * 512, BF16)
            h2b = sb2("h2b", 128, KD * 512, BF16)
            outst = sb2("outst", 128, 2 * DM, F32)

            first = {e: True for e in ENGS}

            def em(eng, fn, waits=(), inc=None, step=1):
                w = list(waits)
                if first[eng]:
                    w = barrier + w
                    first[eng] = False
                P.emit(eng, fn, w, inc, step)

            cc1 = P.v("cc")
            em("sync", dmaf(woutb[:].rearrange("p (a b) -> p a b", a=16),
                            wout_bfh.rearrange("(a p) b -> p a b", p=128)),
               inc="sy", step=16)
            em("sync", dmaf(lngT[:], lng_t), inc="sy", step=16)
            em("sync", dmaf(lnbT[:], lnb_t), inc="sy", step=16)
            em("sync", dmaf(fbT[:], fbias_t), inc="sy", step=16)
            w2_sy = P.v("sy")

            R2_t = y2
            em("vector", lambda e: e.memset(R_t[:], 0.0), inc="ve")
            for u in range(NCORE):
                em("scalar", lambda e, u=u: e.dma_start(
                    out=Dm_t[:], in_=AP(gath.tensor,
                                        gath.offset + (u * 129 + 128) * DI,
                                        [[0, 128], [1, 32]])),
                    waits=[("cc", cc1), ("ve", P.v("ve"))], inc="sc", step=16)
                em("vector", lambda e, u=u: e.tensor_scalar(
                    out=Dm2_t[:], in0=Dm_t[:], scalar1=gamt[:, u:u + 1],
                    scalar2=igamt[:, u:u + 1], op0=ALU.mult, op1=ALU.add),
                    waits=[("sc", P.v("sc"))], inc="ve")
                em("gpsimd", lambda e, u=u: e.dma_start(
                    out=gr[:], in_=gath[u * 129:u * 129 + 128, :]),
                    waits=[("cc", cc1), ("ve", P.v("ve"))], inc="gp", step=16)
                em("vector", lambda e: e.tensor_tensor(
                    out=hq(R2_t[:]), in0=hq(R_t[:]),
                    in1=r3(Dm2_t, 0, 128, [(1, NH), (0, HP)]), op=ALU.mult),
                    inc="ve")
                em("vector", lambda e, u=u: e.scalar_tensor_tensor(
                    out=R_t[:], in0=gr[:], scalar=alct[:, u:u + 1], in1=R2_t[:],
                    op0=ALU.mult, op1=ALU.add),
                    waits=[("gp", P.v("gp"))], inc="ve")
            em("vector", lambda e: e.tensor_scalar(
                out=R_bf[:], in0=R_t[:], scalar1=1.0, scalar2=None, op0=ALU.mult),
                inc="ve")
            R_ve = P.v("ve")
            if debug:
                em("gpsimd", dmaf(dbg["d_R"], R_t[:]), waits=[("ve", R_ve)],
                   inc="gp", step=16)

            yprd_free = {0: 0, 1: 0, 2: 0}
            szrd_free = {0: 0, 1: 0, 2: 0}
            crd_free = {0: 0, 1: 0, 2: 0}
            grd_free = {0: 0, 1: 0, 2: 0}
            ynfm_free = {0: 0, 1: 0}
            stash_sc = {}
            yn_sy_d = {}
            pa_free = [0]
            pb_lo_free = [0]
            pb_hi_free = [0]
            f1_free = [0]
            ln2_sy = {}
            ph2_xnfm_free = [P.v("te")]

            def emit_ln2_chunk(s2, c2):
                t0 = s2 * SUP + c2 * Q
                em("gpsimd", dmaf(x_t[:], x_seg[t0:t0 + 128, :]),
                   waits=[("ve", P.v("ve")), ("po", P.v("po"))], inc="gp", step=16)
                xs_ = P.v("gp")
                em("vector", lambda e: e.bn_stats(
                    out=bnr[:, 0:6], in_=x_t[:, 0:512]),
                    waits=[("gp", xs_), ("se", P.v("se"))], inc="ve")
                em("vector", lambda e: e.bn_stats(
                    out=bnr[:, 6:12], in_=x_t[:, 512:1024]),
                    inc="ve")
                em("vector", lambda e: e.bn_aggr(out=bna[:], in_=bnr[:, 0:12]),
                   waits=[("ve", P.v("ve"))], inc="ve")
                em("scalar", lambda e: e.activation(out=col1[:], in_=bna[:, 1:2],
                                                    func=AF.Ln, bias=cep[:]),
                   waits=[("ve", P.v("ve"))], inc="se")
                em("scalar", lambda e: e.activation(out=col2[:], in_=col1[:],
                                                    func=AF.Exp, scale=-0.5), inc="se")
                em("scalar", lambda e, c2=c2: e.activation(
                    out=sigS[:, c2:c2 + 1], in_=col1[:], func=AF.Exp, scale=0.5),
                    inc="se")
                em("vector", lambda e, c2=c2: e.tensor_copy(
                    muS[:, c2:c2 + 1], bna[:, 0:1]),
                    waits=[("se", P.v("se"))], inc="ve")
                em("vector", lambda e, c2=c2: e.scalar_tensor_tensor(
                    out=xn32[:, c2 * DM:(c2 + 1) * DM],
                    in0=x_t[:], scalar=bna[:, 0:1],
                    in1=bcast_col(col2, DM), op0=ALU.subtract, op1=ALU.mult),
                    inc="ve")
                em("vector", lambda e, c2=c2: e.tensor_scalar(
                    out=xn_tmb[:], in0=xn32[:, c2 * DM:(c2 + 1) * DM],
                    scalar1=1.0, scalar2=None, op0=ALU.mult),
                    waits=[("sy", P.v("sy"))], inc="ve")
                em("sync", lambda e, c2=c2: e.dma_start_transpose(
                    r3(xn_fm, c2 * 128, 128, [(SUP, KD), (1, 128)]), xn_tmb[:]),
                    waits=[("ve", P.v("ve")), ("te", ph2_xnfm_free[0])],
                    inc="sy", step=16)
                ln2_sy[s2] = P.v("sy")

            for s in range(nsup):
                sup_off = s * SUP
                if s == 0:
                    for c in range(CPS):
                        emit_ln2_chunk(0, c)
                xn2_sy = ln2_sy[s]

                em("gpsimd", dmaf(wffn[:].rearrange("p (a b) -> p a b", a=KD),
                                  wf_bfh.rearrange("(a p) b -> p a b", p=128)),
                   waits=[("te", P.v("te"))], inc="gp", step=16)
                wl_gp = P.v("gp")
                for o in range(KD):
                    for k in range(KD):
                        w0 = ([("sy", xn2_sy), ("gp", wl_gp), ("ve", P.v("ve")),
                               ("se", P.v("se"))]
                              if (o == 0 and k == 0) else
                              ([("se", P.v("se"))] if k == 0 else []))
                        em("tensor", lambda e, o=o, k=k: e.matmul(
                            pB[:, 1024:1536],
                            wffn[:, k * 1024 + o * 128:k * 1024 + (o + 1) * 128],
                            xn_fm[:, k * SUP:(k + 1) * SUP],
                            start=(k == 0), stop=(k == KD - 1)),
                            waits=w0, inc=("te" if k == KD - 1 else None))
                    em("scalar", lambda e, o=o: e.copy(
                        out=h1b[:, o * 512:(o + 1) * 512], in_=pB[:, 1024:1536]),
                        waits=[("te", P.v("te"))], inc="se")
                h1_se = P.v("se")
                em("gpsimd", dmaf(wffn[:].rearrange("p (a b) -> p a b", a=KD),
                                  a1_bfh.rearrange("(a p) b -> p a b", p=128)),
                   waits=[("te", P.v("te"))], inc="gp", step=16)
                wl_gp = P.v("gp")
                for o in range(KD):
                    for k in range(KD):
                        w0 = ([("se", h1_se), ("gp", wl_gp)] if (o == 0 and k == 0)
                              else ([("se", P.v("se"))] if k == 0 else []))
                        em("tensor", lambda e, o=o, k=k: e.matmul(
                            pB[:, 1024:1536],
                            wffn[:, k * 1024 + o * 128:k * 1024 + (o + 1) * 128],
                            h1b[:, k * 512:(k + 1) * 512],
                            start=(k == 0), stop=(k == KD - 1)),
                            waits=w0, inc=("te" if k == KD - 1 else None))
                    em("scalar", lambda e, o=o: e.copy(
                        out=h2b[:, o * 512:(o + 1) * 512], in_=pB[:, 1024:1536]),
                        waits=[("te", P.v("te"))], inc="se")
                h2_se = P.v("se")
                ph2_xnfm_free[0] = P.v("te")
                em("gpsimd", dmaf(wffn[:].rearrange("p (a b) -> p a b", a=KD),
                                  a2_bfh.rearrange("(a p) b -> p a b", p=128)),
                   waits=[("te", P.v("te"))], inc="gp", step=16)
                a2_gp = P.v("gp")

                def emit_stash(ch2):
                    if ch2 >= nch or ch2 in stash_sc:
                        return
                    sl2 = ch2 % 3
                    em("scalar", lambda e, ch2=ch2, sl2=sl2: e.dma_start(
                        out=yprd[:, sl2 * DI:(sl2 + 1) * DI],
                        in_=st_ypre[ch2 * 128:(ch2 + 1) * 128, :]),
                        waits=[("ve", yprd_free[sl2])], inc="sc", step=16)
                    em("scalar", lambda e, ch2=ch2, sl2=sl2: e.dma_start(
                        out=szrd[:, sl2 * DI:(sl2 + 1) * DI],
                        in_=st_sz[ch2 * 128:(ch2 + 1) * 128, :]),
                        waits=[("ve", szrd_free[sl2])], inc="sc", step=16)
                    em("scalar", lambda e, ch2=ch2, sl2=sl2: e.dma_start(
                        out=crd[:, sl2 * 128:(sl2 + 1) * 128], in_=st_c[ch2]),
                        waits=[("te", crd_free[sl2])], inc="sc", step=16)
                    em("scalar", lambda e, ch2=ch2, sl2=sl2: e.dma_start(
                        out=grd[:, sl2 * 32:(sl2 + 1) * 32], in_=st_g[ch2]),
                        waits=[("ve", grd_free[sl2])], inc="sc", step=16)
                    stash_sc[ch2] = P.v("sc")

                if s == 0:
                    emit_stash(0)
                    emit_stash(1)

                def emit_A(c):
                    ch = s * CPS + c
                    sl = ch % 2
                    sl3 = ch % 3
                    for q in range(4):
                        w0 = ([("ve", R_ve), ("sc", stash_sc[ch]),
                               ("ve", pa_free[0])] if q == 0 else [])
                        em("tensor", lambda e, sl3=sl3, q=q: e.matmul(
                            pA[:, q * 512:(q + 1) * 512],
                            crd[:, sl3 * 128:(sl3 + 1) * 128],
                            R_bf[:, q * 512:(q + 1) * 512], start=True, stop=True),
                            waits=w0, inc=("te" if q == 3 else None))
                    dy_te = P.v("te")
                    crd_free[sl3] = dy_te
                    em("vector", lambda e, sl3=sl3: e.tensor_tensor(
                        out=hq(gr[:]), in0=hq(pA[:]),
                        in1=r3(grd, sl3 * 32, 128, [(1, NH), (0, HP)]), op=ALU.mult),
                        waits=[("te", dy_te)] + ([("gp", P.v("gp"))] if debug else []),
                        inc="ve")
                    gate_ve = P.v("ve")
                    pa_free[0] = gate_ve
                    grd_free[sl3] = gate_ve
                    em("vector", lambda e, sl3=sl3: e.tensor_tensor(
                        out=y2[:], in0=gr[:], in1=yprd[:, sl3 * DI:(sl3 + 1) * DI],
                        op=ALU.add), inc="ve")
                    yprd_free[sl3] = P.v("ve")
                    em("vector", lambda e, sl3=sl3: e.tensor_tensor(
                        out=gr[:], in0=y2[:], in1=szrd[:, sl3 * DI:(sl3 + 1) * DI],
                        op=ALU.mult), inc="ve")
                    yg_ve = P.v("ve")
                    szrd_free[sl3] = yg_ve
                    if debug and s == nsup - 1:
                        em("gpsimd", lambda e, c=c: e.dma_start(
                            out=dbg["d_yg"][c * 128:(c + 1) * 128, :], in_=gr[:]),
                            waits=[("ve", yg_ve)], inc="gp", step=16)
                    for t4 in range(4):
                        em("vector", lambda e, t4=t4: e.bn_stats(
                            out=bnr[:, t4 * 6:(t4 + 1) * 6],
                            in_=gr[:, t4 * 512:(t4 + 1) * 512]),
                            waits=([("se", P.v("se"))] if t4 == 0 else []), inc="ve")
                    em("vector", lambda e: e.bn_aggr(out=bna[:], in_=bnr[:, 0:24]),
                       waits=[("ve", P.v("ve"))], inc="ve")
                    em("vector", lambda e: e.scalar_tensor_tensor(
                        out=col1[:], in0=bna[:, 0:1], scalar=bna[:, 0:1],
                        in1=bna[:, 1:2], op0=ALU.mult, op1=ALU.add), inc="ve")
                    em("scalar", lambda e: e.activation(out=col1[:], in_=col1[:],
                                                        func=AF.Ln, bias=cep[:]),
                       waits=[("ve", P.v("ve"))], inc="se")
                    em("scalar", lambda e: e.activation(out=col2[:], in_=col1[:],
                                                        func=AF.Exp, scale=-0.5), inc="se")
                    em("vector", lambda e: e.tensor_scalar(
                        out=ynb[:], in0=gr[:], scalar1=col2[:], scalar2=None,
                        op0=ALU.mult),
                        waits=[("se", P.v("se")), ("sy", P.v("sy"))] +
                              ([("gp", P.v("gp"))] if debug else []), inc="ve")
                    if debug and s == nsup - 1:
                        em("gpsimd", lambda e, c=c: e.dma_start(
                            out=dbg["d_ynb"][c * 128:(c + 1) * 128, :], in_=ynb[:]),
                            waits=[("ve", P.v("ve"))], inc="gp", step=16)
                        em("gpsimd", lambda e, c=c: e.dma_start(
                            out=dbg["d_xn32"][c * 128:(c + 1) * 128, :],
                            in_=xn32[:, c * DM:(c + 1) * DM]),
                            inc="gp", step=16)
                    em("sync", lambda e, sl=sl: e.dma_start_transpose(
                        r3(ynfm, sl * DI, 128, [(128, NXS), (1, 128)]), ynb[:]),
                        waits=[("ve", P.v("ve")), ("te", ynfm_free[sl])],
                        inc="sy", step=16)
                    yn_sy_d[ch] = P.v("sy")
                    emit_stash(ch + 2)

                def emit_B(c):
                    ch = s * CPS + c
                    sl = ch % 2
                    mcol = muS[:, c:c + 1]
                    em("vector", lambda e, c=c, mcol=mcol: e.scalar_tensor_tensor(
                        out=f1_t[:], in0=xn32[:, c * DM:(c + 1) * DM],
                        scalar=sigS[:, c:c + 1],
                        in1=AP(mcol.tensor, mcol.offset,
                               [[mcol.ap[0][0], 128], [0, DM]]),
                        op0=ALU.mult, op1=ALU.add),
                        waits=[("se", f1_free[0])], inc="ve")
                    xrec_ve = P.v("ve")
                    for q in range(2):
                        for k in range(16):
                            w0 = ([("sy", yn_sy_d[ch]), ("ve", pb_lo_free[0])]
                                  if (q == 0 and k == 0) else [])
                            em("tensor", lambda e, sl=sl, q=q, k=k: e.matmul(
                                pB[:, q * 512:(q + 1) * 512],
                                ynfm[:, sl * DI + k * 128:sl * DI + (k + 1) * 128],
                                woutb[:, k * 1024 + q * 512:k * 1024 + (q + 1) * 512],
                                start=(k == 0), stop=(k == 15)),
                                waits=w0,
                                inc=("te" if (k == 15 and q == 1) else None))
                    wo_te = P.v("te")
                    ynfm_free[sl] = wo_te
                    em("vector", lambda e, c=c: e.tensor_tensor(
                        out=h_t[:], in0=xn32[:, c * DM:(c + 1) * DM],
                        in1=pB[:, 0:1024], op=ALU.add),
                        waits=[("te", wo_te)], inc="ve")
                    pb_lo_free[0] = P.v("ve")
                    if debug and s == nsup - 1:
                        em("gpsimd", lambda e, c=c: e.dma_start(
                            out=dbg["d_h"][c * 128:(c + 1) * 128, :], in_=h_t[:]),
                            waits=[("ve", P.v("ve"))], inc="gp", step=16)
                    em("vector", lambda e: e.bn_stats(out=bnrB[:, 0:6],
                                                      in_=h_t[:, 0:512]),
                       waits=[("se", P.v("se"))], inc="ve")
                    em("vector", lambda e: e.bn_stats(out=bnrB[:, 6:12],
                                                      in_=h_t[:, 512:1024]),
                       inc="ve")
                    em("vector", lambda e: e.bn_aggr(out=bnaB[:], in_=bnrB[:, 0:12]),
                       waits=[("ve", P.v("ve"))], inc="ve")
                    em("scalar", lambda e: e.activation(out=col1B[:], in_=bnaB[:, 1:2],
                                                        func=AF.Ln, bias=cep[:]),
                       waits=[("ve", P.v("ve"))], inc="se")
                    em("scalar", lambda e: e.activation(out=col2B[:], in_=col1B[:],
                                                        func=AF.Exp, scale=-0.5),
                       inc="se")
                    em("vector", lambda e: e.scalar_tensor_tensor(
                        out=u1_t[:], in0=h_t[:], scalar=bnaB[:, 0:1],
                        in1=bcast_col(col2B, DM), op0=ALU.subtract, op1=ALU.mult),
                        waits=[("se", P.v("se"))] + ([("gp", P.v("gp"))] if debug else []),
                        inc="ve")
                    em("vector", lambda e: e.tensor_tensor(
                        out=h_t[:], in0=u1_t[:], in1=lngT[:], op=ALU.mult),
                        waits=([("gp", P.v("gp"))] if debug else []), inc="ve")
                    em("gpsimd", lambda e: e.tensor_tensor(
                        out=u1_t[:], in0=h_t[:], in1=lnbT[:], op=ALU.add),
                        waits=[("ve", P.v("ve"))], inc="po")
                    em("gpsimd", lambda e: e.tensor_tensor(
                        out=h_t[:], in0=u1_t[:], in1=f1_t[:],
                        op=ALU.add),
                        waits=[("ve", xrec_ve)], inc="po")
                    u1_po = P.v("po")
                    for q in range(2):
                        for k in range(KD):
                            w0 = ([("se", h2_se), ("gp", a2_gp),
                                   ("ve", pb_hi_free[0])]
                                  if (q == 0 and k == 0) else [])
                            em("tensor", lambda e, c=c, q=q, k=k: e.matmul(
                                pB[:, 1024 + q * 512:1024 + (q + 1) * 512],
                                h2b[:, k * 512 + c * 128:k * 512 + (c + 1) * 128],
                                wffn[:, k * 1024 + q * 512:k * 1024 + (q + 1) * 512],
                                start=(k == 0), stop=(k == KD - 1)),
                                waits=w0,
                                inc=("te" if (k == KD - 1 and q == 1) else None))
                    f3_te = P.v("te")
                    em("vector", lambda e: e.tensor_tensor(
                        out=f1_t[:], in0=pB[:, 1024:2048], in1=fbT[:], op=ALU.add),
                        waits=[("te", f3_te), ("po", u1_po)], inc="ve")
                    pb_hi_free[0] = P.v("ve")
                    em("scalar", lambda e: e.activation(out=u1_t[:], in_=f1_t[:],
                                                        func=AF.Tanh),
                       waits=[("ve", P.v("ve")), ("po", u1_po)] +
                             ([("gp", P.v("gp"))] if debug else []), inc="se")
                    f1_free[0] = P.v("se")
                    if debug and s == nsup - 1:
                        em("gpsimd", lambda e, c=c: e.dma_start(
                            out=dbg["d_f1"][c * 128:(c + 1) * 128, :], in_=u1_t[:]),
                            waits=[("ve", P.v("ve"))], inc="gp", step=16)
                    em("vector", lambda e, sl=sl: e.tensor_tensor(
                        out=outst[:, sl * DM:(sl + 1) * DM], in0=h_t[:], in1=u1_t[:],
                        op=ALU.add),
                        waits=[("se", P.v("se")), ("po", u1_po), ("gp", P.v("gp"))],
                        inc="ve")
                    em("gpsimd", lambda e, ch=ch, sl=sl: e.dma_start(
                        out=out_seg[ch * 128:(ch + 1) * 128, :],
                        in_=outst[:, sl * DM:(sl + 1) * DM]),
                        waits=[("ve", P.v("ve"))], inc="gp", step=16)

                for c in range(CPS):
                    emit_A(c)
                    if c > 0:
                        emit_B(c - 1)
                        if s + 1 < nsup:
                            emit_ln2_chunk(s + 1, c - 1)
                emit_B(CPS - 1)
                if s + 1 < nsup:
                    emit_ln2_chunk(s + 1, CPS - 1)

            P.emit("gpsimd", lambda e: e.nop(), waits=[("gp", P.v("gp"))])

            OWN = {"vector": "ve", "scalar": "se", "gpsimd": "po", "sync": "sy"}
            with nc.Block() as block:
                for eng_name in ENGS:
                    def make(eng_name):
                        def body(e):
                            own = OWN.get(eng_name)
                            own_cnt = 0
                            last_wait = 0
                            for (fn, waits, inc, step) in P.items[eng_name]:
                                if own is not None and own_cnt > last_wait:
                                    e.wait_ge(sems[own], own_cnt)
                                    last_wait = own_cnt
                                for (sname, val) in waits:
                                    e.wait_ge(sems[sname], val)
                                r = fn(e)
                                if inc is not None:
                                    r.then_inc(sems[inc], step)
                                    if sname_is_own := (inc == own):
                                        own_cnt += step
                        return body
                    getattr(block, eng_name)(make(eng_name))

    return nc, dbg


def kernel(**inputs):
    return _run(inputs, debug=False)[0]


def _run(inputs, debug=False):
    x = np.asarray(inputs["L0_attn_hidden_states"])
    B, T, _ = x.shape
    seg_len = T // NSEG
    w_in = np.ascontiguousarray(inputs["L0_attn_W_in"])
    conv_w = np.asarray(inputs["L0_attn_conv_w"])
    conv_b = np.asarray(inputs["L0_attn_conv_b"])
    dt_bias = np.asarray(inputs["L0_attn_dt_bias"])
    A_log = np.asarray(inputs["L0_attn_A_log"])
    D = np.asarray(inputs["L0_attn_D"])
    rms_w = np.asarray(inputs["L0_attn_rms_w"])
    w_out = np.ascontiguousarray(inputs["L0_attn_W_out"])
    ln_g = np.asarray(inputs["L0_attn_ln_g"])
    ln_b = np.asarray(inputs["L0_attn_ln_b"])
    wf = np.ascontiguousarray(inputs["L0_ffn_weight"])
    a1 = np.ascontiguousarray(inputs["L0_ffn_aux_weight1"])
    a2 = np.ascontiguousarray(inputs["L0_ffn_aux_weight2"])
    fb = np.asarray(inputs["L0_ffn_bias"])

    conv_wt = np.ascontiguousarray(
        conv_w.reshape(NBC, 128, 4).transpose(1, 0, 2).reshape(128, NBC * 4))
    conv_bt = np.ascontiguousarray(conv_b.reshape(NBC, 128).T)
    dtb_col = np.ascontiguousarray(dt_bias.reshape(32, 1))
    alog_col = np.ascontiguousarray(A_log.reshape(32, 1))
    d_tile = np.ascontiguousarray(np.tile(D.reshape(1, 32), (128, 1)))
    rms_t = np.ascontiguousarray(rms_w.reshape(16, 128).T)
    lng_t = np.ascontiguousarray(np.tile(ln_g.reshape(1, DM), (128, 1)))
    lnb_t = np.ascontiguousarray(np.tile(ln_b.reshape(1, DM), (128, 1)))
    fbias_t = np.ascontiguousarray(np.tile(fb.reshape(1, DM), (128, 1)))
    i_bf = np.eye(128, dtype=BF)
    irep = np.ascontiguousarray(np.tile(np.eye(128, dtype=BF), (1, 4)))

    nc, _ = build_nc(seg_len, debug=debug)

    in_maps = []
    for core in range(NCORE):
        b, s = core // NSEG, core % NSEG
        t0 = s * seg_len
        xs = np.ascontiguousarray(x[b, t0:t0 + seg_len])
        if s == 0:
            xh = np.zeros((16, DM), np.float32)
        else:
            xh = np.ascontiguousarray(x[b, t0 - 16:t0])
        alpha = np.zeros(8, np.float32)
        gam = np.zeros(8, np.float32)
        for u in range(NCORE):
            if u // NSEG == b and (u % NSEG) < s:
                alpha[u] = 1.0
                gam[u] = 1.0
        alpha_t = np.ascontiguousarray(np.tile(alpha.reshape(1, 8), (128, 1)))
        gam_t = np.ascontiguousarray(np.tile(gam.reshape(1, 8), (128, 1)))
        igam_t = np.ascontiguousarray(1.0 - gam_t)
        in_maps.append(dict(
            x_seg=xs, x_halo=xh, w_in=w_in, conv_wt=conv_wt, conv_bt=conv_bt,
            dtb_col=dtb_col, alog_col=alog_col, d_tile=d_tile, rms_t=rms_t,
            w_out=w_out, wf=wf, a1=a1, a2=a2, lng_t=lng_t, lnb_t=lnb_t,
            fbias_t=fbias_t, i_bf=i_bf, irep=irep, alpha_t=alpha_t,
            gam_t=gam_t, igam_t=igam_t,
            eps_c=np.full((128, 1), EPS, np.float32)))

    res = run_bass_kernel_spmd(nc, in_maps, core_ids=list(range(NCORE)))
    out = np.empty((B, T, DM), np.float32)
    for core in range(NCORE):
        b, s = core // NSEG, core % NSEG
        out[b, s * seg_len:(s + 1) * seg_len] = res.results[core]["out_seg"]
    return out, res.results
